# version 01
# baseline (speedup 1.0000x reference)
import numpy as np
import ml_dtypes
from contextlib import ExitStack

import concourse.bass as bass
import concourse.mybir as mybir
from concourse.bass_utils import run_bass_kernel_spmd

F32 = mybir.dt.float32
BF16 = mybir.dt.bfloat16
AF = mybir.ActivationFunctionType
ALU = mybir.AluOpType
AX = mybir.AxisListType

NCORES = 8
B_PER_CORE = 2
T = 2048
D = 2048
EPS = 1e-5

ENGINES = ("T", "V", "S", "G", "Q")


class Prog:

    def __init__(self, nc, same_engine_sync=True):
        self.nc = nc
        self.items = {e: [] for e in ENGINES}
        self.dma_cnt = {}
        self.stream_names = []
        self.lw = {}
        self.rd = {}
        self.same_sync = same_engine_sync

    def _gather(self, reads, writes, after):
        deps = set(t for t in after if t is not None)
        for r in reads:
            t = self.lw.get(r)
            if t is not None:
                deps.add(t)
        for w in writes:
            t = self.lw.get(w)
            if t is not None:
                deps.add(t)
            for t2 in self.rd.get(w, ()):
                deps.add(t2)
        for t in deps:
            if t[0] == "E":
                self.items[t[1]][t[2]][2] = True
        return deps

    def _record_rw(self, tok, reads, writes):
        for r in reads:
            self.rd.setdefault(r, []).append(tok)
        for w in writes:
            self.lw[w] = tok
            self.rd[w] = []

    def op(self, engine, fn, reads=(), writes=(), after=(), inc=True):
        deps = self._gather(reads, writes, after)
        tok = ("E", engine, len(self.items[engine]))
        self.items[engine].append([fn, deps, inc, []])
        self._record_rw(tok, reads, writes)
        return tok

    def dma(self, stream, out, in_, engine="Q", reads=(), writes=(), after=()):
        deps = self._gather(reads, writes, after)
        idx = self.dma_cnt.get(stream, 0)
        self.dma_cnt[stream] = idx + 1
        if stream not in self.stream_names:
            self.stream_names.append(stream)
        tok = ("D", stream, idx)

        def fn(eng, out=out, in_=in_):
            return eng.dma_start(out, in_)

        self.items[engine].append([fn, deps, False, [(stream, 16)]])
        self._record_rw(tok, reads, writes)
        return tok

    def wait_only(self, engine, after):
        deps = self._gather((), (), after)
        self.items[engine].append([None, deps, False, []])

    def emit(self, stack):
        nc = self.nc
        sems = {}
        for name in list(ENGINES) + self.stream_names:
            sems[name] = stack.enter_context(nc.semaphore("sem_" + name))

        resolve = {}
        for e in ENGINES:
            n = len(self.items[e])
            nxt = [None] * n
            cnt = 0
            incs_upto = [0] * (n + 1)
            for i in range(n):
                incs_upto[i + 1] = incs_upto[i] + (1 if self.items[e][i][2] else 0)
            next_inc = [None] * (n + 1)
            last = None
            for i in range(n - 1, -1, -1):
                if self.items[e][i][2]:
                    last = i
                next_inc[i] = last
            resolve[e] = (incs_upto, next_inc)

        def wait_of(tok):
            if tok[0] == "D":
                return (tok[1], 16 * (tok[2] + 1))
            e, idx = tok[1], tok[2]
            incs_upto, next_inc = resolve[e]
            j = next_inc[idx]
            assert j is not None, f"dep on non-inc'd tail op {tok}"
            return (e, incs_upto[j + 1])

        block = stack.enter_context(nc.Block())

        def make_emitter(ename):
            ops = self.items[ename]

            def emitter(eng):
                floors = {}
                for fn, deps, inc, extra in ops:
                    waits = []
                    for t in deps:
                        if t[0] == "E" and t[1] == ename:
                            if ename == "T" or not self.same_sync:
                                continue
                        waits.append(wait_of(t))
                    for (s, v) in sorted(waits):
                        if floors.get(s, 0) >= v:
                            continue
                        eng.wait_ge(sems[s], v)
                        floors[s] = v
                    if fn is None:
                        continue
                    inst = fn(eng)
                    if inc:
                        inst.then_inc(sems[ename], 1)
                    for (s, n) in extra:
                        inst.then_inc(sems[s], n)

            return emitter

        block.tensor(make_emitter("T"))
        block.vector(make_emitter("V"))
        block.scalar(make_emitter("S"))
        block.gpsimd(make_emitter("G"))
        block.sync(make_emitter("Q"))

    def fence(self):
        toks = []
        for e in ENGINES:
            for i in range(len(self.items[e]) - 1, -1, -1):
                if not self.items[e][i][3]:
                    if self.items[e][i][0] is not None:
                        self.items[e][i][2] = True
                        toks.append(("E", e, i))
                    break
        for s, cdma in self.dma_cnt.items():
            toks.append(("D", s, cdma - 1))
        for e in ENGINES:
            deps = set(t for t in toks if not (t[0] == "E" and t[1] == e))
            self.items[e].append([None, deps, False, []])

    def finish(self, streams):
        toks = [("D", s, self.dma_cnt[s] - 1) for s in streams if s in self.dma_cnt]
        self.wait_only("Q", toks)


def _l(x):
    return [x] if isinstance(x, str) else list(x)


def ln_normalize(p, x_ap, out_ap, xbuf, outbuf, c_ap, cbuf, F=2048):
    xbuf, outbuf, cbuf = _l(xbuf), _l(outbuf), _l(cbuf)
    p.op("V", lambda e: e.reduce_sum(c_ap[:, 0:1], x_ap, axis=AX.X),
         reads=xbuf, writes=cbuf)
    p.op("S", lambda e: e.activation(out_ap, x_ap, AF.Square, accum_out=c_ap[:, 1:2]),
         reads=xbuf, writes=outbuf + cbuf)
    p.op("V", lambda e: e.tensor_scalar_mul(c_ap[:, 2:3], c_ap[:, 0:1], 1.0 / F),
         reads=cbuf, writes=cbuf)
    p.op("V", lambda e: e.tensor_scalar_mul(c_ap[:, 3:4], c_ap[:, 1:2], 1.0 / F),
         reads=cbuf, writes=cbuf)
    p.op("V", lambda e: e.tensor_tensor(c_ap[:, 6:7], c_ap[:, 2:3], c_ap[:, 2:3], ALU.mult),
         reads=cbuf, writes=cbuf)
    p.op("V", lambda e: e.tensor_tensor(c_ap[:, 3:4], c_ap[:, 3:4], c_ap[:, 6:7], ALU.subtract),
         reads=cbuf, writes=cbuf)
    p.op("V", lambda e: e.tensor_scalar_add(c_ap[:, 3:4], c_ap[:, 3:4], EPS),
         reads=cbuf, writes=cbuf)
    p.op("S", lambda e: e.activation(c_ap[:, 7:8], c_ap[:, 3:4], AF.Ln),
         reads=cbuf, writes=cbuf)
    p.op("S", lambda e: e.activation(c_ap[:, 4:5], c_ap[:, 7:8], AF.Exp, scale=-0.5),
         reads=cbuf, writes=cbuf)
    p.op("V", lambda e: e.tensor_tensor(c_ap[:, 6:7], c_ap[:, 2:3], c_ap[:, 4:5], ALU.mult),
         reads=cbuf, writes=cbuf)
    p.op("V", lambda e: e.tensor_scalar_mul(c_ap[:, 5:6], c_ap[:, 6:7], -1.0),
         reads=cbuf, writes=cbuf)
    p.op("S", lambda e: e.activation(out_ap, x_ap, AF.Identity, scale=c_ap[:, 4:5], bias=c_ap[:, 5:6]),
         reads=xbuf + cbuf, writes=outbuf)



def build_l1ffn(p, st, io, pf="f1"):
    nc = p.nc
    e = st.enter_context

    identF = e(nc.sbuf_tensor(pf + "identF", [128, 128], F32))
    p.dma(pf + "cid", identF[:, :], io["identF"], writes=["identF"])

    psA = e(nc.psum_tensor(pf + "psA", [128, 2048], F32))
    psB = e(nc.psum_tensor(pf + "psB", [128, 2048], F32))
    psAb = ["psA0", "psA1", "psA2", "psA3"]
    psBb = ["psB0", "psB1", "psB2", "psB3"]

    wstg = e(nc.sbuf_tensor(pf + "wstg", [128, 2048], F32))
    SGN = e(nc.sbuf_tensor(pf + "SGN", [128, 16 * 2048], BF16))
    ABSB = e(nc.sbuf_tensor(pf + "ABS", [128, 2048], BF16))
    ones_col = e(nc.sbuf_tensor(pf + "ones_col", [128, 1], BF16))
    p.dma(pf + "cones", ones_col[:, :], io["ones128"], writes=["ones_col"])
    ab_bc = e(nc.sbuf_tensor(pf + "ab_bc", [128, 2 * 2048], F32))
    xt = e(nc.sbuf_tensor(pf + "xt", [128, 2 * 2048], F32))
    xn = e(nc.sbuf_tensor(pf + "xn", [128, 2048], F32))
    lnm = e(nc.sbuf_tensor(pf + "lnm", [128, 2048], F32))
    colsb = e(nc.sbuf_tensor(pf + "cols", [128, 16], F32))
    XHT = e(nc.sbuf_tensor(pf + "XHT", [128, 16 * 2048], BF16))
    mix = xt[:, 0:2048]
    xr = xt[:, 2048:4096]
    outt = xn
    a_row = lnm

    for k in range(16):
        p.dma(pf + "wld", wstg[:, :],
              io["wT"][k * 128:(k + 1) * 128, :], writes=[pf + "wstg"])
        p.op("S", lambda e_, k=k: e_.activation(
            SGN[:, k * 2048:(k + 1) * 2048], wstg[:, :], AF.Sign),
            reads=[pf + "wstg"], writes=[pf + f"SGN{k}"])
        p.op("S", lambda e_, k=k: e_.activation(
            ABSB[:, :], wstg[:, :], AF.Abs),
            reads=[pf + "wstg"], writes=[pf + "ABS"])
        for c in range(4):
            p.op("T", lambda e_, k=k, c=c: e_.matmul(
                psA[0:1, c * 512:c * 512 + 512], ones_col[:, :],
                ABSB[:, c * 512:(c + 1) * 512],
                start=(k == 0), stop=(k == 15)),
                reads=[pf + "ABS", "ones_col"], writes=[psAb[c]],
                inc=False)
    p.op("S", lambda e_: e_.activation(a_row[0:1, 0:2048], psA[0:1, 0:2048], AF.Copy, scale=1.0 / 2048),
         reads=psAb, writes=[pf + "a_row", pf + "lnm"])
    p.dma(pf + "ast", io["scr_row"], a_row[0:1, 0:2048], reads=[pf + "a_row"], writes=[pf + "srow"])
    p.dma(pf + "ald", ab_bc[:, 0:2048],
          io["scr_row"].partition_broadcast(128), reads=[pf + "srow"], writes=[pf + "ab"])
    p.dma(pf + "bld", ab_bc[:, 2048:4096],
          io["bias"].unsqueeze(0).partition_broadcast(128), writes=[pf + "ab"])


    def sl(t_, s, w=2048):
        return t_[:, s * w:(s + 1) * w]

    for b in range(B_PER_CORE):
        xin = io["xin"][b]
        xout = io["xout"][b]
        for i in range(16):
            s = i % 2
            p.dma(pf + "xld" + str(s), sl(xt, s), xin[i * 128:(i + 1) * 128, :],
                  writes=[pf + f"xt{s}"])
            ln_normalize(p, sl(xt, s), xn[:, :], [pf + f"xt{s}"], [pf + "xn", pf + "outt"],
                         colsb[:, s * 8:(s + 1) * 8], pf + f"cols{s}")
            for k in range(16):
                bank = k % 4
                p.op("T", lambda e_, k=k, bank=bank: e_.transpose(
                    psB[:, bank * 512:bank * 512 + 128],
                    xn[:, k * 128:(k + 1) * 128], identF[:, :]),
                    reads=[pf + "xn", "identF"], writes=[psBb[bank]])
                p.op("S", lambda e_, k=k, bank=bank, i=i: e_.copy(
                    XHT[:, k * 2048 + i * 128:k * 2048 + i * 128 + 128],
                    psB[:, bank * 512:bank * 512 + 128]),
                    reads=[psBb[bank]], writes=[pf + f"XHT{k}"])
        for i in range(16):
            s = i % 2
            p.dma(pf + "xrl", xr[:, :], xin[i * 128:(i + 1) * 128, :],
                  writes=[pf + "xr0", pf + "xt1"])
            for c in range(4):
                pb = (4 * i + c) % 4
                for k in range(16):
                    p.op("T", lambda e_, i=i, c=c, k=k, pb=pb: e_.matmul(
                        psA[:, pb * 512:(pb + 1) * 512],
                        XHT[:, k * 2048 + i * 128:k * 2048 + i * 128 + 128],
                        SGN[:, k * 2048 + c * 512:k * 2048 + c * 512 + 512],
                        start=(k == 0), stop=(k == 15)),
                        reads=[pf + f"XHT{k}", pf + f"SGN{k}"], writes=[psAb[pb]],
                        inc=False)
                p.op("V", lambda e_, c=c, pb=pb: e_.tensor_tensor(
                    mix[:, c * 512:(c + 1) * 512], psA[:, pb * 512:(pb + 1) * 512],
                    ab_bc[:, c * 512:(c + 1) * 512], ALU.mult),
                    reads=[psAb[pb], pf + "ab"], writes=[pf + "mix", pf + "xt0"])
                p.op("V", lambda e_, c=c: e_.tensor_tensor(
                    mix[:, c * 512:(c + 1) * 512], mix[:, c * 512:(c + 1) * 512],
                    ab_bc[:, 2048 + c * 512:2048 + (c + 1) * 512], ALU.add),
                    reads=[pf + "mix", pf + "xt0", pf + "ab"], writes=[pf + "mix", pf + "xt0"])
            ln_normalize(p, mix[:, :], lnm[:, :], [pf + "mix", pf + "xt0"],
                         [pf + "lnm", pf + "a_row"],
                         colsb[:, s * 8:(s + 1) * 8], pf + f"cols{s}")
            p.op("V", lambda e_: e_.tensor_tensor(
                outt[:, :], xr[:, :], lnm[:, :], ALU.add),
                reads=[pf + "xr0", pf + "xt1", pf + "lnm"], writes=[pf + "outt", pf + "xn"])
            p.dma(pf + "xst", xout[i * 128:(i + 1) * 128, :], outt[:, :],
                  reads=[pf + "outt", pf + "xn"], writes=[pf + f"xo{b}_{i}"])
    return [pf + "xst"]


def mk_l1ffn_program():
    nc = bass.Bass("TRN2", target_bir_lowering=False)
    io = {}
    io["wT"] = nc.dram_tensor("wT", [2048, 2048], F32, kind="ExternalInput").ap()
    io["bias"] = nc.dram_tensor("bias", [2048], F32, kind="ExternalInput").ap()
    io["identF"] = nc.dram_tensor("identF", [128, 128], F32, kind="ExternalInput").ap()
    io["ones128"] = nc.dram_tensor("ones128", [128, 1], BF16, kind="ExternalInput").ap()
    io["scr_row"] = nc.dram_tensor("scr_row", [1, 2048], F32, kind="Internal").ap()
    xin = nc.dram_tensor("xin", [B_PER_CORE, 2048, 2048], F32, kind="ExternalInput").ap()
    xout = nc.dram_tensor("xout", [B_PER_CORE, 2048, 2048], F32, kind="ExternalOutput").ap()
    io["xin"] = [xin[b] for b in range(B_PER_CORE)]
    io["xout"] = [xout[b] for b in range(B_PER_CORE)]

    with ExitStack() as st:
        p = Prog(nc)
        streams = build_l1ffn(p, st, io)
        p.finish(streams)
        p.emit(st)
    return nc


def host_consts():
    return {
        "identF": np.eye(128, dtype=np.float32),
        "ones128": np.ones((128, 1), dtype=ml_dtypes.bfloat16),
    }




def sl2(t_, s, w=2048):
    return t_[:, s * w:(s + 1) * w]



def build_l0ffn(p, st, io, pf="f0"):
    nc = p.nc
    e = st.enter_context

    identF = e(nc.sbuf_tensor(pf + "identF", [128, 128], F32))
    H128 = e(nc.sbuf_tensor(pf + "H128", [128, 128], BF16))
    BD64 = e(nc.sbuf_tensor(pf + "BD64", [128, 128], BF16))
    p.dma(pf + "cid", identF[:, :], io["identF"], writes=["identF"])
    p.dma(pf + "chh", H128[:, :], io["H128"], writes=["H128"])
    p.dma(pf + "cbd", BD64[:, :], io["BD64"], writes=["BD64"])

    psA = e(nc.psum_tensor(pf + "psA", [128, 2048], F32))
    psB = e(nc.psum_tensor(pf + "psB", [128, 2048], F32))
    psAb = ["psA0", "psA1", "psA2", "psA3"]
    psBb = ["psB0", "psB1", "psB2", "psB3"]

    XFT = e(nc.sbuf_tensor(pf + "XFT", [128, 16 * 2048], BF16))
    FW1 = e(nc.sbuf_tensor(pf + "FW1", [128, 64 * 512], BF16))
    W1B = e(nc.sbuf_tensor(pf + "W1B", [128, 16 * 512], BF16))
    W3B = e(nc.sbuf_tensor(pf + "W3B", [128, 16 * 512], BF16))
    W2B = e(nc.sbuf_tensor(pf + "W2B", [128, 8 * 512], BF16))
    xt = e(nc.sbuf_tensor(pf + "xt", [128, 2 * 2048], F32))
    xn = e(nc.sbuf_tensor(pf + "xn", [128, 2048], F32))
    lnm = xn
    sg = e(nc.sbuf_tensor(pf + "sg", [128, 512], F32))
    hid = e(nc.sbuf_tensor(pf + "hid", [128, 512], BF16))
    mo = e(nc.sbuf_tensor(pf + "mo", [128, 2 * 512], F32))
    colsb = e(nc.sbuf_tensor(pf + "cols", [128, 16], F32))
    wst = FW1[:, 0:16384].bitcast(F32)
    wbo = FW1[:, 16384:24576]
    w2s = FW1[:, 24576:28672].bitcast(F32)
    w2c = FW1[:, 28672:30720]
    w2o = FW1[:, 30720:32768]
    FWALL = [pf + f"FW1_{g}" for g in range(64)]
    WAL = [pf + "wst", pf + "wbo", pf + "w2s", pf + "w2c", pf + "w2o"]
    mix = xt[:, 2048:4096]
    xr = xt[:, 0:2048]

    SC = 1.0 / float(np.sqrt(8192.0))

    for (wsrc, wdst, stw) in ((io["w1T"], io["s_w1tb"], "w1c"), (io["w3T"], io["s_w3tb"], "w3c")):
        for k in range(16):
            p.dma(pf + stw + "l", wst, wsrc[k * 128:(k + 1) * 128, :], writes=[pf + "wst"])
            p.op("V", lambda e_: e_.tensor_copy(wbo, wst),
                 reads=[pf + "wst"], writes=[pf + "wbo"])
            p.dma(pf + stw + "s", wdst[k * 128:(k + 1) * 128, :], wbo, reads=[pf + "wbo"],
                  writes=[pf + stw + "d"])
    w2hv = io["s_w2hgm"].rearrange("(g q) d -> q g d", q=128)
    for i in range(64):
        p.dma(pf + "w2l", w2s, io["w2pm"][i * 128:(i + 1) * 128, :], writes=[pf + "w2s"])
        p.op("V", lambda e_: e_.tensor_copy(w2c, w2s), reads=[pf + "w2s"], writes=[pf + "w2c"])
        for c in range(4):
            p.op("T", lambda e_, c=c: e_.matmul(
                psB[:, c * 512:(c + 1) * 512], BD64[:, :], w2c[:, c * 512:(c + 1) * 512],
                start=True, stop=True),
                reads=[pf + "w2c", "BD64"], writes=[psBb[c]], inc=False)
            p.op("S", lambda e_, c=c: e_.activation(
                w2o[:, c * 512:(c + 1) * 512], psB[:, c * 512:(c + 1) * 512], AF.Copy, scale=SC),
                reads=[psBb[c]], writes=[pf + "w2o"])
        p.dma(pf + "w2st", w2hv[2 * i:2 * i + 2, :, :], w2o,
              reads=[pf + "w2o"], writes=[pf + "w2d"])

    w1bv = io["s_w1tb"].rearrange("(k q) h -> q k h", q=128)
    w3bv = io["s_w3tb"].rearrange("(k q) h -> q k h", q=128)
    w2gv = io["s_w2hgm"].rearrange("(gg g8 q) d -> gg q g8 d", gg=8, g8=8, q=128)

    for b in range(B_PER_CORE):
        xin = io["xin"][b]
        xout = io["xout"][b]
        smix = io["s_mix"]
        for i in range(16):
            s = i % 2
            xw = [pf + f"xt{s}"] + ([pf + "xr"] if i == 0 else [pf + "mix"] if i == 1 else [])
            p.dma(pf + "xld" + str(s), sl2(xt, s), xin[i * 128:(i + 1) * 128, :],
                  writes=xw)
            ln_normalize(p, sl2(xt, s), xn[:, :], [pf + f"xt{s}"], [pf + "xn"],
                         colsb[:, s * 8:(s + 1) * 8], pf + f"cols{s}")
            for k in range(16):
                bank = k % 4
                p.op("T", lambda e_, k=k, bank=bank: e_.transpose(
                    psB[:, bank * 512:bank * 512 + 128],
                    xn[:, k * 128:(k + 1) * 128], identF[:, :]),
                    reads=[pf + "xn", "identF"], writes=[psBb[bank]], inc=False)
                p.op("S", lambda e_, k=k, bank=bank, i=i: e_.copy(
                    XFT[:, k * 2048 + i * 128:k * 2048 + i * 128 + 128],
                    psB[:, bank * 512:bank * 512 + 128]),
                    reads=[psBb[bank]], writes=[pf + f"XFT{k}"])
        for c in range(4):
            for hb in range(16):
                p.dma(pf + "w1bl", W1B[:, :], w1bv[:, :, hb * 512:(hb + 1) * 512],
                      reads=[pf + "w1cd"], writes=[pf + "W1B"])
                p.dma(pf + "w3bl", W3B[:, :], w3bv[:, :, hb * 512:(hb + 1) * 512],
                      reads=[pf + "w3cd"], writes=[pf + "W3B"])
                for m in range(4):
                    pg, pl = (2 * m) % 4, (2 * m + 1) % 4
                    for k in range(16):
                        p.op("T", lambda e_, m=m, k=k, c=c, pg=pg: e_.matmul(
                            psA[:, pg * 512:(pg + 1) * 512],
                            W1B[:, k * 512 + m * 128:k * 512 + m * 128 + 128],
                            XFT[:, k * 2048 + c * 512:k * 2048 + (c + 1) * 512],
                            start=(k == 0), stop=(k == 15)),
                            reads=[pf + "W1B", pf + f"XFT{k}"], writes=[psAb[pg]], inc=False)
                        p.op("T", lambda e_, m=m, k=k, c=c, pl=pl: e_.matmul(
                            psA[:, pl * 512:(pl + 1) * 512],
                            W3B[:, k * 512 + m * 128:k * 512 + m * 128 + 128],
                            XFT[:, k * 2048 + c * 512:k * 2048 + (c + 1) * 512],
                            start=(k == 0), stop=(k == 15)),
                            reads=[pf + "W3B", pf + f"XFT{k}"], writes=[psAb[pl]], inc=False)
                    p.op("S", lambda e_, pg=pg: e_.activation(
                        sg[:, :], psA[:, pg * 512:(pg + 1) * 512], AF.Silu),
                        reads=[psAb[pg]], writes=[pf + "sg"])
                    p.op("V", lambda e_, pl=pl: e_.tensor_tensor(
                        hid[:, :], sg[:, :], psA[:, pl * 512:(pl + 1) * 512], ALU.mult),
                        reads=[pf + "sg", psAb[pl]], writes=[pf + "hid"])
                    pb = m % 4
                    p.op("T", lambda e_, pb=pb: e_.matmul(
                        psB[:, pb * 512:(pb + 1) * 512], H128[:, :], hid[:, :],
                        start=True, stop=True),
                        reads=[pf + "hid", "H128"], writes=[psBb[pb]], inc=False)
                    g = hb * 4 + m
                    al = (pf + "wst" if g < 32 else pf + "wbo" if g < 48 else
                          pf + "w2s" if g < 56 else pf + "w2c" if g < 60 else pf + "w2o")
                    p.op("S", lambda e_, g=g, pb=pb: e_.copy(
                        FW1[:, g * 512:(g + 1) * 512], psB[:, pb * 512:(pb + 1) * 512]),
                        reads=[psBb[pb]], writes=[pf + f"FW1_{g}", al])
            for dc in range(4):
                for gg in range(8):
                    ws = gg % 1
                    p.dma(pf + "w2bl", W2B[:, :],
                          w2gv[gg][:, :, dc * 512:(dc + 1) * 512],
                          reads=[pf + "w2d"], writes=[pf + "W2B"])
                    for g8 in range(8):
                        g = gg * 8 + g8
                        for i4 in range(4):
                            p.op("T", lambda e_, g=g, g8=g8, i4=i4, c=c: e_.matmul(
                                psA[:, i4 * 512:(i4 + 1) * 512],
                                FW1[:, g * 512 + i4 * 128:g * 512 + i4 * 128 + 128],
                                W2B[:, g8 * 512:(g8 + 1) * 512],
                                start=(g == 0), stop=(g == 63)),
                                reads=[pf + "W2B", pf + f"FW1_{g}"], writes=[psAb[i4]], inc=False)
                for i4 in range(4):
                    ms = i4 % 2
                    it = c * 4 + i4
                    p.op("S", lambda e_, i4=i4, ms=ms: e_.copy(
                        mo[:, ms * 512:(ms + 1) * 512], psA[:, i4 * 512:(i4 + 1) * 512]),
                        reads=[psAb[i4]], writes=[pf + f"mo{ms}"])
                    p.dma(pf + "most" + str(ms), smix[it * 128:(it + 1) * 128, dc * 512:(dc + 1) * 512],
                          mo[:, ms * 512:(ms + 1) * 512], reads=[pf + f"mo{ms}"],
                          writes=[pf + f"smx{it}_{dc}"])
            for i4 in range(4):
                it = c * 4 + i4
                p.dma(pf + "mxl", mix, smix[it * 128:(it + 1) * 128, :],
                      reads=[pf + f"smx{it}_{d_}" for d_ in range(4)],
                      writes=[pf + "mix", pf + "xt1"])
                p.dma(pf + "xrl", xr, xin[it * 128:(it + 1) * 128, :],
                      writes=[pf + "xr", pf + "xt0"])
                ln_normalize(p, mix, lnm[:, :], [pf + "mix", pf + "xt1"],
                             [pf + "lnm", pf + "xn"], colsb[:, 0:8], pf + "cols0")
                p.op("V", lambda e_: e_.tensor_tensor(mix, xr, lnm[:, :], ALU.add),
                     reads=[pf + "xr", pf + "xt0", pf + "lnm", pf + "xn"],
                     writes=[pf + "mix", pf + "xt1"])
                p.dma(pf + "xst", xout[it * 128:(it + 1) * 128, :], mix,
                      reads=[pf + "mix", pf + "xt1"], writes=[pf + f"xo{b}_{it}"])
    return [pf + "xst", pf + "most0", pf + "most1"]


def mk_l0ffn_program():
    nc = bass.Bass("TRN2", target_bir_lowering=False)
    io = {}
    io["w1T"] = nc.dram_tensor("w1T", [2048, 8192], F32, kind="ExternalInput").ap()
    io["w3T"] = nc.dram_tensor("w3T", [2048, 8192], F32, kind="ExternalInput").ap()
    io["w2pm"] = nc.dram_tensor("w2pm", [8192, 2048], F32, kind="ExternalInput").ap()
    io["identF"] = nc.dram_tensor("identF", [128, 128], F32, kind="ExternalInput").ap()
    io["H128"] = nc.dram_tensor("H128", [128, 128], BF16, kind="ExternalInput").ap()
    io["BD64"] = nc.dram_tensor("BD64", [128, 128], BF16, kind="ExternalInput").ap()
    io["s_w1tb"] = nc.dram_tensor("s_w1tb", [2048, 8192], BF16, kind="Internal").ap()
    io["s_w3tb"] = nc.dram_tensor("s_w3tb", [2048, 8192], BF16, kind="Internal").ap()
    io["s_w2hgm"] = nc.dram_tensor("s_w2hgm", [8192, 2048], BF16, kind="Internal").ap()
    io["s_mix"] = nc.dram_tensor("s_mix", [2048, 2048], F32, kind="Internal").ap()
    xin = nc.dram_tensor("xin", [B_PER_CORE, 2048, 2048], F32, kind="ExternalInput").ap()
    xout = nc.dram_tensor("xout", [B_PER_CORE, 2048, 2048], F32, kind="ExternalOutput").ap()
    io["xin"] = [xin[b] for b in range(B_PER_CORE)]
    io["xout"] = [xout[b] for b in range(B_PER_CORE)]
    with ExitStack() as st:
        p = Prog(nc)
        streams = build_l0ffn(p, st, io)
        p.finish(streams)
        p.emit(st)
    return nc


def _hadamard(n):
    h = np.array([[1.0]])
    while h.shape[0] < n:
        h = np.block([[h, h], [h, -h]])
    return h


def host_consts_l0ffn():
    H128h = _hadamard(128).astype(ml_dtypes.bfloat16)
    H64h = _hadamard(64)
    BD = np.zeros((128, 128), dtype=np.float64)
    BD[0:64, 0:64] = H64h
    BD[64:128, 64:128] = H64h
    return {
        "identF": np.eye(128, dtype=np.float32),
        "H128": H128h,
        "BD64": BD.astype(ml_dtypes.bfloat16),
    }



def build_l1attn(p, st, io, pf="a1"):
    nc = p.nc
    e = st.enter_context
    SQT = 1.0 / float(np.sqrt(2048.0))

    identF = e(nc.sbuf_tensor(pf + "identF", [128, 128], F32))
    masks = e(nc.sbuf_tensor(pf + "masks", [128, 4 * 512], BF16))
    onesc = e(nc.sbuf_tensor(pf + "onesc", [128, 1], BF16))
    onesr = e(nc.sbuf_tensor(pf + "onesr", [1, 512], BF16))
    p.dma(pf + "cid", identF[:, :], io["identF"], writes=["identF"])
    p.dma(pf + "cmk", masks[:, :], io["masks"], writes=["masks"])
    p.dma(pf + "co1", onesc[:, :], io["ones128"], writes=["onesc"])
    p.dma(pf + "co2", onesr[:, :], io["ones1x512"], writes=["onesr"])

    psA = e(nc.psum_tensor(pf + "psA", [128, 2048], F32))
    psB = e(nc.psum_tensor(pf + "psB", [128, 2048], F32))
    psAb = ["psA0", "psA1", "psA2", "psA3"]
    psBb = ["psB0", "psB1", "psB2", "psB3"]

    XHT = e(nc.sbuf_tensor(pf + "XHT", [128, 16 * 2048], BF16))
    WQB = e(nc.sbuf_tensor(pf + "WQB", [128, 16 * 256], BF16))
    WKB = e(nc.sbuf_tensor(pf + "WKB", [128, 16 * 256], BF16))
    WVB = e(nc.sbuf_tensor(pf + "WVB", [128, 16 * 256], BF16))
    QTG = e(nc.sbuf_tensor(pf + "QTG", [128, 2 * 2048], BF16))
    KTG = e(nc.sbuf_tensor(pf + "KTG", [128, 2 * 2048], BF16))
    VG = e(nc.sbuf_tensor(pf + "VG", [128, 16 * 256], BF16))
    EXP2 = e(nc.sbuf_tensor(pf + "EXP", [128, 2 * 16 * 512], BF16))
    oth = e(nc.sbuf_tensor(pf + "oth", [128, 2048], BF16))
    zr = e(nc.sbuf_tensor(pf + "zr", [128, 512], BF16))
    zrowf = e(nc.sbuf_tensor(pf + "zrowf", [1, 512], F32))
    zrow = e(nc.sbuf_tensor(pf + "zrow", [1, 512], BF16))
    sufr = e(nc.sbuf_tensor(pf + "sufr", [1, 6 * 128], BF16))
    xt = e(nc.sbuf_tensor(pf + "xt", [128, 2 * 2048], F32))
    colsb = e(nc.sbuf_tensor(pf + "cols", [128, 16], F32))
    xn = WVB[:, 0:4096].bitcast(F32)
    mo = EXP2[:, 0:1024].bitcast(F32)
    lnm = xn
    mix = xt[:, 2048:4096]
    xr = xt[:, 0:2048]
    wstage = XHT[:, 0:4096].bitcast(F32)
    wsbf = XHT[:, 4096:6144]
    LHS = KTG[:, 0:2048]
    WOB = XHT[:, 0:8192]
    XAL = [pf + "XHT0", pf + "XHT1", pf + "XHT2", pf + "XHT3",
           pf + "wstage", pf + "wsbf"]

    for (wsrc, wdst, stw) in ((io["Wq"], io["s_wqb"], "q"), (io["Wk"], io["s_wkb"], "k"),
                              (io["Wv"], io["s_wvb"], "v"), (io["Wo"], io["s_wob"], "o")):
        for k in range(16):
            p.dma(pf + "wc" + stw, wstage, wsrc[k * 128:(k + 1) * 128, :],
                  writes=[pf + "wstage"])
            p.op("V", lambda e_: e_.tensor_copy(wsbf, wstage),
                 reads=[pf + "wstage"], writes=[pf + "wsbf"])
            p.dma(pf + "ws" + stw, wdst[k * 128:(k + 1) * 128, :], wsbf,
                  reads=[pf + "wsbf"], writes=[pf + "wcd" + stw])

    wqv = io["s_wqb"].rearrange("(k q) o -> q k o", q=128)
    wkv = io["s_wkb"].rearrange("(k q) o -> q k o", q=128)
    wvv = io["s_wvb"].rearrange("(k q) o -> q k o", q=128)
    wov = io["s_wob"].rearrange("(h q) o -> q h o", q=128)
    soutv = io["s_outT"]

    for b in range(B_PER_CORE):
        xin = io["xin"][b]
        xout = io["xout"][b]
        smix = io["s_mix"]
        for i in range(16):
            s = i % 2
            xw = [pf + f"xt{s}"] + ([pf + "xr"] if i == 0 else [pf + "mix"] if i == 1 else [])
            p.dma(pf + "xld" + str(s), sl2(xt, s), xin[i * 128:(i + 1) * 128, :], writes=xw)
            ln_normalize(p, sl2(xt, s), xn[:, :], [pf + f"xt{s}"], [pf + "xn", pf + "WVB"],
                         colsb[:, s * 8:(s + 1) * 8], pf + f"cols{s}")
            for k in range(16):
                bank = k % 4
                xwr = [pf + f"XHT{k}"] + ([pf + "wstage", pf + "wsbf"] if k < 3 and i == 0 else [])
                p.op("T", lambda e_, k=k, bank=bank: e_.transpose(
                    psB[:, bank * 512:bank * 512 + 128],
                    xn[:, k * 128:(k + 1) * 128], identF[:, :]),
                    reads=[pf + "xn", "identF"], writes=[psBb[bank]], inc=False)
                p.op("S", lambda e_, k=k, bank=bank, i=i: e_.copy(
                    XHT[:, k * 2048 + i * 128:k * 2048 + i * 128 + 128],
                    psB[:, bank * 512:bank * 512 + 128]),
                    reads=[psBb[bank]], writes=xwr)
        for g2 in range(8):
            co = g2 * 256
            p.dma(pf + "wqbl", WQB[:, :], wqv[:, :, co:co + 256],
                  reads=[pf + "wcdq"], writes=[pf + "WQB"])
            p.dma(pf + "wkbl", WKB[:, :], wkv[:, :, co:co + 256],
                  reads=[pf + "wcdk"], writes=[pf + "WKB"])
            p.dma(pf + "wvbl", WVB[:, :], wvv[:, :, co:co + 256],
                  reads=[pf + "wcdv"], writes=[pf + "WVB", pf + "xn", pf + "lnm"])
            for l in range(2):
                for tc in range(4):
                    pb = (4 * l + tc) % 4
                    for k in range(16):
                        p.op("T", lambda e_, l=l, tc=tc, k=k, pb=pb: e_.matmul(
                            psA[:, pb * 512:(pb + 1) * 512],
                            WQB[:, k * 256 + l * 128:k * 256 + l * 128 + 128],
                            XHT[:, k * 2048 + tc * 512:k * 2048 + (tc + 1) * 512],
                            start=(k == 0), stop=(k == 15)),
                            reads=[pf + "WQB", pf + f"XHT{k}"], writes=[psAb[pb]], inc=False)
                    p.op("S", lambda e_, l=l, tc=tc, pb=pb: e_.copy(
                        QTG[:, l * 2048 + tc * 512:l * 2048 + (tc + 1) * 512],
                        psA[:, pb * 512:(pb + 1) * 512]),
                        reads=[psAb[pb]], writes=[pf + f"QTG{l}"])
                    for k in range(16):
                        p.op("T", lambda e_, l=l, tc=tc, k=k, pb=pb: e_.matmul(
                            psB[:, pb * 512:(pb + 1) * 512],
                            WKB[:, k * 256 + l * 128:k * 256 + l * 128 + 128],
                            XHT[:, k * 2048 + tc * 512:k * 2048 + (tc + 1) * 512],
                            start=(k == 0), stop=(k == 15)),
                            reads=[pf + "WKB", pf + f"XHT{k}"], writes=[psBb[pb]], inc=False)
                    p.op("S", lambda e_, l=l, tc=tc, pb=pb: e_.copy(
                        KTG[:, l * 2048 + tc * 512:l * 2048 + (tc + 1) * 512],
                        psB[:, pb * 512:(pb + 1) * 512]),
                        reads=[psBb[pb]], writes=[pf + f"KTG{l}", pf + "LHS"])
            for i in range(16):
                pb = i % 4
                for k in range(16):
                    p.op("T", lambda e_, i=i, k=k, pb=pb: e_.matmul(
                        psA[:, pb * 512:pb * 512 + 256],
                        XHT[:, k * 2048 + i * 128:k * 2048 + i * 128 + 128],
                        WVB[:, k * 256:(k + 1) * 256],
                        start=(k == 0), stop=(k == 15)),
                        reads=[pf + "WVB", pf + f"XHT{k}"], writes=[psAb[pb]], inc=False)
                p.op("S", lambda e_, i=i, pb=pb: e_.copy(
                    VG[:, i * 256:(i + 1) * 256], psA[:, pb * 512:pb * 512 + 256]),
                    reads=[psAb[pb]], writes=[pf + f"VG{i}"])
            for l in range(2):
                h = g2 * 2 + l
                EXP = EXP2[:, l * 8192:(l + 1) * 8192]
                EB = pf + f"E{l}_"
                qT = QTG[:, l * 2048:(l + 1) * 2048]
                kT = KTG[:, l * 2048:(l + 1) * 2048]
                for cc in range(3):
                    sbs = list(range(4 * cc + 4, 16))
                    for j, sb in enumerate(sbs):
                        p.op("T", lambda e_, sb=sb, cc=cc, j=j, n=len(sbs), l=l: e_.matmul(
                            psB[0:1, (1 + cc) * 512:(1 + cc) * 512 + 128], onesc[:, :],
                            VG[:, sb * 256 + l * 128:sb * 256 + l * 128 + 128],
                            start=(j == 0), stop=(j == n - 1)),
                            reads=[pf + f"VG{sb}", "onesc"], writes=[psBb[1 + cc]], inc=False)
                    p.op("S", lambda e_, cc=cc, l=l: e_.copy(
                        sufr[:, (l * 3 + cc) * 128:(l * 3 + cc + 1) * 128],
                        psB[0:1, (1 + cc) * 512:(1 + cc) * 512 + 128]),
                        reads=[psBb[1 + cc]], writes=[pf + "sufr"])
                for c in range(4):
                    smax = 4 * c + 3
                    osl = c % 2
                    ssl = 2 + (c % 2)
                    for sb in range(smax + 1):
                        p.op("T", lambda e_, sb=sb, c=c, ssl=ssl, kT=kT, qT=qT: e_.matmul(
                            psA[:, ssl * 512:(ssl + 1) * 512],
                            kT[:, sb * 128:(sb + 1) * 128],
                            qT[:, c * 512:(c + 1) * 512], start=True, stop=True),
                            reads=[pf + f"KTG{l}", pf + f"QTG{l}"], writes=[psAb[ssl]], inc=False)
                        off = sb - 4 * c
                        if off >= 0:
                            p.op("V", lambda e_, off=off, ssl=ssl: e_.tensor_tensor(
                                psA[:, ssl * 512:(ssl + 1) * 512],
                                psA[:, ssl * 512:(ssl + 1) * 512],
                                masks[:, off * 512:(off + 1) * 512], ALU.mult),
                                reads=[psAb[ssl], "masks"], writes=[psAb[ssl]])
                        p.op("S", lambda e_, sb=sb, ssl=ssl, EXP=EXP: e_.activation(
                            EXP[:, sb * 512:(sb + 1) * 512],
                            psA[:, ssl * 512:(ssl + 1) * 512], AF.Exp, scale=SQT),
                            reads=[psAb[ssl]],
                            writes=[EB + f"EXP{sb}"] + ([pf + "mo0"] if sb < 2 and l == 0 else []))
                    for sb in range(smax + 1):
                        p.op("T", lambda e_, sb=sb, smax=smax, EXP=EXP: e_.matmul(
                            psB[0:1, 0:512], onesc[:, :], EXP[:, sb * 512:(sb + 1) * 512],
                            start=(sb == 0), stop=(sb == smax)),
                            reads=[EB + f"EXP{sb}", "onesc"], writes=[psBb[0]], inc=False)
                    cnt = float(2048 - 512 * (c + 1))
                    p.op("V", lambda e_, cnt=cnt: e_.tensor_scalar_add(
                        zrowf[0:1, :], psB[0:1, 0:512], cnt),
                        reads=[psBb[0]], writes=[pf + "zrowf"])
                    p.op("V", lambda e_: e_.reciprocal(zrowf[0:1, :], zrowf[0:1, :]),
                         reads=[pf + "zrowf"], writes=[pf + "zrowf"])
                    p.op("V", lambda e_: e_.tensor_copy(zrow[0:1, :], zrowf[0:1, :]),
                         reads=[pf + "zrowf"], writes=[pf + "zrow"])
                    for sb in range(smax + 1):
                        p.op("T", lambda e_, sb=sb, c=c, osl=osl, l=l, smax=smax, EXP=EXP: e_.matmul(
                            psA[:, osl * 512:(osl + 1) * 512],
                            VG[:, sb * 256 + l * 128:sb * 256 + l * 128 + 128],
                            EXP[:, sb * 512:(sb + 1) * 512],
                            start=(sb == 0), stop=(sb == smax and c == 3)),
                            reads=[pf + f"VG{sb}", EB + f"EXP{sb}"], writes=[psAb[osl]], inc=False)
                    if c < 3:
                        p.op("T", lambda e_, c=c, osl=osl, l=l: e_.matmul(
                            psA[:, osl * 512:(osl + 1) * 512],
                            sufr[0:1, (l * 3 + c) * 128:(l * 3 + c + 1) * 128],
                            onesr[:, :], start=False, stop=True),
                            reads=[pf + "sufr", "onesr"], writes=[psAb[osl]], inc=False)
                    p.op("T", lambda e_: e_.matmul(
                        psB[:, 1 * 512:2 * 512], onesr[0:1, 0:128], zrow[0:1, :],
                        start=True, stop=True),
                        reads=[pf + "zrow", "onesr"], writes=[psBb[1]], inc=False)
                    p.op("S", lambda e_: e_.copy(zr[:, :], psB[:, 1 * 512:2 * 512]),
                         reads=[psBb[1]], writes=[pf + "zr"])
                    p.op("V", lambda e_, c=c, osl=osl: e_.tensor_tensor(
                        oth[:, c * 512:(c + 1) * 512], psA[:, osl * 512:(osl + 1) * 512],
                        zr[:, :], ALU.mult),
                        reads=[psAb[osl], pf + "zr"], writes=[pf + "oth"])
                p.dma(pf + "ost", soutv[h], oth[:, :], reads=[pf + "oth"],
                      writes=[pf + f"soth{h}"])
        for dc in range(4):
            p.dma(pf + "wobl", WOB[:, :], wov[:, :, dc * 512:(dc + 1) * 512],
                  reads=[pf + "wcdo"] + [pf + f"soth{h}" for h in range(16)],
                  writes=[pf + "WOB"] + XAL)
            for i in range(16):
                for q4 in range(4):
                    p.dma(pf + "lhl" + str(q4),
                          LHS[:, q4 * 512:(q4 + 1) * 512],
                          soutv[:, :, i * 128:(i + 1) * 128].transpose([1, 0, 2])[:, q4 * 4:(q4 + 1) * 4, :],
                          reads=[pf + f"soth{q4 * 4 + j}" for j in range(4)],
                          writes=[pf + f"LHSq{q4}", pf + "KTG0", pf + "KTG1", pf + "LHS"])
                pb = i % 4
                for h in range(16):
                    p.op("T", lambda e_, h=h, pb=pb: e_.matmul(
                        psA[:, pb * 512:(pb + 1) * 512], LHS[:, h * 128:(h + 1) * 128],
                        WOB[:, h * 512:(h + 1) * 512], start=(h == 0), stop=(h == 15)),
                        reads=[pf + f"LHSq{h // 4}", pf + "WOB"], writes=[psAb[pb]], inc=False)
                p.op("S", lambda e_, pb=pb: e_.copy(
                    mo[:, :], psA[:, pb * 512:(pb + 1) * 512]),
                    reads=[psAb[pb]], writes=[pf + "mo0", pf + "E0_EXP0", pf + "E0_EXP1"])
                p.dma(pf + "most0", smix[i * 128:(i + 1) * 128, dc * 512:(dc + 1) * 512],
                      mo[:, :], reads=[pf + "mo0"],
                      writes=[pf + f"smx{i}_{dc}"])
        for i in range(16):
            p.dma(pf + "mxl", mix, smix[i * 128:(i + 1) * 128, :],
                  reads=[pf + f"smx{i}_{d_}" for d_ in range(4)],
                  writes=[pf + "mix", pf + "xt1"])
            p.dma(pf + "xrl", xr, xin[i * 128:(i + 1) * 128, :],
                  writes=[pf + "xr", pf + "xt0"])
            ln_normalize(p, mix, lnm[:, :], [pf + "mix", pf + "xt1"],
                         [pf + "lnm", pf + "xn", pf + "WVB"], colsb[:, 0:8], pf + "cols0")
            p.op("V", lambda e_: e_.tensor_tensor(mix, xr, lnm[:, :], ALU.add),
                 reads=[pf + "xr", pf + "xt0", pf + "lnm", pf + "xn"],
                 writes=[pf + "mix", pf + "xt1"])
            p.dma(pf + "xst", xout[i * 128:(i + 1) * 128, :], mix,
                  reads=[pf + "mix", pf + "xt1"], writes=[pf + f"xo{b}_{i}"])
    return [pf + "xst", pf + "most0", pf + "ost"]


def mk_l1attn_program():
    nc = bass.Bass("TRN2", target_bir_lowering=False)
    io = {}
    for nm in ("Wq", "Wk", "Wv", "Wo"):
        io[nm] = nc.dram_tensor(nm, [2048, 2048], F32, kind="ExternalInput").ap()
    io["identF"] = nc.dram_tensor("identF", [128, 128], F32, kind="ExternalInput").ap()
    io["masks"] = nc.dram_tensor("masks", [128, 4 * 512], BF16, kind="ExternalInput").ap()
    io["ones128"] = nc.dram_tensor("ones128", [128, 1], BF16, kind="ExternalInput").ap()
    io["ones1x512"] = nc.dram_tensor("ones1x512", [1, 512], BF16, kind="ExternalInput").ap()
    for nm in ("s_wqb", "s_wkb", "s_wvb", "s_wob"):
        io[nm] = nc.dram_tensor(nm, [2048, 2048], BF16, kind="Internal").ap()
    io["s_outT"] = nc.dram_tensor("s_outT", [16, 128, 2048], BF16, kind="Internal").ap()
    io["s_mix"] = nc.dram_tensor("s_mix", [2048, 2048], F32, kind="Internal").ap()
    xin = nc.dram_tensor("xin", [B_PER_CORE, 2048, 2048], F32, kind="ExternalInput").ap()
    xout = nc.dram_tensor("xout", [B_PER_CORE, 2048, 2048], F32, kind="ExternalOutput").ap()
    io["xin"] = [xin[b] for b in range(B_PER_CORE)]
    io["xout"] = [xout[b] for b in range(B_PER_CORE)]
    with ExitStack() as st:
        p = Prog(nc)
        streams = build_l1attn(p, st, io)
        p.finish(streams)
        p.emit(st)
    return nc


def host_consts_l1attn():
    m = np.zeros((128, 4 * 512), dtype=np.float32)
    for off in range(4):
        s = np.arange(128)[:, None]
        t = np.arange(512)[None, :]
        m[:, off * 512:(off + 1) * 512] = (s + 128 * off <= t).astype(np.float32)
    return {
        "identF": np.eye(128, dtype=np.float32),
        "masks": m.astype(ml_dtypes.bfloat16),
        "ones128": np.ones((128, 1), dtype=ml_dtypes.bfloat16),
        "ones1x512": np.ones((1, 512), dtype=ml_dtypes.bfloat16),
    }



def build_l0attn(p, st, io, pf="a0"):
    nc = p.nc
    e = st.enter_context

    identF = e(nc.sbuf_tensor(pf + "identF", [128, 128], F32))
    sel64 = e(nc.sbuf_tensor(pf + "sel64", [64, 128], BF16))
    EJT = e(nc.sbuf_tensor(pf + "EJT", [128, 64 * 128], BF16))
    YST = e(nc.sbuf_tensor(pf + "YST", [128, 256], BF16))
    p.dma(pf + "cid", identF[:, :], io["identF"], writes=["identF"])
    p.dma(pf + "cs64", sel64[:, :], io["sel64"], writes=["sel64"])
    p.dma(pf + "cej", EJT[:, :], io["EJT"], writes=["EJT"])
    p.dma(pf + "cys", YST[:, :], io["YST"], writes=["YST"])

    psA = e(nc.psum_tensor(pf + "psA", [128, 2048], F32))
    psB = e(nc.psum_tensor(pf + "psB", [128, 2048], F32))
    psAb = ["psA0", "psA1", "psA2", "psA3"]
    psBb = ["psB0", "psB1", "psB2", "psB3"]

    XT = e(nc.sbuf_tensor(pf + "XT", [128, 16 * 2048], BF16))
    LT = e(nc.sbuf_tensor(pf + "LT", [128, 16 * 2048], BF16))
    dA = e(nc.sbuf_tensor(pf + "dA", [128, 2048], F32))
    ub = e(nc.sbuf_tensor(pf + "ub", [128, 2048], BF16))
    hh = e(nc.sbuf_tensor(pf + "hh", [128, 2048], BF16))
    DTK = e(nc.sbuf_tensor(pf + "DTK", [128, 2048], BF16))
    BREP = e(nc.sbuf_tensor(pf + "BREP", [128, 2048], BF16))
    CREP = e(nc.sbuf_tensor(pf + "CREP", [128, 512], BF16))
    ALC = e(nc.sbuf_tensor(pf + "ALC", [128, 1024], F32))
    WBB = e(nc.sbuf_tensor(pf + "WBB", [128, 2 * 2048], BF16))
    DCOL = e(nc.sbuf_tensor(pf + "DCOL", [128, 32], F32))
    xt = e(nc.sbuf_tensor(pf + "xt", [128, 2 * 2048], F32))
    DPJ = e(nc.sbuf_tensor(pf + "DPJ", [128, 2048], BF16))
    DRT = e(nc.sbuf_tensor(pf + "DRT", [128, 2048], BF16))
    CT = e(nc.sbuf_tensor(pf + "CT", [64, 512], BF16))
    colsb = e(nc.sbuf_tensor(pf + "cols", [128, 16], F32))
    BT = DTK[0:64, :]
    XPJ = hh
    xn = dA[:, 0:2048]
    stg = dA[:, 0:2048]
    OPB = dA.bitcast(BF16)
    mo = ub.bitcast(F32)
    mix = xt[:, 2048:4096]
    xr = xt[:, 0:2048]
    RDA = [pf + "dA", pf + "xn", pf + "OPB"]
    RXT0 = [pf + "xt0", pf + "xr"]
    RXT1 = [pf + "xt1", pf + "mix"]
    RUB = [pf + "ub", pf + "mo"]

    for b in range(B_PER_CORE):
        xin = io["xin"][b]
        xout = io["xout"][b]
        smix = io["s_mix"]
        if b == 0:
            p.dma(pf + "cwb", WBB[:, 0:2048],
                  io["norm_w"].unsqueeze(0).partition_broadcast(128), engine="G",
                  writes=[pf + "WBB"])
            p.dma(pf + "cbb", WBB[:, 2048:4096],
                  io["norm_b"].unsqueeze(0).partition_broadcast(128), engine="G",
                  writes=[pf + "WBB"])
            p.dma(pf + "cdc", DCOL[:, 0:16], io["dcols"], writes=[pf + "DCOL"])
            p.dma(pf + "cbc", DCOL[:, 16:32], io["bcols"], writes=[pf + "DCOL"])
            p.dma(pf + "cal", ALC[:, :], io["alogcols"], writes=[pf + "ALC"])
            p.op("S", lambda e_: e_.activation(ALC[:, :], ALC[:, :], AF.Exp),
                 reads=[pf + "ALC"], writes=[pf + "ALC"])
            p.op("V", lambda e_: e_.tensor_scalar_mul(ALC[:, :], ALC[:, :], -1.0),
                 reads=[pf + "ALC"], writes=[pf + "ALC"])
            for k in range(16):
                p.dma(pf + "opl", stg[:, :], io["oprojT"][k * 128:(k + 1) * 128, :],
                      reads=RDA, writes=RDA)
                p.op("V", lambda e_: e_.tensor_copy(ub[:, :], stg[:, :]),
                     reads=RDA, writes=RUB)
                p.dma(pf + "ops", io["s_oprojb"][k * 128:(k + 1) * 128, :], ub[:, :],
                      reads=RUB, writes=[pf + "opd"])
        for i in range(16):
            s = 0
            p.dma(pf + "xld0", sl2(xt, 0), xin[i * 128:(i + 1) * 128, :], writes=RXT0)
            ln_normalize(p, sl2(xt, 0), xn[:, :], RXT0, RDA,
                         colsb[:, 0:8], pf + "cols0")
            for k in range(16):
                bank = k % 4
                p.op("T", lambda e_, k=k, bank=bank: e_.transpose(
                    psB[:, bank * 512:bank * 512 + 128],
                    xn[:, k * 128:(k + 1) * 128], identF[:, :]),
                    reads=RDA + ["identF"], writes=[psBb[bank]], inc=False)
                p.op("S", lambda e_, k=k, bank=bank, i=i: e_.activation(
                    LT[:, k * 2048 + i * 128:k * 2048 + i * 128 + 128],
                    psB[:, bank * 512:bank * 512 + 128], AF.Identity,
                    scale=DCOL[:, k:k + 1]),
                    reads=[psBb[bank], pf + "DCOL"], writes=[pf + f"LT{k}"])
            ln_normalize(p, xn[:, :], xt[:, 2048:4096], RDA, RXT1,
                         colsb[:, 8:16], pf + "cols1")
            p.op("V", lambda e_: e_.tensor_tensor(xt[:, 2048:4096], xt[:, 2048:4096], WBB[:, 0:2048], ALU.mult),
                 reads=RXT1 + [pf + "WBB"], writes=RXT1)
            p.op("V", lambda e_: e_.tensor_tensor(xt[:, 2048:4096], xt[:, 2048:4096], WBB[:, 2048:4096], ALU.add),
                 reads=RXT1 + [pf + "WBB"], writes=RXT1)
            for k in range(16):
                bank = k % 4
                p.op("T", lambda e_, k=k, bank=bank: e_.transpose(
                    psB[:, bank * 512:bank * 512 + 128],
                    xt[:, 2048 + k * 128:2048 + (k + 1) * 128], identF[:, :]),
                    reads=RXT1 + ["identF"], writes=[psBb[bank]], inc=False)
                p.op("S", lambda e_, k=k, bank=bank, i=i: e_.copy(
                    XT[:, k * 2048 + i * 128:k * 2048 + i * 128 + 128],
                    psB[:, bank * 512:bank * 512 + 128]),
                    reads=[psBb[bank]], writes=[pf + f"XT{k}"])
        xpv = io["xprojT"].rearrange("(k q) o -> q k o", q=128)
        for half in range(2):
            p.dma(pf + "xpl", stg[:, 0:2048], xpv[:, half * 8:(half + 1) * 8, :],
                  reads=RDA, writes=RDA)
            p.op("V", lambda e_, half=half: e_.tensor_copy(XPJ[:, :], stg[:, 0:2048]),
                 reads=RDA, writes=[pf + "XPJ", pf + "hh"])
            for tc in range(4):
                for kk in range(8):
                    k = half * 8 + kk
                    p.op("T", lambda e_, tc=tc, k=k, kk=kk: e_.matmul(
                        psB[:, tc * 512:(tc + 1) * 512],
                        XPJ[:, kk * 256:kk * 256 + 128],
                        XT[:, k * 2048 + tc * 512:k * 2048 + (tc + 1) * 512],
                        start=(k == 0), stop=(k == 15)),
                        reads=[pf + "XPJ", pf + "hh", pf + f"XT{k}"], writes=[psBb[tc]], inc=False)
        for tc in range(4):
            p.op("S", lambda e_, tc=tc: e_.copy(
                DRT[:, tc * 512:(tc + 1) * 512], psB[:, tc * 512:(tc + 1) * 512]),
                reads=[psBb[tc]], writes=[pf + "DRT"])
        for half in range(2):
            p.dma(pf + "xpl", stg[:, 0:2048], xpv[:, half * 8:(half + 1) * 8, :],
                  reads=RDA, writes=RDA)
            p.op("V", lambda e_, half=half: e_.tensor_copy(XPJ[:, :], stg[:, 0:2048]),
                 reads=RDA, writes=[pf + "XPJ", pf + "hh"])
            for tc in range(4):
                for kk in range(8):
                    k = half * 8 + kk
                    p.op("T", lambda e_, tc=tc, k=k, kk=kk: e_.matmul(
                        psB[0:64, tc * 512:(tc + 1) * 512],
                        XPJ[:, kk * 256 + 128:kk * 256 + 192],
                        XT[:, k * 2048 + tc * 512:k * 2048 + (tc + 1) * 512],
                        start=(k == 0), stop=(k == 15)),
                        reads=[pf + "XPJ", pf + "hh", pf + f"XT{k}"], writes=[psBb[tc]], inc=False)
        for tc in range(4):
            p.op("S", lambda e_, tc=tc: e_.copy(
                BT[:, tc * 512:(tc + 1) * 512], psB[0:64, tc * 512:(tc + 1) * 512]),
                reads=[psBb[tc]], writes=[pf + "BT", pf + "DTK"])
        for half in range(2):
            p.dma(pf + "xpl", stg[:, 0:2048], xpv[:, half * 8:(half + 1) * 8, :],
                  reads=RDA, writes=RDA)
            p.op("V", lambda e_, half=half: e_.tensor_copy(XPJ[:, :], stg[:, 0:2048]),
                 reads=RDA, writes=[pf + "XPJ", pf + "hh"])
            for kk in range(8):
                k = half * 8 + kk
                p.op("T", lambda e_, k=k, kk=kk: e_.matmul(
                    psB[0:64, 0:512],
                    XPJ[:, kk * 256 + 192:kk * 256 + 256],
                    XT[:, k * 2048 + 0:k * 2048 + 512],
                    start=(k == 0), stop=(k == 15)),
                    reads=[pf + "XPJ", pf + "hh", pf + f"XT{k}"], writes=[psBb[0]], inc=False)
        p.op("S", lambda e_: e_.copy(CT[:, :], psB[0:64, 0:512]),
             reads=[psBb[0]], writes=[pf + "CT"])
        for tc in range(4):
            p.op("T", lambda e_, tc=tc: e_.matmul(
                psB[:, tc * 512:(tc + 1) * 512], sel64[:, :],
                BT[:, tc * 512:(tc + 1) * 512], start=True, stop=True),
                reads=[pf + "BT", pf + "DTK", "sel64"], writes=[psBb[tc]], inc=False)
            p.op("S", lambda e_, tc=tc: e_.copy(
                BREP[:, tc * 512:(tc + 1) * 512], psB[:, tc * 512:(tc + 1) * 512]),
                reads=[psBb[tc]], writes=[pf + "BREP"])
        p.op("T", lambda e_: e_.matmul(psA[:, 0:512], sel64[:, :], CT[:, :],
                                       start=True, stop=True),
             reads=[pf + "CT", "sel64"], writes=[psAb[0], pf + "psAc"], inc=False)
        p.op("S", lambda e_: e_.copy(CREP[:, :], psA[:, 0:512]),
             reads=[psAb[0]], writes=[pf + "CREP"])
        p.dma(pf + "dpl", stg[:, :], io["dtprojT"], reads=RDA, writes=RDA)
        p.op("V", lambda e_: e_.tensor_copy(DPJ[:, :], stg[:, :]),
             reads=RDA, writes=[pf + "DPJ"])
        for k in range(16):
            for c in range(4):
                p.op("T", lambda e_, k=k, c=c: e_.matmul(
                    psA[:, c * 512:(c + 1) * 512],
                    DPJ[:, k * 128:(k + 1) * 128],
                    DRT[:, c * 512:(c + 1) * 512], start=True, stop=True),
                    reads=[pf + "DPJ", pf + "DRT"], writes=[psAb[c]], inc=False)
                p.op("V", lambda e_, k=k, c=c: e_.tensor_scalar(
                    psB[:, c * 512:(c + 1) * 512], psA[:, c * 512:(c + 1) * 512],
                    DCOL[:, 16 + k:17 + k], None, ALU.add),
                    reads=[psAb[c], pf + "DCOL"], writes=[psBb[c]])
                p.op("S", lambda e_, c=c: e_.activation(
                    psA[:, c * 512:(c + 1) * 512], psB[:, c * 512:(c + 1) * 512], AF.Abs),
                    reads=[psBb[c]], writes=[psAb[c]])
                p.op("S", lambda e_, c=c: e_.activation(
                    dA[:, c * 512:(c + 1) * 512], psA[:, c * 512:(c + 1) * 512],
                    AF.Exp, scale=-1.0),
                    reads=[psAb[c]], writes=RDA)
                p.op("V", lambda e_, c=c: e_.tensor_scalar_add(
                    dA[:, c * 512:(c + 1) * 512], dA[:, c * 512:(c + 1) * 512], 1.0),
                    reads=RDA, writes=RDA)
                p.op("S", lambda e_, c=c: e_.activation(
                    dA[:, c * 512:(c + 1) * 512], dA[:, c * 512:(c + 1) * 512], AF.Ln),
                    reads=RDA, writes=RDA)
                p.op("V", lambda e_, c=c: e_.tensor_scalar_max(
                    psB[:, c * 512:(c + 1) * 512], psB[:, c * 512:(c + 1) * 512], 0.0),
                    reads=[psBb[c]], writes=[psBb[c]])
                p.op("V", lambda e_, k=k, c=c: e_.tensor_tensor(
                    DTK[:, c * 512:(c + 1) * 512], psB[:, c * 512:(c + 1) * 512],
                    dA[:, c * 512:(c + 1) * 512], ALU.add),
                    reads=[psBb[c]] + RDA, writes=[pf + "DTK"])
                p.op("V", lambda e_, k=k, c=c: e_.tensor_tensor(
                    XT[:, k * 2048 + c * 512:k * 2048 + (c + 1) * 512],
                    DTK[:, c * 512:(c + 1) * 512],
                    XT[:, k * 2048 + c * 512:k * 2048 + (c + 1) * 512], ALU.mult),
                    reads=[pf + "DTK", pf + f"XT{k}"], writes=[pf + f"XT{k}"])
            for j in range(64):
                J = k * 64 + j
                for c in range(4):
                    p.op("T", lambda e_, j=j, c=c: e_.matmul(
                        psA[:, c * 512:(c + 1) * 512], EJT[:, j * 128:(j + 1) * 128],
                        DTK[:, c * 512:(c + 1) * 512], start=True, stop=True),
                        reads=[pf + "DTK", "EJT"], writes=[psAb[c]], inc=False)
                p.op("S", lambda e_, J=J: e_.activation(
                    dA[:, 0:2048], psA[:, 0:2048], AF.Exp, scale=ALC[:, J:J + 1]),
                    reads=psAb + [pf + "ALC"], writes=RDA)
                for c in range(4):
                    p.op("T", lambda e_, j=j, k=k, c=c: e_.matmul(
                        psA[:, c * 512:(c + 1) * 512], EJT[:, j * 128:(j + 1) * 128],
                        XT[:, k * 2048 + c * 512:k * 2048 + (c + 1) * 512],
                        start=True, stop=True),
                        reads=[pf + f"XT{k}", "EJT"], writes=[psAb[c]], inc=False)
                p.op("V", lambda e_: e_.tensor_tensor(
                    ub[:, :], psA[:, 0:2048], BREP[:, :], ALU.mult),
                    reads=psAb + [pf + "BREP"], writes=[pf + "ub"])
                p.op("V", lambda e_: e_.tensor_tensor_scan(
                    hh[:, :], dA[:, 0:2048], ub[:, :], 0.0, ALU.mult, ALU.add),
                    reads=RDA + [pf + "ub"], writes=[pf + "hh", pf + "XPJ"])
                p.op("V", lambda e_: e_.tensor_tensor(
                    hh[:, :].rearrange("q (a b) -> q a b", a=4),
                    hh[:, :].rearrange("q (a b) -> q a b", a=4),
                    CREP[:, :].unsqueeze(1).broadcast_to([128, 4, 512]), ALU.mult),
                    reads=[pf + "hh", pf + "CREP"], writes=[pf + "hh"])
                for c in range(4):
                    p.op("T", lambda e_, j=j, c=c: e_.matmul(
                        psB[:, c * 512:(c + 1) * 512],
                        YST[:, 126 - 2 * j:254 - 2 * j],
                        hh[:, c * 512:(c + 1) * 512],
                        start=(j == 0), stop=(j == 63)),
                        reads=[pf + "hh", "YST"], writes=[psBb[c]], inc=False)
            for c in range(4):
                p.op("V", lambda e_, k=k, c=c: e_.tensor_tensor(
                    LT[:, k * 2048 + c * 512:k * 2048 + (c + 1) * 512],
                    psB[:, c * 512:(c + 1) * 512],
                    LT[:, k * 2048 + c * 512:k * 2048 + (c + 1) * 512], ALU.add),
                    reads=[psBb[c], pf + f"LT{k}"], writes=[pf + f"LT{k}"])
        opv = io["s_oprojb"].rearrange("(k q) o -> q k o", q=128)
        for dc in range(4):
            for ig in range(4):
                for half in range(2):
                    p.dma(pf + "opbl", OPB[:, :],
                          opv[:, half * 8:(half + 1) * 8, dc * 512:(dc + 1) * 512],
                          reads=[pf + "opd"] + RDA, writes=RDA)
                    for ii in range(4):
                        i = ig * 4 + ii
                        pb = ii
                        for kk in range(8):
                            k = half * 8 + kk
                            p.op("T", lambda e_, i=i, k=k, kk=kk, pb=pb: e_.matmul(
                                psA[:, pb * 512:(pb + 1) * 512],
                                LT[:, k * 2048 + i * 128:k * 2048 + i * 128 + 128],
                                OPB[:, kk * 512:(kk + 1) * 512],
                                start=(k == 0), stop=(k == 15)),
                                reads=[pf + f"LT{k}"] + RDA, writes=[psAb[pb]], inc=False)
                for ii in range(4):
                    i = ig * 4 + ii
                    pb = ii
                    ms = ii % 2
                    p.op("S", lambda e_, pb=pb, ms=ms: e_.copy(
                        mo[:, ms * 512:(ms + 1) * 512], psA[:, pb * 512:(pb + 1) * 512]),
                        reads=[psAb[pb]], writes=RUB)
                    p.dma(pf + "most" + str(ms),
                          smix[i * 128:(i + 1) * 128, dc * 512:(dc + 1) * 512],
                          mo[:, ms * 512:(ms + 1) * 512], reads=RUB,
                          writes=[pf + f"smx{i}_{dc}"])
        for i in range(16):
            p.dma(pf + "mxl", mix, smix[i * 128:(i + 1) * 128, :],
                  reads=[pf + f"smx{i}_{d_}" for d_ in range(4)],
                  writes=RXT1)
            p.dma(pf + "xrl", xr, xin[i * 128:(i + 1) * 128, :], writes=RXT0)
            ln_normalize(p, mix, xn[:, :], RXT1, RDA, colsb[:, 0:8], pf + "cols0")
            p.op("V", lambda e_: e_.tensor_tensor(mix, xr, xn[:, :], ALU.add),
                 reads=RXT0 + RDA + [pf + "mix"], writes=RXT1)
            p.dma(pf + "xst", xout[i * 128:(i + 1) * 128, :], mix,
                  reads=RXT1, writes=[pf + f"xo{b}_{i}"])
    return [pf + "xst", pf + "most0", pf + "most1"]


def mk_l0attn_program():
    nc = bass.Bass("TRN2", target_bir_lowering=False)
    io = {}
    io["xprojT"] = nc.dram_tensor("xprojT", [2048, 256], F32, kind="ExternalInput").ap()
    io["dtprojT"] = nc.dram_tensor("dtprojT", [128, 2048], F32, kind="ExternalInput").ap()
    io["oprojT"] = nc.dram_tensor("oprojT", [2048, 2048], F32, kind="ExternalInput").ap()
    io["norm_w"] = nc.dram_tensor("norm_w", [2048], F32, kind="ExternalInput").ap()
    io["norm_b"] = nc.dram_tensor("norm_b", [2048], F32, kind="ExternalInput").ap()
    io["dcols"] = nc.dram_tensor("dcols", [128, 16], F32, kind="ExternalInput").ap()
    io["bcols"] = nc.dram_tensor("bcols", [128, 16], F32, kind="ExternalInput").ap()
    io["alogcols"] = nc.dram_tensor("alogcols", [128, 1024], F32, kind="ExternalInput").ap()
    io["identF"] = nc.dram_tensor("identF", [128, 128], F32, kind="ExternalInput").ap()
    io["sel64"] = nc.dram_tensor("sel64", [64, 128], BF16, kind="ExternalInput").ap()
    io["EJT"] = nc.dram_tensor("EJT", [128, 64 * 128], BF16, kind="ExternalInput").ap()
    io["YST"] = nc.dram_tensor("YST", [128, 256], BF16, kind="ExternalInput").ap()
    io["s_oprojb"] = nc.dram_tensor("s_oprojb", [2048, 2048], BF16, kind="Internal").ap()
    io["s_mix"] = nc.dram_tensor("s_mix", [2048, 2048], F32, kind="Internal").ap()
    xin = nc.dram_tensor("xin", [B_PER_CORE, 2048, 2048], F32, kind="ExternalInput").ap()
    xout = nc.dram_tensor("xout", [B_PER_CORE, 2048, 2048], F32, kind="ExternalOutput").ap()
    io["xin"] = [xin[b] for b in range(B_PER_CORE)]
    io["xout"] = [xout[b] for b in range(B_PER_CORE)]
    with ExitStack() as st:
        p = Prog(nc)
        streams = build_l0attn(p, st, io)
        p.finish(streams)
        p.emit(st)
    return nc


def host_consts_l0attn(A_log, D_param, dt_bias):
    dcols = np.ascontiguousarray(D_param.reshape(16, 128).T)
    bcols = np.ascontiguousarray(dt_bias.reshape(16, 128).T)
    alog = np.empty((128, 1024), dtype=np.float32)
    for pp in range(128):
        J = np.arange(1024)
        d = (J // 64) * 128 + 2 * (J % 64) + (1 if pp >= 64 else 0)
        alog[pp, :] = A_log[d, pp % 64]
    sel = np.zeros((64, 128), dtype=np.float32)
    for n in range(64):
        sel[n, n] = 1.0
        sel[n, 64 + n] = 1.0
    ej = np.zeros((128, 64 * 128), dtype=np.float32)
    for j in range(64):
        ej[2 * j, j * 128 + 0:j * 128 + 64] = 1.0
        ej[2 * j + 1, j * 128 + 64:j * 128 + 128] = 1.0
    ys = np.zeros((128, 256), dtype=np.float32)
    ys[0:64, 126] = 1.0
    ys[64:128, 127] = 1.0
    return {
        "identF": np.eye(128, dtype=np.float32),
        "sel64": sel.astype(ml_dtypes.bfloat16),
        "EJT": ej.astype(ml_dtypes.bfloat16),
        "YST": ys.astype(ml_dtypes.bfloat16),
        "dcols": dcols.astype(np.float32),
        "bcols": bcols.astype(np.float32),
        "alogcols": alog.astype(np.float32),
    }


def kernel(L0_attn_x, L0_attn_norm_w, L0_attn_norm_b, L0_attn_x_proj_w,
           L0_attn_dt_proj_w, L0_attn_dt_proj_b, L0_attn_A_log, L0_attn_D_param,
           L0_attn_out_proj_w, L0_ffn_W1, L0_ffn_W2, L0_ffn_W3,
           L1_attn_Wq, L1_attn_Wk, L1_attn_Wv, L1_attn_Wo,
           L1_ffn_weight, L1_ffn_bias):
    x = np.ascontiguousarray(L0_attn_x, dtype=np.float32)
    n = NCORES

    def shards(arr):
        return [np.ascontiguousarray(arr[2 * c:2 * c + 2]) for c in range(n)]

    cn0 = host_consts_l0attn(np.asarray(L0_attn_A_log, dtype=np.float32),
                             np.asarray(L0_attn_D_param, dtype=np.float32),
                             np.asarray(L0_attn_dt_proj_b, dtype=np.float32))
    w0 = dict(
        xprojT=np.ascontiguousarray(np.asarray(L0_attn_x_proj_w, np.float32).T),
        dtprojT=np.ascontiguousarray(np.asarray(L0_attn_dt_proj_w, np.float32).T),
        oprojT=np.ascontiguousarray(np.asarray(L0_attn_out_proj_w, np.float32).T),
        norm_w=np.asarray(L0_attn_norm_w, np.float32),
        norm_b=np.asarray(L0_attn_norm_b, np.float32), **cn0)
    nc = mk_l0attn_program()
    res = run_bass_kernel_spmd(nc, [dict(xin=s, **w0) for s in shards(x)],
                               core_ids=list(range(n)))
    x = np.concatenate([r["xout"] for r in res.results], axis=0)

    W2T = np.ascontiguousarray(np.asarray(L0_ffn_W2, np.float32).T)
    w2pm = np.ascontiguousarray(
        W2T.reshape(64, 128, 2048).transpose(1, 0, 2).reshape(8192, 2048))
    w1 = dict(
        w1T=np.ascontiguousarray(np.asarray(L0_ffn_W1, np.float32).T),
        w3T=np.ascontiguousarray(np.asarray(L0_ffn_W3, np.float32).T),
        w2pm=w2pm, **host_consts_l0ffn())
    nc = mk_l0ffn_program()
    res = run_bass_kernel_spmd(nc, [dict(xin=s, **w1) for s in shards(x)],
                               core_ids=list(range(n)))
    x = np.concatenate([r["xout"] for r in res.results], axis=0)

    w2 = dict(Wq=np.asarray(L1_attn_Wq, np.float32), Wk=np.asarray(L1_attn_Wk, np.float32),
              Wv=np.asarray(L1_attn_Wv, np.float32), Wo=np.asarray(L1_attn_Wo, np.float32),
              **host_consts_l1attn())
    nc = mk_l1attn_program()
    res = run_bass_kernel_spmd(nc, [dict(xin=s, **w2) for s in shards(x)],
                               core_ids=list(range(n)))
    x = np.concatenate([r["xout"] for r in res.results], axis=0)

    w3 = dict(wT=np.ascontiguousarray(np.asarray(L1_ffn_weight, np.float32).T),
              bias=np.asarray(L1_ffn_bias, np.float32), **host_consts())
    nc = mk_l1ffn_program()
    res = run_bass_kernel_spmd(nc, [dict(xin=s, **w3) for s in shards(x)],
                               core_ids=list(range(n)))
    x = np.concatenate([r["xout"] for r in res.results], axis=0)
    return np.ascontiguousarray(x, dtype=np.float32)


# version 04
# speedup vs baseline: 1.4343x; 1.4343x over previous
import numpy as np
import ml_dtypes
from contextlib import ExitStack

import concourse.bass as bass
import concourse.mybir as mybir
from concourse.bass_utils import run_bass_kernel_spmd

F32 = mybir.dt.float32
BF16 = mybir.dt.bfloat16
AF = mybir.ActivationFunctionType
ALU = mybir.AluOpType
AX = mybir.AxisListType

NCORES = 8
B_PER_CORE = 2
T = 2048
D = 2048
EPS = 1e-5

ENGINES = ("T", "V", "S", "G", "Q")


class Prog:

    def __init__(self, nc, same_engine_sync=True):
        self.nc = nc
        self.items = {e: [] for e in ENGINES}
        self.dma_cnt = {}
        self.stream_names = []
        self.lw = {}
        self.rd = {}
        self.same_sync = same_engine_sync

    def _gather(self, reads, writes, after):
        deps = set(t for t in after if t is not None)
        for r in reads:
            t = self.lw.get(r)
            if t is not None:
                deps.add(t)
        for w in writes:
            t = self.lw.get(w)
            if t is not None:
                deps.add(t)
            for t2 in self.rd.get(w, ()):
                deps.add(t2)
        for t in deps:
            if t[0] == "E":
                self.items[t[1]][t[2]][2] = True
        return deps

    def _record_rw(self, tok, reads, writes):
        for r in reads:
            self.rd.setdefault(r, []).append(tok)
        for w in writes:
            self.lw[w] = tok
            self.rd[w] = []

    def op(self, engine, fn, reads=(), writes=(), after=(), inc=True):
        deps = self._gather(reads, writes, after)
        tok = ("E", engine, len(self.items[engine]))
        self.items[engine].append([fn, deps, inc, []])
        self._record_rw(tok, reads, writes)
        return tok

    def dma(self, stream, out, in_, engine="Q", reads=(), writes=(), after=()):
        deps = self._gather(reads, writes, after)
        idx = self.dma_cnt.get(stream, 0)
        self.dma_cnt[stream] = idx + 1
        if stream not in self.stream_names:
            self.stream_names.append(stream)
        tok = ("D", stream, idx)

        def fn(eng, out=out, in_=in_):
            return eng.dma_start(out, in_)

        self.items[engine].append([fn, deps, False, [(stream, 16)]])
        self._record_rw(tok, reads, writes)
        return tok

    def wait_only(self, engine, after):
        deps = self._gather((), (), after)
        self.items[engine].append([None, deps, False, []])

    def emit(self, stack):
        nc = self.nc
        sems = {}
        for name in list(ENGINES) + self.stream_names:
            sems[name] = stack.enter_context(nc.semaphore("sem_" + name))

        resolve = {}
        for e in ENGINES:
            n = len(self.items[e])
            nxt = [None] * n
            cnt = 0
            incs_upto = [0] * (n + 1)
            for i in range(n):
                incs_upto[i + 1] = incs_upto[i] + (1 if self.items[e][i][2] else 0)
            next_inc = [None] * (n + 1)
            last = None
            for i in range(n - 1, -1, -1):
                if self.items[e][i][2]:
                    last = i
                next_inc[i] = last
            resolve[e] = (incs_upto, next_inc)

        def wait_of(tok):
            if tok[0] == "D":
                return (tok[1], 16 * (tok[2] + 1))
            e, idx = tok[1], tok[2]
            incs_upto, next_inc = resolve[e]
            j = next_inc[idx]
            assert j is not None, f"dep on non-inc'd tail op {tok}"
            return (e, incs_upto[j + 1])

        block = stack.enter_context(nc.Block())

        def make_emitter(ename):
            ops = self.items[ename]

            def emitter(eng):
                floors = {}
                for fn, deps, inc, extra in ops:
                    waits = []
                    for t in deps:
                        if t[0] == "E" and t[1] == ename:
                            if ename == "T" or not self.same_sync:
                                continue
                        waits.append(wait_of(t))
                    for (s, v) in sorted(waits):
                        if floors.get(s, 0) >= v:
                            continue
                        eng.wait_ge(sems[s], v)
                        floors[s] = v
                    if fn is None:
                        continue
                    inst = fn(eng)
                    if inc:
                        inst.then_inc(sems[ename], 1)
                    for (s, n) in extra:
                        inst.then_inc(sems[s], n)

            return emitter

        block.tensor(make_emitter("T"))
        block.vector(make_emitter("V"))
        block.scalar(make_emitter("S"))
        block.gpsimd(make_emitter("G"))
        block.sync(make_emitter("Q"))

    def fence(self):
        toks = []
        for e in ENGINES:
            for i in range(len(self.items[e]) - 1, -1, -1):
                if not self.items[e][i][3]:
                    if self.items[e][i][0] is not None:
                        self.items[e][i][2] = True
                        toks.append(("E", e, i))
                    break
        for s, cdma in self.dma_cnt.items():
            toks.append(("D", s, cdma - 1))
        for e in ENGINES:
            deps = set(t for t in toks if not (t[0] == "E" and t[1] == e))
            self.items[e].append([None, deps, False, []])

    def finish(self, streams):
        toks = [("D", s, self.dma_cnt[s] - 1) for s in streams if s in self.dma_cnt]
        self.wait_only("Q", toks)


def _l(x):
    return [x] if isinstance(x, str) else list(x)


def ln_normalize(p, x_ap, out_ap, xbuf, outbuf, c_ap, cbuf, F=2048):
    xbuf, outbuf, cbuf = _l(xbuf), _l(outbuf), _l(cbuf)
    p.op("V", lambda e: e.reduce_sum(c_ap[:, 0:1], x_ap, axis=AX.X),
         reads=xbuf, writes=cbuf)
    p.op("S", lambda e: e.activation(out_ap, x_ap, AF.Square, accum_out=c_ap[:, 1:2]),
         reads=xbuf, writes=outbuf + cbuf)
    p.op("V", lambda e: e.tensor_scalar_mul(c_ap[:, 2:3], c_ap[:, 0:1], 1.0 / F),
         reads=cbuf, writes=cbuf)
    p.op("V", lambda e: e.tensor_scalar_mul(c_ap[:, 3:4], c_ap[:, 1:2], 1.0 / F),
         reads=cbuf, writes=cbuf)
    p.op("V", lambda e: e.tensor_tensor(c_ap[:, 6:7], c_ap[:, 2:3], c_ap[:, 2:3], ALU.mult),
         reads=cbuf, writes=cbuf)
    p.op("V", lambda e: e.tensor_tensor(c_ap[:, 3:4], c_ap[:, 3:4], c_ap[:, 6:7], ALU.subtract),
         reads=cbuf, writes=cbuf)
    p.op("V", lambda e: e.tensor_scalar_add(c_ap[:, 3:4], c_ap[:, 3:4], EPS),
         reads=cbuf, writes=cbuf)
    p.op("S", lambda e: e.activation(c_ap[:, 7:8], c_ap[:, 3:4], AF.Ln),
         reads=cbuf, writes=cbuf)
    p.op("S", lambda e: e.activation(c_ap[:, 4:5], c_ap[:, 7:8], AF.Exp, scale=-0.5),
         reads=cbuf, writes=cbuf)
    p.op("V", lambda e: e.tensor_tensor(c_ap[:, 6:7], c_ap[:, 2:3], c_ap[:, 4:5], ALU.mult),
         reads=cbuf, writes=cbuf)
    p.op("V", lambda e: e.tensor_scalar_mul(c_ap[:, 5:6], c_ap[:, 6:7], -1.0),
         reads=cbuf, writes=cbuf)
    p.op("S", lambda e: e.activation(out_ap, x_ap, AF.Identity, scale=c_ap[:, 4:5], bias=c_ap[:, 5:6]),
         reads=xbuf + cbuf, writes=outbuf)



def build_l1ffn(p, st, io, pf="f1"):
    nc = p.nc
    e = st.enter_context

    identF = e(nc.sbuf_tensor(pf + "identF", [128, 128], F32))
    p.dma(pf + "cid", identF[:, :], io["identF"], writes=["identF"])

    psA = e(nc.psum_tensor(pf + "psA", [128, 2048], F32))
    psB = e(nc.psum_tensor(pf + "psB", [128, 2048], F32))
    psAb = ["psA0", "psA1", "psA2", "psA3"]
    psBb = ["psB0", "psB1", "psB2", "psB3"]

    wstg = e(nc.sbuf_tensor(pf + "wstg", [128, 2048], F32))
    SGN = e(nc.sbuf_tensor(pf + "SGN", [128, 16 * 2048], BF16))
    ABSB = e(nc.sbuf_tensor(pf + "ABS", [128, 2048], BF16))
    ones_col = e(nc.sbuf_tensor(pf + "ones_col", [128, 1], BF16))
    p.dma(pf + "cones", ones_col[:, :], io["ones128"], writes=["ones_col"])
    ab_bc = e(nc.sbuf_tensor(pf + "ab_bc", [128, 2 * 2048], F32))
    xt = e(nc.sbuf_tensor(pf + "xt", [128, 2 * 2048], F32))
    xn = e(nc.sbuf_tensor(pf + "xn", [128, 2048], F32))
    lnm = e(nc.sbuf_tensor(pf + "lnm", [128, 2048], F32))
    colsb = e(nc.sbuf_tensor(pf + "cols", [128, 16], F32))
    XHT = e(nc.sbuf_tensor(pf + "XHT", [128, 16 * 2048], BF16))
    mix = xt[:, 0:2048]
    xr = xt[:, 2048:4096]
    outt = xn
    a_row = lnm

    for k in range(16):
        p.dma(pf + "wld", wstg[:, :],
              io["wT"][k * 128:(k + 1) * 128, :], writes=[pf + "wstg"])
        p.op("S", lambda e_, k=k: e_.activation(
            SGN[:, k * 2048:(k + 1) * 2048], wstg[:, :], AF.Sign),
            reads=[pf + "wstg"], writes=[pf + f"SGN{k}"])
        p.op("S", lambda e_, k=k: e_.activation(
            ABSB[:, :], wstg[:, :], AF.Abs),
            reads=[pf + "wstg"], writes=[pf + "ABS"])
        for c in range(4):
            p.op("T", lambda e_, k=k, c=c: e_.matmul(
                psA[0:1, c * 512:c * 512 + 512], ones_col[:, :],
                ABSB[:, c * 512:(c + 1) * 512],
                start=(k == 0), stop=(k == 15)),
                reads=[pf + "ABS", "ones_col"], writes=[psAb[c]],
                inc=False)
    p.op("S", lambda e_: e_.activation(a_row[0:1, 0:2048], psA[0:1, 0:2048], AF.Copy, scale=1.0 / 2048),
         reads=psAb, writes=[pf + "a_row", pf + "lnm"])
    p.dma(pf + "ast", io["scr_row"], a_row[0:1, 0:2048], reads=[pf + "a_row"], writes=[pf + "srow"])
    p.dma(pf + "ald", ab_bc[:, 0:2048],
          io["scr_row"].partition_broadcast(128), reads=[pf + "srow"], writes=[pf + "ab"])
    p.dma(pf + "bld", ab_bc[:, 2048:4096],
          io["bias"].unsqueeze(0).partition_broadcast(128), writes=[pf + "ab"])


    def sl(t_, s, w=2048):
        return t_[:, s * w:(s + 1) * w]

    for b in range(B_PER_CORE):
        xin = io["xin"][b]
        xout = io["xout"][b]
        for i in range(16):
            s = i % 2
            p.dma(pf + "xld" + str(s), sl(xt, s), xin[i * 128:(i + 1) * 128, :],
                  writes=[pf + f"xt{s}"])
            ln_normalize(p, sl(xt, s), xn[:, :], [pf + f"xt{s}"], [pf + "xn", pf + "outt"],
                         colsb[:, s * 8:(s + 1) * 8], pf + f"cols{s}")
            for k in range(16):
                bank = k % 4
                p.op("T", lambda e_, k=k, bank=bank: e_.transpose(
                    psB[:, bank * 512:bank * 512 + 128],
                    xn[:, k * 128:(k + 1) * 128], identF[:, :]),
                    reads=[pf + "xn", "identF"], writes=[psBb[bank]])
                p.op("S", lambda e_, k=k, bank=bank, i=i: e_.copy(
                    XHT[:, k * 2048 + i * 128:k * 2048 + i * 128 + 128],
                    psB[:, bank * 512:bank * 512 + 128]),
                    reads=[psBb[bank]], writes=[pf + f"XHT{k}"])
        for i in range(16):
            s = i % 2
            p.dma(pf + "xrl", xr[:, :], xin[i * 128:(i + 1) * 128, :],
                  writes=[pf + "xr0", pf + "xt1"])
            for c in range(4):
                pb = (4 * i + c) % 4
                for k in range(16):
                    p.op("T", lambda e_, i=i, c=c, k=k, pb=pb: e_.matmul(
                        psA[:, pb * 512:(pb + 1) * 512],
                        XHT[:, k * 2048 + i * 128:k * 2048 + i * 128 + 128],
                        SGN[:, k * 2048 + c * 512:k * 2048 + c * 512 + 512],
                        start=(k == 0), stop=(k == 15)),
                        reads=[pf + f"XHT{k}", pf + f"SGN{k}"], writes=[psAb[pb]],
                        inc=False)
                p.op("V", lambda e_, c=c, pb=pb: e_.tensor_tensor(
                    mix[:, c * 512:(c + 1) * 512], psA[:, pb * 512:(pb + 1) * 512],
                    ab_bc[:, c * 512:(c + 1) * 512], ALU.mult),
                    reads=[psAb[pb], pf + "ab"], writes=[pf + "mix", pf + "xt0"])
                p.op("V", lambda e_, c=c: e_.tensor_tensor(
                    mix[:, c * 512:(c + 1) * 512], mix[:, c * 512:(c + 1) * 512],
                    ab_bc[:, 2048 + c * 512:2048 + (c + 1) * 512], ALU.add),
                    reads=[pf + "mix", pf + "xt0", pf + "ab"], writes=[pf + "mix", pf + "xt0"])
            ln_normalize(p, mix[:, :], lnm[:, :], [pf + "mix", pf + "xt0"],
                         [pf + "lnm", pf + "a_row"],
                         colsb[:, s * 8:(s + 1) * 8], pf + f"cols{s}")
            p.op("V", lambda e_: e_.tensor_tensor(
                outt[:, :], xr[:, :], lnm[:, :], ALU.add),
                reads=[pf + "xr0", pf + "xt1", pf + "lnm"], writes=[pf + "outt", pf + "xn"])
            p.dma(pf + "xst", xout[i * 128:(i + 1) * 128, :], outt[:, :],
                  reads=[pf + "outt", pf + "xn"], writes=[pf + f"xo{b}_{i}"])
    return [pf + "xst"]


def mk_l1ffn_program():
    nc = bass.Bass("TRN2", target_bir_lowering=False)
    io = {}
    io["wT"] = nc.dram_tensor("wT", [2048, 2048], F32, kind="ExternalInput").ap()
    io["bias"] = nc.dram_tensor("bias", [2048], F32, kind="ExternalInput").ap()
    io["identF"] = nc.dram_tensor("identF", [128, 128], F32, kind="ExternalInput").ap()
    io["ones128"] = nc.dram_tensor("ones128", [128, 1], BF16, kind="ExternalInput").ap()
    io["scr_row"] = nc.dram_tensor("scr_row", [1, 2048], F32, kind="Internal").ap()
    xin = nc.dram_tensor("xin", [B_PER_CORE, 2048, 2048], F32, kind="ExternalInput").ap()
    xout = nc.dram_tensor("xout", [B_PER_CORE, 2048, 2048], F32, kind="ExternalOutput").ap()
    io["xin"] = [xin[b] for b in range(B_PER_CORE)]
    io["xout"] = [xout[b] for b in range(B_PER_CORE)]

    with ExitStack() as st:
        p = Prog(nc)
        streams = build_l1ffn(p, st, io)
        p.finish(streams)
        p.emit(st)
    return nc


def host_consts():
    return {
        "identF": np.eye(128, dtype=np.float32),
        "ones128": np.ones((128, 1), dtype=ml_dtypes.bfloat16),
    }




def sl2(t_, s, w=2048):
    return t_[:, s * w:(s + 1) * w]



def build_l0ffn(p, st, io, pf="f0"):
    nc = p.nc
    e = st.enter_context

    identF = e(nc.sbuf_tensor(pf + "identF", [128, 128], F32))
    H128 = e(nc.sbuf_tensor(pf + "H128", [128, 128], BF16))
    BD64 = e(nc.sbuf_tensor(pf + "BD64", [128, 128], BF16))
    p.dma(pf + "cid", identF[:, :], io["identF"], writes=["identF"])
    p.dma(pf + "chh", H128[:, :], io["H128"], writes=["H128"])
    p.dma(pf + "cbd", BD64[:, :], io["BD64"], writes=["BD64"])

    psA = e(nc.psum_tensor(pf + "psA", [128, 2048], F32))
    psB = e(nc.psum_tensor(pf + "psB", [128, 2048], F32))
    psAb = ["psA0", "psA1", "psA2", "psA3"]
    psBb = ["psB0", "psB1", "psB2", "psB3"]

    XFT = e(nc.sbuf_tensor(pf + "XFT", [128, 16 * 2048], BF16))
    FW1 = e(nc.sbuf_tensor(pf + "FW1", [128, 64 * 512], BF16))
    W1B = e(nc.sbuf_tensor(pf + "W1B", [128, 16 * 512], BF16))
    W3B = e(nc.sbuf_tensor(pf + "W3B", [128, 16 * 512], BF16))
    W2B = e(nc.sbuf_tensor(pf + "W2B", [128, 8 * 512], BF16))
    xt = e(nc.sbuf_tensor(pf + "xt", [128, 2 * 2048], F32))
    xn = e(nc.sbuf_tensor(pf + "xn", [128, 2048], F32))
    lnm = xn
    W2B2 = xn.bitcast(BF16)
    sg = e(nc.sbuf_tensor(pf + "sg", [128, 2 * 512], F32))
    hid = e(nc.sbuf_tensor(pf + "hid", [128, 2 * 512], BF16))
    mo = e(nc.sbuf_tensor(pf + "mo", [128, 2 * 512], F32))
    colsb = e(nc.sbuf_tensor(pf + "cols", [128, 16], F32))
    wst = FW1[:, 0:16384].bitcast(F32)
    wbo = FW1[:, 16384:24576]
    w2s = FW1[:, 24576:28672].bitcast(F32)
    w2c = FW1[:, 28672:30720]
    w2o = FW1[:, 30720:32768]
    FWALL = [pf + f"FW1_{g}" for g in range(64)]
    WAL = [pf + "wst", pf + "wbo", pf + "w2s", pf + "w2c", pf + "w2o"]
    mix = xt[:, 2048:4096]
    xr = xt[:, 0:2048]

    SC = 1.0 / float(np.sqrt(8192.0))

    for (wsrc, wdst, stw) in ((io["w1T"], io["s_w1tb"], "w1c"), (io["w3T"], io["s_w3tb"], "w3c")):
        for k in range(16):
            p.dma(pf + stw + "l", wst, wsrc[k * 128:(k + 1) * 128, :], writes=[pf + "wst"])
            p.op("V", lambda e_: e_.tensor_copy(wbo, wst),
                 reads=[pf + "wst"], writes=[pf + "wbo"])
            p.dma(pf + stw + "s", wdst[k * 128:(k + 1) * 128, :], wbo, reads=[pf + "wbo"],
                  writes=[pf + stw + "d"])
    w2hv = io["s_w2hgm"].rearrange("(g q) d -> q g d", q=128)
    for i in range(64):
        p.dma(pf + "w2l", w2s, io["w2pm"][i * 128:(i + 1) * 128, :], writes=[pf + "w2s"])
        p.op("V", lambda e_: e_.tensor_copy(w2c, w2s), reads=[pf + "w2s"], writes=[pf + "w2c"])
        for c in range(4):
            p.op("T", lambda e_, c=c: e_.matmul(
                psB[:, c * 512:(c + 1) * 512], BD64[:, :], w2c[:, c * 512:(c + 1) * 512],
                start=True, stop=True),
                reads=[pf + "w2c", "BD64"], writes=[psBb[c]], inc=False)
            p.op("S", lambda e_, c=c: e_.activation(
                w2o[:, c * 512:(c + 1) * 512], psB[:, c * 512:(c + 1) * 512], AF.Copy, scale=SC),
                reads=[psBb[c]], writes=[pf + "w2o"])
        p.dma(pf + "w2st", w2hv[2 * i:2 * i + 2, :, :], w2o,
              reads=[pf + "w2o"], writes=[pf + "w2d"])

    w1bv = io["s_w1tb"].rearrange("(k q) h -> q k h", q=128)
    w3bv = io["s_w3tb"].rearrange("(k q) h -> q k h", q=128)
    w2gv = io["s_w2hgm"].rearrange("(gg g8 q) d -> gg q g8 d", gg=8, g8=8, q=128)

    for b in range(B_PER_CORE):
        xin = io["xin"][b]
        xout = io["xout"][b]
        smix = io["s_mix"]
        for i in range(16):
            s = i % 2
            xw = [pf + f"xt{s}"] + ([pf + "xr"] if i == 0 else [pf + "mix"] if i == 1 else [])
            p.dma(pf + "xld" + str(s), sl2(xt, s), xin[i * 128:(i + 1) * 128, :],
                  writes=xw)
            ln_normalize(p, sl2(xt, s), xn[:, :], [pf + f"xt{s}"],
                         [pf + "xn", pf + "W2B2", pf + "lnm"],
                         colsb[:, s * 8:(s + 1) * 8], pf + f"cols{s}")
            for k in range(16):
                bank = k % 4
                p.op("T", lambda e_, k=k, bank=bank: e_.transpose(
                    psB[:, bank * 512:bank * 512 + 128],
                    xn[:, k * 128:(k + 1) * 128], identF[:, :]),
                    reads=[pf + "xn", "identF"], writes=[psBb[bank]], inc=False)
                p.op("S", lambda e_, k=k, bank=bank, i=i: e_.copy(
                    XFT[:, k * 2048 + i * 128:k * 2048 + i * 128 + 128],
                    psB[:, bank * 512:bank * 512 + 128]),
                    reads=[psBb[bank]], writes=[pf + f"XFT{k}"])
        for c in range(4):
            pend_st1 = None
            for hb in range(16):
                p.dma(pf + "w1bl", W1B[:, :], w1bv[:, :, hb * 512:(hb + 1) * 512],
                      reads=[pf + "w1cd"], writes=[pf + "W1B"])
                p.dma(pf + "w3bl", W3B[:, :], w3bv[:, :, hb * 512:(hb + 1) * 512],
                      reads=[pf + "w3cd"], writes=[pf + "W3B"])
                for m in range(4):
                    sm = m % 2
                    pg, pl = (2 * m) % 4, (2 * m + 1) % 4
                    for k in range(16):
                        p.op("T", lambda e_, m=m, k=k, c=c, pg=pg: e_.matmul(
                            psA[:, pg * 512:(pg + 1) * 512],
                            W1B[:, k * 512 + m * 128:k * 512 + m * 128 + 128],
                            XFT[:, k * 2048 + c * 512:k * 2048 + (c + 1) * 512],
                            start=(k == 0), stop=(k == 15)),
                            reads=[pf + "W1B", pf + f"XFT{k}"], writes=[psAb[pg]], inc=False)
                        p.op("T", lambda e_, m=m, k=k, c=c, pl=pl: e_.matmul(
                            psA[:, pl * 512:(pl + 1) * 512],
                            W3B[:, k * 512 + m * 128:k * 512 + m * 128 + 128],
                            XFT[:, k * 2048 + c * 512:k * 2048 + (c + 1) * 512],
                            start=(k == 0), stop=(k == 15)),
                            reads=[pf + "W3B", pf + f"XFT{k}"], writes=[psAb[pl]], inc=False)
                    if pend_st1 is not None:
                        (pg_, pb_, g_, al_, sm_) = pend_st1
                        p.op("T", lambda e_, pb=pb_, sm=sm_: e_.matmul(
                            psB[:, pb * 512:(pb + 1) * 512], H128[:, :],
                            hid[:, sm * 512:(sm + 1) * 512],
                            start=True, stop=True),
                            reads=[pf + f"hid{sm_}", "H128"], writes=[psBb[pb_]], inc=False)
                        p.op("S", lambda e_, g=g_, pb=pb_: e_.copy(
                            FW1[:, g * 512:(g + 1) * 512], psB[:, pb * 512:(pb + 1) * 512]),
                            reads=[psBb[pb_]], writes=[pf + f"FW1_{g_}", al_])
                    p.op("S", lambda e_, pg=pg, sm=sm: e_.activation(
                        sg[:, sm * 512:(sm + 1) * 512], psA[:, pg * 512:(pg + 1) * 512], AF.Silu),
                        reads=[psAb[pg]], writes=[pf + f"sg{sm}"])
                    p.op("V", lambda e_, pl=pl, sm=sm: e_.tensor_tensor(
                        hid[:, sm * 512:(sm + 1) * 512], sg[:, sm * 512:(sm + 1) * 512],
                        psA[:, pl * 512:(pl + 1) * 512], ALU.mult),
                        reads=[pf + f"sg{sm}", psAb[pl]], writes=[pf + f"hid{sm}"])
                    g = hb * 4 + m
                    al = (pf + "wst" if g < 32 else pf + "wbo" if g < 48 else
                          pf + "w2s" if g < 56 else pf + "w2c" if g < 60 else pf + "w2o")
                    pend_st1 = (pg, m % 4, g, al, sm)
            if pend_st1 is not None:
                (pg_, pb_, g_, al_, sm_) = pend_st1
                p.op("T", lambda e_, pb=pb_, sm=sm_: e_.matmul(
                    psB[:, pb * 512:(pb + 1) * 512], H128[:, :],
                    hid[:, sm * 512:(sm + 1) * 512], start=True, stop=True),
                    reads=[pf + f"hid{sm_}", "H128"], writes=[psBb[pb_]], inc=False)
                p.op("S", lambda e_, g=g_, pb=pb_: e_.copy(
                    FW1[:, g * 512:(g + 1) * 512], psB[:, pb * 512:(pb + 1) * 512]),
                    reads=[psBb[pb_]], writes=[pf + f"FW1_{g_}", al_])
                pend_st1 = None
            for dc in range(4):
                for gg in range(8):
                    ws = gg % 2
                    wap = W2B if ws == 0 else W2B2
                    wnm = [pf + "W2B"] if ws == 0 else [pf + "W2B2", pf + "xn", pf + "lnm"]
                    p.dma(pf + "w2bl" + str(ws), wap[:, :],
                          w2gv[gg][:, :, dc * 512:(dc + 1) * 512],
                          reads=[pf + "w2d"], writes=wnm)
                    for g8 in range(8):
                        g = gg * 8 + g8
                        for i4 in range(4):
                            p.op("T", lambda e_, g=g, g8=g8, i4=i4, c=c, wap=wap: e_.matmul(
                                psA[:, i4 * 512:(i4 + 1) * 512],
                                FW1[:, g * 512 + i4 * 128:g * 512 + i4 * 128 + 128],
                                wap[:, g8 * 512:(g8 + 1) * 512],
                                start=(g == 0), stop=(g == 63)),
                                reads=wnm + [pf + f"FW1_{g}"], writes=[psAb[i4]], inc=False)
                for i4 in range(4):
                    ms = i4 % 2
                    it = c * 4 + i4
                    p.op("S", lambda e_, i4=i4, ms=ms: e_.copy(
                        mo[:, ms * 512:(ms + 1) * 512], psA[:, i4 * 512:(i4 + 1) * 512]),
                        reads=[psAb[i4]], writes=[pf + f"mo{ms}"])
                    p.dma(pf + "most" + str(ms), smix[it * 128:(it + 1) * 128, dc * 512:(dc + 1) * 512],
                          mo[:, ms * 512:(ms + 1) * 512], reads=[pf + f"mo{ms}"],
                          writes=[pf + f"smx{it}_{dc}"])
            for i4 in range(4):
                it = c * 4 + i4
                p.dma(pf + "mxl", mix, smix[it * 128:(it + 1) * 128, :],
                      reads=[pf + f"smx{it}_{d_}" for d_ in range(4)],
                      writes=[pf + "mix", pf + "xt1"])
                p.dma(pf + "xrl", xr, xin[it * 128:(it + 1) * 128, :],
                      writes=[pf + "xr", pf + "xt0"])
                ln_normalize(p, mix, lnm[:, :], [pf + "mix", pf + "xt1"],
                             [pf + "lnm", pf + "xn", pf + "W2B2"], colsb[:, 0:8], pf + "cols0")
                p.op("V", lambda e_: e_.tensor_tensor(mix, xr, lnm[:, :], ALU.add),
                     reads=[pf + "xr", pf + "xt0", pf + "lnm", pf + "xn"],
                     writes=[pf + "mix", pf + "xt1"])
                p.dma(pf + "xst", xout[it * 128:(it + 1) * 128, :], mix,
                      reads=[pf + "mix", pf + "xt1"], writes=[pf + f"xo{b}_{it}"])
    return [pf + "xst", pf + "most0", pf + "most1"]


def mk_l0ffn_program():
    nc = bass.Bass("TRN2", target_bir_lowering=False)
    io = {}
    io["w1T"] = nc.dram_tensor("w1T", [2048, 8192], F32, kind="ExternalInput").ap()
    io["w3T"] = nc.dram_tensor("w3T", [2048, 8192], F32, kind="ExternalInput").ap()
    io["w2pm"] = nc.dram_tensor("w2pm", [8192, 2048], F32, kind="ExternalInput").ap()
    io["identF"] = nc.dram_tensor("identF", [128, 128], F32, kind="ExternalInput").ap()
    io["H128"] = nc.dram_tensor("H128", [128, 128], BF16, kind="ExternalInput").ap()
    io["BD64"] = nc.dram_tensor("BD64", [128, 128], BF16, kind="ExternalInput").ap()
    io["s_w1tb"] = nc.dram_tensor("s_w1tb", [2048, 8192], BF16, kind="Internal").ap()
    io["s_w3tb"] = nc.dram_tensor("s_w3tb", [2048, 8192], BF16, kind="Internal").ap()
    io["s_w2hgm"] = nc.dram_tensor("s_w2hgm", [8192, 2048], BF16, kind="Internal").ap()
    io["s_mix"] = nc.dram_tensor("s_mix", [2048, 2048], F32, kind="Internal").ap()
    xin = nc.dram_tensor("xin", [B_PER_CORE, 2048, 2048], F32, kind="ExternalInput").ap()
    xout = nc.dram_tensor("xout", [B_PER_CORE, 2048, 2048], F32, kind="ExternalOutput").ap()
    io["xin"] = [xin[b] for b in range(B_PER_CORE)]
    io["xout"] = [xout[b] for b in range(B_PER_CORE)]
    with ExitStack() as st:
        p = Prog(nc)
        streams = build_l0ffn(p, st, io)
        p.finish(streams)
        p.emit(st)
    return nc


def _hadamard(n):
    h = np.array([[1.0]])
    while h.shape[0] < n:
        h = np.block([[h, h], [h, -h]])
    return h


def host_consts_l0ffn():
    H128h = _hadamard(128).astype(ml_dtypes.bfloat16)
    H64h = _hadamard(64)
    BD = np.zeros((128, 128), dtype=np.float64)
    BD[0:64, 0:64] = H64h
    BD[64:128, 64:128] = H64h
    return {
        "identF": np.eye(128, dtype=np.float32),
        "H128": H128h,
        "BD64": BD.astype(ml_dtypes.bfloat16),
    }



def build_l1attn(p, st, io, pf="a1"):
    nc = p.nc
    e = st.enter_context
    SQT = 1.0 / float(np.sqrt(2048.0))

    identF = e(nc.sbuf_tensor(pf + "identF", [128, 128], F32))
    masks = e(nc.sbuf_tensor(pf + "masks", [128, 4 * 512], BF16))
    onesc = e(nc.sbuf_tensor(pf + "onesc", [128, 1], BF16))
    onesr = e(nc.sbuf_tensor(pf + "onesr", [1, 512], BF16))
    p.dma(pf + "cid", identF[:, :], io["identF"], writes=["identF"])
    p.dma(pf + "cmk", masks[:, :], io["masks"], writes=["masks"])
    p.dma(pf + "co1", onesc[:, :], io["ones128"], writes=["onesc"])
    p.dma(pf + "co2", onesr[:, :], io["ones1x512"], writes=["onesr"])

    psA = e(nc.psum_tensor(pf + "psA", [128, 2048], F32))
    psB = e(nc.psum_tensor(pf + "psB", [128, 2048], F32))
    psAb = ["psA0", "psA1", "psA2", "psA3"]
    psBb = ["psB0", "psB1", "psB2", "psB3"]

    XHT = e(nc.sbuf_tensor(pf + "XHT", [128, 16 * 2048], BF16))
    WQB = e(nc.sbuf_tensor(pf + "WQB", [128, 16 * 256], BF16))
    WKB = e(nc.sbuf_tensor(pf + "WKB", [128, 16 * 256], BF16))
    WVB = e(nc.sbuf_tensor(pf + "WVB", [128, 16 * 256], BF16))
    QTG = e(nc.sbuf_tensor(pf + "QTG", [128, 2 * 2048], BF16))
    KTG = e(nc.sbuf_tensor(pf + "KTG", [128, 2 * 2048], BF16))
    VG = e(nc.sbuf_tensor(pf + "VG", [128, 16 * 256], BF16))
    EXP2 = e(nc.sbuf_tensor(pf + "EXP", [128, 2 * 16 * 512], BF16))
    oth = e(nc.sbuf_tensor(pf + "oth", [128, 2048], BF16))
    zr = e(nc.sbuf_tensor(pf + "zr", [128, 512], BF16))
    zrowf = e(nc.sbuf_tensor(pf + "zrowf", [1, 512], F32))
    zrow = e(nc.sbuf_tensor(pf + "zrow", [1, 512], BF16))
    sufr = e(nc.sbuf_tensor(pf + "sufr", [1, 6 * 128], BF16))
    xt = e(nc.sbuf_tensor(pf + "xt", [128, 2 * 2048], F32))
    colsb = e(nc.sbuf_tensor(pf + "cols", [128, 16], F32))
    xn = WVB[:, 0:4096].bitcast(F32)
    mo = EXP2[:, 0:1024].bitcast(F32)
    lnm = xn
    mix = xt[:, 2048:4096]
    xr = xt[:, 0:2048]
    wstage = XHT[:, 0:4096].bitcast(F32)
    wsbf = XHT[:, 4096:6144]
    LHS = KTG[:, 0:2048]
    WOB = XHT[:, 0:8192]
    XAL = [pf + "XHT0", pf + "XHT1", pf + "XHT2", pf + "XHT3",
           pf + "wstage", pf + "wsbf"]

    for (wsrc, wdst, stw) in ((io["Wq"], io["s_wqb"], "q"), (io["Wk"], io["s_wkb"], "k"),
                              (io["Wv"], io["s_wvb"], "v"), (io["Wo"], io["s_wob"], "o")):
        for k in range(16):
            p.dma(pf + "wc" + stw, wstage, wsrc[k * 128:(k + 1) * 128, :],
                  writes=[pf + "wstage"])
            p.op("V", lambda e_: e_.tensor_copy(wsbf, wstage),
                 reads=[pf + "wstage"], writes=[pf + "wsbf"])
            p.dma(pf + "ws" + stw, wdst[k * 128:(k + 1) * 128, :], wsbf,
                  reads=[pf + "wsbf"], writes=[pf + "wcd" + stw])

    wqv = io["s_wqb"].rearrange("(k q) o -> q k o", q=128)
    wkv = io["s_wkb"].rearrange("(k q) o -> q k o", q=128)
    wvv = io["s_wvb"].rearrange("(k q) o -> q k o", q=128)
    wov = io["s_wob"].rearrange("(h q) o -> q h o", q=128)
    soutv = io["s_outT"]

    for b in range(B_PER_CORE):
        xin = io["xin"][b]
        xout = io["xout"][b]
        smix = io["s_mix"]
        for i in range(16):
            s = i % 2
            xw = [pf + f"xt{s}"] + ([pf + "xr"] if i == 0 else [pf + "mix"] if i == 1 else [])
            p.dma(pf + "xld" + str(s), sl2(xt, s), xin[i * 128:(i + 1) * 128, :], writes=xw)
            ln_normalize(p, sl2(xt, s), xn[:, :], [pf + f"xt{s}"], [pf + "xn", pf + "WVB"],
                         colsb[:, s * 8:(s + 1) * 8], pf + f"cols{s}")
            for k in range(16):
                bank = k % 4
                xwr = [pf + f"XHT{k}"] + ([pf + "wstage", pf + "wsbf"] if k < 3 and i == 0 else [])
                p.op("T", lambda e_, k=k, bank=bank: e_.transpose(
                    psB[:, bank * 512:bank * 512 + 128],
                    xn[:, k * 128:(k + 1) * 128], identF[:, :]),
                    reads=[pf + "xn", "identF"], writes=[psBb[bank]], inc=False)
                p.op("S", lambda e_, k=k, bank=bank, i=i: e_.copy(
                    XHT[:, k * 2048 + i * 128:k * 2048 + i * 128 + 128],
                    psB[:, bank * 512:bank * 512 + 128]),
                    reads=[psBb[bank]], writes=xwr)
        for g2 in range(8):
            co = g2 * 256
            p.dma(pf + "wqbl", WQB[:, :], wqv[:, :, co:co + 256],
                  reads=[pf + "wcdq"], writes=[pf + "WQB"])
            p.dma(pf + "wkbl", WKB[:, :], wkv[:, :, co:co + 256],
                  reads=[pf + "wcdk"], writes=[pf + "WKB"])
            p.dma(pf + "wvbl", WVB[:, :], wvv[:, :, co:co + 256],
                  reads=[pf + "wcdv"], writes=[pf + "WVB", pf + "xn", pf + "lnm"])
            for l in range(2):
                for tc in range(4):
                    pb = (4 * l + tc) % 4
                    for k in range(16):
                        p.op("T", lambda e_, l=l, tc=tc, k=k, pb=pb: e_.matmul(
                            psA[:, pb * 512:(pb + 1) * 512],
                            WQB[:, k * 256 + l * 128:k * 256 + l * 128 + 128],
                            XHT[:, k * 2048 + tc * 512:k * 2048 + (tc + 1) * 512],
                            start=(k == 0), stop=(k == 15)),
                            reads=[pf + "WQB", pf + f"XHT{k}"], writes=[psAb[pb]], inc=False)
                    p.op("S", lambda e_, l=l, tc=tc, pb=pb: e_.copy(
                        QTG[:, l * 2048 + tc * 512:l * 2048 + (tc + 1) * 512],
                        psA[:, pb * 512:(pb + 1) * 512]),
                        reads=[psAb[pb]], writes=[pf + f"QTG{l}"])
                    for k in range(16):
                        p.op("T", lambda e_, l=l, tc=tc, k=k, pb=pb: e_.matmul(
                            psB[:, pb * 512:(pb + 1) * 512],
                            WKB[:, k * 256 + l * 128:k * 256 + l * 128 + 128],
                            XHT[:, k * 2048 + tc * 512:k * 2048 + (tc + 1) * 512],
                            start=(k == 0), stop=(k == 15)),
                            reads=[pf + "WKB", pf + f"XHT{k}"], writes=[psBb[pb]], inc=False)
                    p.op("S", lambda e_, l=l, tc=tc, pb=pb: e_.copy(
                        KTG[:, l * 2048 + tc * 512:l * 2048 + (tc + 1) * 512],
                        psB[:, pb * 512:(pb + 1) * 512]),
                        reads=[psBb[pb]], writes=[pf + f"KTG{l}", pf + "LHS"])
            for i in range(16):
                pb = i % 4
                for k in range(16):
                    p.op("T", lambda e_, i=i, k=k, pb=pb: e_.matmul(
                        psA[:, pb * 512:pb * 512 + 256],
                        XHT[:, k * 2048 + i * 128:k * 2048 + i * 128 + 128],
                        WVB[:, k * 256:(k + 1) * 256],
                        start=(k == 0), stop=(k == 15)),
                        reads=[pf + "WVB", pf + f"XHT{k}"], writes=[psAb[pb]], inc=False)
                p.op("S", lambda e_, i=i, pb=pb: e_.copy(
                    VG[:, i * 256:(i + 1) * 256], psA[:, pb * 512:pb * 512 + 256]),
                    reads=[psAb[pb]], writes=[pf + f"VG{i}"])
            for l in range(2):
                h = g2 * 2 + l
                EXP = EXP2[:, l * 8192:(l + 1) * 8192]
                EB = pf + f"E{l}_"
                qT = QTG[:, l * 2048:(l + 1) * 2048]
                kT = KTG[:, l * 2048:(l + 1) * 2048]
                for cc in range(3):
                    sbs = list(range(4 * cc + 4, 16))
                    for j, sb in enumerate(sbs):
                        p.op("T", lambda e_, sb=sb, cc=cc, j=j, n=len(sbs), l=l: e_.matmul(
                            psB[0:1, (1 + cc) * 512:(1 + cc) * 512 + 128], onesc[:, :],
                            VG[:, sb * 256 + l * 128:sb * 256 + l * 128 + 128],
                            start=(j == 0), stop=(j == n - 1)),
                            reads=[pf + f"VG{sb}", "onesc"], writes=[psBb[1 + cc]], inc=False)
                    p.op("S", lambda e_, cc=cc, l=l: e_.copy(
                        sufr[:, (l * 3 + cc) * 128:(l * 3 + cc + 1) * 128],
                        psB[0:1, (1 + cc) * 512:(1 + cc) * 512 + 128]),
                        reads=[psBb[1 + cc]], writes=[pf + "sufr"])
                for c in range(4):
                    smax = 4 * c + 3
                    osl = c % 2
                    ssl = 2 + (c % 2)
                    for sb in range(smax + 1):
                        p.op("T", lambda e_, sb=sb, c=c, ssl=ssl, kT=kT, qT=qT: e_.matmul(
                            psA[:, ssl * 512:(ssl + 1) * 512],
                            kT[:, sb * 128:(sb + 1) * 128],
                            qT[:, c * 512:(c + 1) * 512], start=True, stop=True),
                            reads=[pf + f"KTG{l}", pf + f"QTG{l}"], writes=[psAb[ssl]], inc=False)
                        off = sb - 4 * c
                        if off >= 0:
                            p.op("V", lambda e_, off=off, ssl=ssl: e_.tensor_tensor(
                                psA[:, ssl * 512:(ssl + 1) * 512],
                                psA[:, ssl * 512:(ssl + 1) * 512],
                                masks[:, off * 512:(off + 1) * 512], ALU.mult),
                                reads=[psAb[ssl], "masks"], writes=[psAb[ssl]])
                        p.op("S", lambda e_, sb=sb, ssl=ssl, EXP=EXP: e_.activation(
                            EXP[:, sb * 512:(sb + 1) * 512],
                            psA[:, ssl * 512:(ssl + 1) * 512], AF.Exp, scale=SQT),
                            reads=[psAb[ssl]],
                            writes=[EB + f"EXP{sb}"] + ([pf + "mo0"] if sb < 2 and l == 0 else []))
                    for sb in range(smax + 1):
                        p.op("T", lambda e_, sb=sb, smax=smax, EXP=EXP: e_.matmul(
                            psB[0:1, 0:512], onesc[:, :], EXP[:, sb * 512:(sb + 1) * 512],
                            start=(sb == 0), stop=(sb == smax)),
                            reads=[EB + f"EXP{sb}", "onesc"], writes=[psBb[0]], inc=False)
                    cnt = float(2048 - 512 * (c + 1))
                    p.op("V", lambda e_, cnt=cnt: e_.tensor_scalar_add(
                        zrowf[0:1, :], psB[0:1, 0:512], cnt),
                        reads=[psBb[0]], writes=[pf + "zrowf"])
                    p.op("V", lambda e_: e_.reciprocal(zrowf[0:1, :], zrowf[0:1, :]),
                         reads=[pf + "zrowf"], writes=[pf + "zrowf"])
                    p.op("V", lambda e_: e_.tensor_copy(zrow[0:1, :], zrowf[0:1, :]),
                         reads=[pf + "zrowf"], writes=[pf + "zrow"])
                    for sb in range(smax + 1):
                        p.op("T", lambda e_, sb=sb, c=c, osl=osl, l=l, smax=smax, EXP=EXP: e_.matmul(
                            psA[:, osl * 512:(osl + 1) * 512],
                            VG[:, sb * 256 + l * 128:sb * 256 + l * 128 + 128],
                            EXP[:, sb * 512:(sb + 1) * 512],
                            start=(sb == 0), stop=(sb == smax and c == 3)),
                            reads=[pf + f"VG{sb}", EB + f"EXP{sb}"], writes=[psAb[osl]], inc=False)
                    if c < 3:
                        p.op("T", lambda e_, c=c, osl=osl, l=l: e_.matmul(
                            psA[:, osl * 512:(osl + 1) * 512],
                            sufr[0:1, (l * 3 + c) * 128:(l * 3 + c + 1) * 128],
                            onesr[:, :], start=False, stop=True),
                            reads=[pf + "sufr", "onesr"], writes=[psAb[osl]], inc=False)
                    p.op("T", lambda e_: e_.matmul(
                        psB[:, 1 * 512:2 * 512], onesr[0:1, 0:128], zrow[0:1, :],
                        start=True, stop=True),
                        reads=[pf + "zrow", "onesr"], writes=[psBb[1]], inc=False)
                    p.op("S", lambda e_: e_.copy(zr[:, :], psB[:, 1 * 512:2 * 512]),
                         reads=[psBb[1]], writes=[pf + "zr"])
                    p.op("V", lambda e_, c=c, osl=osl: e_.tensor_tensor(
                        oth[:, c * 512:(c + 1) * 512], psA[:, osl * 512:(osl + 1) * 512],
                        zr[:, :], ALU.mult),
                        reads=[psAb[osl], pf + "zr"], writes=[pf + "oth"])
                p.dma(pf + "ost", soutv[h], oth[:, :], reads=[pf + "oth"],
                      writes=[pf + f"soth{h}"])
        for dc in range(4):
            p.dma(pf + "wobl", WOB[:, :], wov[:, :, dc * 512:(dc + 1) * 512],
                  reads=[pf + "wcdo"] + [pf + f"soth{h}" for h in range(16)],
                  writes=[pf + "WOB"] + XAL)
            for i in range(16):
                for q4 in range(4):
                    p.dma(pf + "lhl" + str(q4),
                          LHS[:, q4 * 512:(q4 + 1) * 512],
                          soutv[:, :, i * 128:(i + 1) * 128].transpose([1, 0, 2])[:, q4 * 4:(q4 + 1) * 4, :],
                          reads=[pf + f"soth{q4 * 4 + j}" for j in range(4)],
                          writes=[pf + f"LHSq{q4}", pf + "KTG0", pf + "KTG1", pf + "LHS"])
                pb = i % 4
                for h in range(16):
                    p.op("T", lambda e_, h=h, pb=pb: e_.matmul(
                        psA[:, pb * 512:(pb + 1) * 512], LHS[:, h * 128:(h + 1) * 128],
                        WOB[:, h * 512:(h + 1) * 512], start=(h == 0), stop=(h == 15)),
                        reads=[pf + f"LHSq{h // 4}", pf + "WOB"], writes=[psAb[pb]], inc=False)
                p.op("S", lambda e_, pb=pb: e_.copy(
                    mo[:, :], psA[:, pb * 512:(pb + 1) * 512]),
                    reads=[psAb[pb]], writes=[pf + "mo0", pf + "E0_EXP0", pf + "E0_EXP1"])
                p.dma(pf + "most0", smix[i * 128:(i + 1) * 128, dc * 512:(dc + 1) * 512],
                      mo[:, :], reads=[pf + "mo0"],
                      writes=[pf + f"smx{i}_{dc}"])
        for i in range(16):
            p.dma(pf + "mxl", mix, smix[i * 128:(i + 1) * 128, :],
                  reads=[pf + f"smx{i}_{d_}" for d_ in range(4)],
                  writes=[pf + "mix", pf + "xt1"])
            p.dma(pf + "xrl", xr, xin[i * 128:(i + 1) * 128, :],
                  writes=[pf + "xr", pf + "xt0"])
            ln_normalize(p, mix, lnm[:, :], [pf + "mix", pf + "xt1"],
                         [pf + "lnm", pf + "xn", pf + "WVB"], colsb[:, 0:8], pf + "cols0")
            p.op("V", lambda e_: e_.tensor_tensor(mix, xr, lnm[:, :], ALU.add),
                 reads=[pf + "xr", pf + "xt0", pf + "lnm", pf + "xn"],
                 writes=[pf + "mix", pf + "xt1"])
            p.dma(pf + "xst", xout[i * 128:(i + 1) * 128, :], mix,
                  reads=[pf + "mix", pf + "xt1"], writes=[pf + f"xo{b}_{i}"])
    return [pf + "xst", pf + "most0", pf + "ost"]


def mk_l1attn_program():
    nc = bass.Bass("TRN2", target_bir_lowering=False)
    io = {}
    for nm in ("Wq", "Wk", "Wv", "Wo"):
        io[nm] = nc.dram_tensor(nm, [2048, 2048], F32, kind="ExternalInput").ap()
    io["identF"] = nc.dram_tensor("identF", [128, 128], F32, kind="ExternalInput").ap()
    io["masks"] = nc.dram_tensor("masks", [128, 4 * 512], BF16, kind="ExternalInput").ap()
    io["ones128"] = nc.dram_tensor("ones128", [128, 1], BF16, kind="ExternalInput").ap()
    io["ones1x512"] = nc.dram_tensor("ones1x512", [1, 512], BF16, kind="ExternalInput").ap()
    for nm in ("s_wqb", "s_wkb", "s_wvb", "s_wob"):
        io[nm] = nc.dram_tensor(nm, [2048, 2048], BF16, kind="Internal").ap()
    io["s_outT"] = nc.dram_tensor("s_outT", [16, 128, 2048], BF16, kind="Internal").ap()
    io["s_mix"] = nc.dram_tensor("s_mix", [2048, 2048], F32, kind="Internal").ap()
    xin = nc.dram_tensor("xin", [B_PER_CORE, 2048, 2048], F32, kind="ExternalInput").ap()
    xout = nc.dram_tensor("xout", [B_PER_CORE, 2048, 2048], F32, kind="ExternalOutput").ap()
    io["xin"] = [xin[b] for b in range(B_PER_CORE)]
    io["xout"] = [xout[b] for b in range(B_PER_CORE)]
    with ExitStack() as st:
        p = Prog(nc)
        streams = build_l1attn(p, st, io)
        p.finish(streams)
        p.emit(st)
    return nc


def host_consts_l1attn():
    m = np.zeros((128, 4 * 512), dtype=np.float32)
    for off in range(4):
        s = np.arange(128)[:, None]
        t = np.arange(512)[None, :]
        m[:, off * 512:(off + 1) * 512] = (s + 128 * off <= t).astype(np.float32)
    return {
        "identF": np.eye(128, dtype=np.float32),
        "masks": m.astype(ml_dtypes.bfloat16),
        "ones128": np.ones((128, 1), dtype=ml_dtypes.bfloat16),
        "ones1x512": np.ones((1, 512), dtype=ml_dtypes.bfloat16),
    }



def build_l0attn(p, st, io, pf="a0"):
    nc = p.nc
    e = st.enter_context

    identF = e(nc.sbuf_tensor(pf + "identF", [128, 128], F32))
    sel64 = e(nc.sbuf_tensor(pf + "sel64", [64, 128], BF16))
    EJT = e(nc.sbuf_tensor(pf + "EJT", [128, 64 * 128], BF16))
    YST = e(nc.sbuf_tensor(pf + "YST", [128, 256], BF16))
    p.dma(pf + "cid", identF[:, :], io["identF"], writes=["identF"])
    p.dma(pf + "cs64", sel64[:, :], io["sel64"], writes=["sel64"])
    p.dma(pf + "cej", EJT[:, :], io["EJT"], writes=["EJT"])
    p.dma(pf + "cys", YST[:, :], io["YST"], writes=["YST"])

    psA = e(nc.psum_tensor(pf + "psA", [128, 2048], F32))
    psB = e(nc.psum_tensor(pf + "psB", [128, 2048], F32))
    psAb = ["psA0", "psA1", "psA2", "psA3"]
    psBb = ["psB0", "psB1", "psB2", "psB3"]

    XT = e(nc.sbuf_tensor(pf + "XT", [128, 16 * 2048], BF16))
    LT = e(nc.sbuf_tensor(pf + "LT", [128, 16 * 2048], BF16))
    dA = e(nc.sbuf_tensor(pf + "dA", [128, 2048], F32))
    ub = e(nc.sbuf_tensor(pf + "ub", [128, 2048], BF16))
    hh = e(nc.sbuf_tensor(pf + "hh", [128, 2048], BF16))
    DTK = e(nc.sbuf_tensor(pf + "DTK", [128, 2048], BF16))
    BREP = e(nc.sbuf_tensor(pf + "BREP", [128, 2048], BF16))
    CREP = e(nc.sbuf_tensor(pf + "CREP", [128, 512], BF16))
    ALC = e(nc.sbuf_tensor(pf + "ALC", [128, 1024], F32))
    WBB = e(nc.sbuf_tensor(pf + "WBB", [128, 2 * 2048], BF16))
    DCOL = e(nc.sbuf_tensor(pf + "DCOL", [128, 32], F32))
    xt = e(nc.sbuf_tensor(pf + "xt", [128, 2 * 2048], F32))
    DPJ = e(nc.sbuf_tensor(pf + "DPJ", [128, 2048], BF16))
    DRT = e(nc.sbuf_tensor(pf + "DRT", [128, 2048], BF16))
    CT = e(nc.sbuf_tensor(pf + "CT", [64, 512], BF16))
    colsb = e(nc.sbuf_tensor(pf + "cols", [128, 16], F32))
    BT = DTK[0:64, :]
    XPJ = hh
    xn = dA[:, 0:2048]
    stg = dA[:, 0:2048]
    OPB = dA.bitcast(BF16)
    mo = ub.bitcast(F32)
    mix = xt[:, 2048:4096]
    xr = xt[:, 0:2048]
    RDA = [pf + "dA", pf + "xn", pf + "OPB"]
    RXT0 = [pf + "xt0", pf + "xr"]
    RXT1 = [pf + "xt1", pf + "mix"]
    RUB = [pf + "ub", pf + "mo"]

    for b in range(B_PER_CORE):
        xin = io["xin"][b]
        xout = io["xout"][b]
        smix = io["s_mix"]
        p.dma(pf + "cwb", WBB[:, 0:2048],
              io["norm_w"].unsqueeze(0).partition_broadcast(128), engine="G",
              writes=[pf + "WBB", pf + "WBBu"])
        p.dma(pf + "cbb", WBB[:, 2048:4096],
              io["norm_b"].unsqueeze(0).partition_broadcast(128), engine="G",
              writes=[pf + "WBB", pf + "WBBh"])
        if b == 0:
            p.dma(pf + "cdc", DCOL[:, 0:16], io["dcols"], writes=[pf + "DCOL"])
            p.dma(pf + "cbc", DCOL[:, 16:32], io["bcols"], writes=[pf + "DCOL"])
            p.dma(pf + "cal", ALC[:, :], io["alogcols"], writes=[pf + "ALC"])
            p.op("S", lambda e_: e_.activation(ALC[:, :], ALC[:, :], AF.Exp),
                 reads=[pf + "ALC"], writes=[pf + "ALC"])
            p.op("V", lambda e_: e_.tensor_scalar_mul(ALC[:, :], ALC[:, :], -1.0),
                 reads=[pf + "ALC"], writes=[pf + "ALC"])
            for k in range(16):
                p.dma(pf + "opl", stg[:, :], io["oprojT"][k * 128:(k + 1) * 128, :],
                      reads=RDA, writes=RDA)
                p.op("V", lambda e_: e_.tensor_copy(ub[:, :], stg[:, :]),
                     reads=RDA, writes=RUB)
                p.dma(pf + "ops", io["s_oprojb"][k * 128:(k + 1) * 128, :], ub[:, :],
                      reads=RUB, writes=[pf + "opd"])
        for i in range(16):
            s = 0
            p.dma(pf + "xld0", sl2(xt, 0), xin[i * 128:(i + 1) * 128, :], writes=RXT0)
            ln_normalize(p, sl2(xt, 0), xn[:, :], RXT0, RDA,
                         colsb[:, 0:8], pf + "cols0")
            for k in range(16):
                bank = k % 4
                p.op("T", lambda e_, k=k, bank=bank: e_.transpose(
                    psB[:, bank * 512:bank * 512 + 128],
                    xn[:, k * 128:(k + 1) * 128], identF[:, :]),
                    reads=RDA + ["identF"], writes=[psBb[bank]], inc=False)
                p.op("S", lambda e_, k=k, bank=bank, i=i: e_.activation(
                    LT[:, k * 2048 + i * 128:k * 2048 + i * 128 + 128],
                    psB[:, bank * 512:bank * 512 + 128], AF.Identity,
                    scale=DCOL[:, k:k + 1]),
                    reads=[psBb[bank], pf + "DCOL"], writes=[pf + f"LT{k}"])
            ln_normalize(p, xn[:, :], xt[:, 2048:4096], RDA, RXT1,
                         colsb[:, 8:16], pf + "cols1")
            p.op("V", lambda e_: e_.tensor_tensor(xt[:, 2048:4096], xt[:, 2048:4096], WBB[:, 0:2048], ALU.mult),
                 reads=RXT1 + [pf + "WBB"], writes=RXT1)
            p.op("V", lambda e_: e_.tensor_tensor(xt[:, 2048:4096], xt[:, 2048:4096], WBB[:, 2048:4096], ALU.add),
                 reads=RXT1 + [pf + "WBB"], writes=RXT1)
            for k in range(16):
                bank = k % 4
                p.op("T", lambda e_, k=k, bank=bank: e_.transpose(
                    psB[:, bank * 512:bank * 512 + 128],
                    xt[:, 2048 + k * 128:2048 + (k + 1) * 128], identF[:, :]),
                    reads=RXT1 + ["identF"], writes=[psBb[bank]], inc=False)
                p.op("S", lambda e_, k=k, bank=bank, i=i: e_.copy(
                    XT[:, k * 2048 + i * 128:k * 2048 + i * 128 + 128],
                    psB[:, bank * 512:bank * 512 + 128]),
                    reads=[psBb[bank]], writes=[pf + f"XT{k}"])
        xpv = io["xprojT"].rearrange("(k q) o -> q k o", q=128)
        for half in range(2):
            p.dma(pf + "xpl", stg[:, 0:2048], xpv[:, half * 8:(half + 1) * 8, :],
                  reads=RDA, writes=RDA)
            p.op("V", lambda e_, half=half: e_.tensor_copy(XPJ[:, :], stg[:, 0:2048]),
                 reads=RDA, writes=[pf + "XPJ", pf + "hh"])
            for tc in range(4):
                for kk in range(8):
                    k = half * 8 + kk
                    p.op("T", lambda e_, tc=tc, k=k, kk=kk: e_.matmul(
                        psB[:, tc * 512:(tc + 1) * 512],
                        XPJ[:, kk * 256:kk * 256 + 128],
                        XT[:, k * 2048 + tc * 512:k * 2048 + (tc + 1) * 512],
                        start=(k == 0), stop=(k == 15)),
                        reads=[pf + "XPJ", pf + "hh", pf + f"XT{k}"], writes=[psBb[tc]], inc=False)
        for tc in range(4):
            p.op("S", lambda e_, tc=tc: e_.copy(
                DRT[:, tc * 512:(tc + 1) * 512], psB[:, tc * 512:(tc + 1) * 512]),
                reads=[psBb[tc]], writes=[pf + "DRT"])
        for half in range(2):
            p.dma(pf + "xpl", stg[:, 0:2048], xpv[:, half * 8:(half + 1) * 8, :],
                  reads=RDA, writes=RDA)
            p.op("V", lambda e_, half=half: e_.tensor_copy(XPJ[:, :], stg[:, 0:2048]),
                 reads=RDA, writes=[pf + "XPJ", pf + "hh"])
            for tc in range(4):
                for kk in range(8):
                    k = half * 8 + kk
                    p.op("T", lambda e_, tc=tc, k=k, kk=kk: e_.matmul(
                        psB[0:64, tc * 512:(tc + 1) * 512],
                        XPJ[:, kk * 256 + 128:kk * 256 + 192],
                        XT[:, k * 2048 + tc * 512:k * 2048 + (tc + 1) * 512],
                        start=(k == 0), stop=(k == 15)),
                        reads=[pf + "XPJ", pf + "hh", pf + f"XT{k}"], writes=[psBb[tc]], inc=False)
        for tc in range(4):
            p.op("S", lambda e_, tc=tc: e_.copy(
                BT[:, tc * 512:(tc + 1) * 512], psB[0:64, tc * 512:(tc + 1) * 512]),
                reads=[psBb[tc]], writes=[pf + "BT", pf + "DTK"])
        for half in range(2):
            p.dma(pf + "xpl", stg[:, 0:2048], xpv[:, half * 8:(half + 1) * 8, :],
                  reads=RDA, writes=RDA)
            p.op("V", lambda e_, half=half: e_.tensor_copy(XPJ[:, :], stg[:, 0:2048]),
                 reads=RDA, writes=[pf + "XPJ", pf + "hh"])
            for kk in range(8):
                k = half * 8 + kk
                p.op("T", lambda e_, k=k, kk=kk: e_.matmul(
                    psB[0:64, 0:512],
                    XPJ[:, kk * 256 + 192:kk * 256 + 256],
                    XT[:, k * 2048 + 0:k * 2048 + 512],
                    start=(k == 0), stop=(k == 15)),
                    reads=[pf + "XPJ", pf + "hh", pf + f"XT{k}"], writes=[psBb[0]], inc=False)
        p.op("S", lambda e_: e_.copy(CT[:, :], psB[0:64, 0:512]),
             reads=[psBb[0]], writes=[pf + "CT"])
        for tc in range(4):
            p.op("T", lambda e_, tc=tc: e_.matmul(
                psB[:, tc * 512:(tc + 1) * 512], sel64[:, :],
                BT[:, tc * 512:(tc + 1) * 512], start=True, stop=True),
                reads=[pf + "BT", pf + "DTK", "sel64"], writes=[psBb[tc]], inc=False)
            p.op("S", lambda e_, tc=tc: e_.copy(
                BREP[:, tc * 512:(tc + 1) * 512], psB[:, tc * 512:(tc + 1) * 512]),
                reads=[psBb[tc]], writes=[pf + "BREP"])
        p.op("T", lambda e_: e_.matmul(psA[:, 0:512], sel64[:, :], CT[:, :],
                                       start=True, stop=True),
             reads=[pf + "CT", "sel64"], writes=[psAb[0], pf + "psAc"], inc=False)
        p.op("S", lambda e_: e_.copy(CREP[:, :], psA[:, 0:512]),
             reads=[psAb[0]], writes=[pf + "CREP"])
        p.dma(pf + "dpl", stg[:, :], io["dtprojT"], reads=RDA, writes=RDA)
        p.op("V", lambda e_: e_.tensor_copy(DPJ[:, :], stg[:, :]),
             reads=RDA, writes=[pf + "DPJ"])
        for k in range(16):
            for c in range(4):
                p.op("T", lambda e_, k=k, c=c: e_.matmul(
                    psA[:, c * 512:(c + 1) * 512],
                    DPJ[:, k * 128:(k + 1) * 128],
                    DRT[:, c * 512:(c + 1) * 512], start=True, stop=True),
                    reads=[pf + "DPJ", pf + "DRT"], writes=[psAb[c]], inc=False)
                p.op("V", lambda e_, k=k, c=c: e_.tensor_scalar(
                    psB[:, c * 512:(c + 1) * 512], psA[:, c * 512:(c + 1) * 512],
                    DCOL[:, 16 + k:17 + k], None, ALU.add),
                    reads=[psAb[c], pf + "DCOL"], writes=[psBb[c]])
                p.op("S", lambda e_, c=c: e_.activation(
                    psA[:, c * 512:(c + 1) * 512], psB[:, c * 512:(c + 1) * 512], AF.Abs),
                    reads=[psBb[c]], writes=[psAb[c]])
                p.op("S", lambda e_, c=c: e_.activation(
                    dA[:, c * 512:(c + 1) * 512], psA[:, c * 512:(c + 1) * 512],
                    AF.Exp, scale=-1.0),
                    reads=[psAb[c]], writes=RDA)
                p.op("V", lambda e_, c=c: e_.tensor_scalar_add(
                    dA[:, c * 512:(c + 1) * 512], dA[:, c * 512:(c + 1) * 512], 1.0),
                    reads=RDA, writes=RDA)
                p.op("S", lambda e_, c=c: e_.activation(
                    dA[:, c * 512:(c + 1) * 512], dA[:, c * 512:(c + 1) * 512], AF.Ln),
                    reads=RDA, writes=RDA)
                p.op("V", lambda e_, c=c: e_.tensor_scalar_max(
                    psB[:, c * 512:(c + 1) * 512], psB[:, c * 512:(c + 1) * 512], 0.0),
                    reads=[psBb[c]], writes=[psBb[c]])
                p.op("V", lambda e_, k=k, c=c: e_.tensor_tensor(
                    DTK[:, c * 512:(c + 1) * 512], psB[:, c * 512:(c + 1) * 512],
                    dA[:, c * 512:(c + 1) * 512], ALU.add),
                    reads=[psBb[c]] + RDA, writes=[pf + "DTK"])
                p.op("V", lambda e_, k=k, c=c: e_.tensor_tensor(
                    XT[:, k * 2048 + c * 512:k * 2048 + (c + 1) * 512],
                    DTK[:, c * 512:(c + 1) * 512],
                    XT[:, k * 2048 + c * 512:k * 2048 + (c + 1) * 512], ALU.mult),
                    reads=[pf + "DTK", pf + f"XT{k}"], writes=[pf + f"XT{k}"])
            def emit_y(j, hhap, hhnm):
                for c in range(4):
                    p.op("T", lambda e_, j=j, c=c, hhap=hhap: e_.matmul(
                        psB[:, c * 512:(c + 1) * 512],
                        YST[:, 126 - 2 * j:254 - 2 * j],
                        hhap[:, c * 512:(c + 1) * 512],
                        start=(j == 0), stop=(j == 63)),
                        reads=hhnm + ["YST"], writes=[psBb[c]], inc=False)

            pend = None
            for j in range(64):
                J = k * 64 + j
                sl = j % 2
                dAap = dA[:, 0:2048] if sl == 0 else xt[:, 0:2048]
                dAnm = RDA if sl == 0 else RXT0
                ubap = ub[:, :] if sl == 0 else WBB[:, 0:2048]
                ubnm = [pf + "ub", pf + "mo"] if sl == 0 else [pf + "WBBu", pf + "WBB"]
                hhap = hh[:, :] if sl == 0 else WBB[:, 2048:4096]
                hhnm = [pf + "hh", pf + "XPJ"] if sl == 0 else [pf + "WBBh", pf + "WBB"]
                for c in range(4):
                    p.op("T", lambda e_, j=j, c=c: e_.matmul(
                        psA[:, c * 512:(c + 1) * 512], EJT[:, j * 128:(j + 1) * 128],
                        DTK[:, c * 512:(c + 1) * 512], start=True, stop=True),
                        reads=[pf + "DTK", "EJT"], writes=[psAb[c]], inc=False)
                p.op("S", lambda e_, J=J, dAap=dAap: e_.activation(
                    dAap, psA[:, 0:2048], AF.Exp, scale=ALC[:, J:J + 1]),
                    reads=psAb + [pf + "ALC"], writes=dAnm)
                for c in range(4):
                    p.op("T", lambda e_, j=j, k=k, c=c: e_.matmul(
                        psA[:, c * 512:(c + 1) * 512], EJT[:, j * 128:(j + 1) * 128],
                        XT[:, k * 2048 + c * 512:k * 2048 + (c + 1) * 512],
                        start=True, stop=True),
                        reads=[pf + f"XT{k}", "EJT"], writes=[psAb[c]], inc=False)
                if pend is not None:
                    emit_y(*pend)
                p.op("V", lambda e_, ubap=ubap: e_.tensor_tensor(
                    ubap, psA[:, 0:2048], BREP[:, :], ALU.mult),
                    reads=psAb + [pf + "BREP"], writes=ubnm)
                p.op("V", lambda e_, hhap=hhap, dAap=dAap, ubap=ubap: e_.tensor_tensor_scan(
                    hhap, dAap, ubap, 0.0, ALU.mult, ALU.add),
                    reads=dAnm + ubnm, writes=hhnm)
                p.op("V", lambda e_, hhap=hhap: e_.tensor_tensor(
                    hhap.rearrange("q (a b) -> q a b", a=4),
                    hhap.rearrange("q (a b) -> q a b", a=4),
                    CREP[:, :].unsqueeze(1).broadcast_to([128, 4, 512]), ALU.mult),
                    reads=hhnm + [pf + "CREP"], writes=hhnm)
                pend = (j, hhap, hhnm)
            emit_y(*pend)
            for c in range(4):
                p.op("V", lambda e_, k=k, c=c: e_.tensor_tensor(
                    LT[:, k * 2048 + c * 512:k * 2048 + (c + 1) * 512],
                    psB[:, c * 512:(c + 1) * 512],
                    LT[:, k * 2048 + c * 512:k * 2048 + (c + 1) * 512], ALU.add),
                    reads=[psBb[c], pf + f"LT{k}"], writes=[pf + f"LT{k}"])
        opv = io["s_oprojb"].rearrange("(k q) o -> q k o", q=128)
        for dc in range(4):
            for ig in range(4):
                for half in range(2):
                    p.dma(pf + "opbl", OPB[:, :],
                          opv[:, half * 8:(half + 1) * 8, dc * 512:(dc + 1) * 512],
                          reads=[pf + "opd"] + RDA, writes=RDA)
                    for ii in range(4):
                        i = ig * 4 + ii
                        pb = ii
                        for kk in range(8):
                            k = half * 8 + kk
                            p.op("T", lambda e_, i=i, k=k, kk=kk, pb=pb: e_.matmul(
                                psA[:, pb * 512:(pb + 1) * 512],
                                LT[:, k * 2048 + i * 128:k * 2048 + i * 128 + 128],
                                OPB[:, kk * 512:(kk + 1) * 512],
                                start=(k == 0), stop=(k == 15)),
                                reads=[pf + f"LT{k}"] + RDA, writes=[psAb[pb]], inc=False)
                for ii in range(4):
                    i = ig * 4 + ii
                    pb = ii
                    ms = ii % 2
                    p.op("S", lambda e_, pb=pb, ms=ms: e_.copy(
                        mo[:, ms * 512:(ms + 1) * 512], psA[:, pb * 512:(pb + 1) * 512]),
                        reads=[psAb[pb]], writes=RUB)
                    p.dma(pf + "most" + str(ms),
                          smix[i * 128:(i + 1) * 128, dc * 512:(dc + 1) * 512],
                          mo[:, ms * 512:(ms + 1) * 512], reads=RUB,
                          writes=[pf + f"smx{i}_{dc}"])
        for i in range(16):
            p.dma(pf + "mxl", mix, smix[i * 128:(i + 1) * 128, :],
                  reads=[pf + f"smx{i}_{d_}" for d_ in range(4)],
                  writes=RXT1)
            p.dma(pf + "xrl", xr, xin[i * 128:(i + 1) * 128, :], writes=RXT0)
            ln_normalize(p, mix, xn[:, :], RXT1, RDA, colsb[:, 0:8], pf + "cols0")
            p.op("V", lambda e_: e_.tensor_tensor(mix, xr, xn[:, :], ALU.add),
                 reads=RXT0 + RDA + [pf + "mix"], writes=RXT1)
            p.dma(pf + "xst", xout[i * 128:(i + 1) * 128, :], mix,
                  reads=RXT1, writes=[pf + f"xo{b}_{i}"])
    return [pf + "xst", pf + "most0", pf + "most1"]


def mk_l0attn_program():
    nc = bass.Bass("TRN2", target_bir_lowering=False)
    io = {}
    io["xprojT"] = nc.dram_tensor("xprojT", [2048, 256], F32, kind="ExternalInput").ap()
    io["dtprojT"] = nc.dram_tensor("dtprojT", [128, 2048], F32, kind="ExternalInput").ap()
    io["oprojT"] = nc.dram_tensor("oprojT", [2048, 2048], F32, kind="ExternalInput").ap()
    io["norm_w"] = nc.dram_tensor("norm_w", [2048], F32, kind="ExternalInput").ap()
    io["norm_b"] = nc.dram_tensor("norm_b", [2048], F32, kind="ExternalInput").ap()
    io["dcols"] = nc.dram_tensor("dcols", [128, 16], F32, kind="ExternalInput").ap()
    io["bcols"] = nc.dram_tensor("bcols", [128, 16], F32, kind="ExternalInput").ap()
    io["alogcols"] = nc.dram_tensor("alogcols", [128, 1024], F32, kind="ExternalInput").ap()
    io["identF"] = nc.dram_tensor("identF", [128, 128], F32, kind="ExternalInput").ap()
    io["sel64"] = nc.dram_tensor("sel64", [64, 128], BF16, kind="ExternalInput").ap()
    io["EJT"] = nc.dram_tensor("EJT", [128, 64 * 128], BF16, kind="ExternalInput").ap()
    io["YST"] = nc.dram_tensor("YST", [128, 256], BF16, kind="ExternalInput").ap()
    io["s_oprojb"] = nc.dram_tensor("s_oprojb", [2048, 2048], BF16, kind="Internal").ap()
    io["s_mix"] = nc.dram_tensor("s_mix", [2048, 2048], F32, kind="Internal").ap()
    xin = nc.dram_tensor("xin", [B_PER_CORE, 2048, 2048], F32, kind="ExternalInput").ap()
    xout = nc.dram_tensor("xout", [B_PER_CORE, 2048, 2048], F32, kind="ExternalOutput").ap()
    io["xin"] = [xin[b] for b in range(B_PER_CORE)]
    io["xout"] = [xout[b] for b in range(B_PER_CORE)]
    with ExitStack() as st:
        p = Prog(nc)
        streams = build_l0attn(p, st, io)
        p.finish(streams)
        p.emit(st)
    return nc


def host_consts_l0attn(A_log, D_param, dt_bias):
    dcols = np.ascontiguousarray(D_param.reshape(16, 128).T)
    bcols = np.ascontiguousarray(dt_bias.reshape(16, 128).T)
    alog = np.empty((128, 1024), dtype=np.float32)
    for pp in range(128):
        J = np.arange(1024)
        d = (J // 64) * 128 + 2 * (J % 64) + (1 if pp >= 64 else 0)
        alog[pp, :] = A_log[d, pp % 64]
    sel = np.zeros((64, 128), dtype=np.float32)
    for n in range(64):
        sel[n, n] = 1.0
        sel[n, 64 + n] = 1.0
    ej = np.zeros((128, 64 * 128), dtype=np.float32)
    for j in range(64):
        ej[2 * j, j * 128 + 0:j * 128 + 64] = 1.0
        ej[2 * j + 1, j * 128 + 64:j * 128 + 128] = 1.0
    ys = np.zeros((128, 256), dtype=np.float32)
    ys[0:64, 126] = 1.0
    ys[64:128, 127] = 1.0
    return {
        "identF": np.eye(128, dtype=np.float32),
        "sel64": sel.astype(ml_dtypes.bfloat16),
        "EJT": ej.astype(ml_dtypes.bfloat16),
        "YST": ys.astype(ml_dtypes.bfloat16),
        "dcols": dcols.astype(np.float32),
        "bcols": bcols.astype(np.float32),
        "alogcols": alog.astype(np.float32),
    }


def kernel(L0_attn_x, L0_attn_norm_w, L0_attn_norm_b, L0_attn_x_proj_w,
           L0_attn_dt_proj_w, L0_attn_dt_proj_b, L0_attn_A_log, L0_attn_D_param,
           L0_attn_out_proj_w, L0_ffn_W1, L0_ffn_W2, L0_ffn_W3,
           L1_attn_Wq, L1_attn_Wk, L1_attn_Wv, L1_attn_Wo,
           L1_ffn_weight, L1_ffn_bias):
    x = np.ascontiguousarray(L0_attn_x, dtype=np.float32)
    n = NCORES

    def shards(arr):
        return [np.ascontiguousarray(arr[2 * c:2 * c + 2]) for c in range(n)]

    cn0 = host_consts_l0attn(np.asarray(L0_attn_A_log, dtype=np.float32),
                             np.asarray(L0_attn_D_param, dtype=np.float32),
                             np.asarray(L0_attn_dt_proj_b, dtype=np.float32))
    w0 = dict(
        xprojT=np.ascontiguousarray(np.asarray(L0_attn_x_proj_w, np.float32).T),
        dtprojT=np.ascontiguousarray(np.asarray(L0_attn_dt_proj_w, np.float32).T),
        oprojT=np.ascontiguousarray(np.asarray(L0_attn_out_proj_w, np.float32).T),
        norm_w=np.asarray(L0_attn_norm_w, np.float32),
        norm_b=np.asarray(L0_attn_norm_b, np.float32), **cn0)
    nc = mk_l0attn_program()
    res = run_bass_kernel_spmd(nc, [dict(xin=s, **w0) for s in shards(x)],
                               core_ids=list(range(n)))
    x = np.concatenate([r["xout"] for r in res.results], axis=0)

    W2T = np.ascontiguousarray(np.asarray(L0_ffn_W2, np.float32).T)
    w2pm = np.ascontiguousarray(
        W2T.reshape(64, 128, 2048).transpose(1, 0, 2).reshape(8192, 2048))
    w1 = dict(
        w1T=np.ascontiguousarray(np.asarray(L0_ffn_W1, np.float32).T),
        w3T=np.ascontiguousarray(np.asarray(L0_ffn_W3, np.float32).T),
        w2pm=w2pm, **host_consts_l0ffn())
    nc = mk_l0ffn_program()
    res = run_bass_kernel_spmd(nc, [dict(xin=s, **w1) for s in shards(x)],
                               core_ids=list(range(n)))
    x = np.concatenate([r["xout"] for r in res.results], axis=0)

    w2 = dict(Wq=np.asarray(L1_attn_Wq, np.float32), Wk=np.asarray(L1_attn_Wk, np.float32),
              Wv=np.asarray(L1_attn_Wv, np.float32), Wo=np.asarray(L1_attn_Wo, np.float32),
              **host_consts_l1attn())
    nc = mk_l1attn_program()
    res = run_bass_kernel_spmd(nc, [dict(xin=s, **w2) for s in shards(x)],
                               core_ids=list(range(n)))
    x = np.concatenate([r["xout"] for r in res.results], axis=0)

    w3 = dict(wT=np.ascontiguousarray(np.asarray(L1_ffn_weight, np.float32).T),
              bias=np.asarray(L1_ffn_bias, np.float32), **host_consts())
    nc = mk_l1ffn_program()
    res = run_bass_kernel_spmd(nc, [dict(xin=s, **w3) for s in shards(x)],
                               core_ids=list(range(n)))
    x = np.concatenate([r["xout"] for r in res.results], axis=0)
    return np.ascontiguousarray(x, dtype=np.float32)


def mk_fused_program():
    nc = bass.Bass("TRN2", target_bir_lowering=False)
    g = {}
    g["identF"] = nc.dram_tensor("identF", [128, 128], F32, kind="ExternalInput").ap()
    g["ones128"] = nc.dram_tensor("ones128", [128, 1], BF16, kind="ExternalInput").ap()
    g["ones1x512"] = nc.dram_tensor("ones1x512", [1, 512], BF16, kind="ExternalInput").ap()
    g["s_mix"] = nc.dram_tensor("s_mix", [2048, 2048], F32, kind="Internal").ap()
    xin = nc.dram_tensor("xin", [B_PER_CORE, 2048, 2048], F32, kind="ExternalInput").ap()
    xout = nc.dram_tensor("xout", [B_PER_CORE, 2048, 2048], F32, kind="ExternalOutput").ap()
    x1 = nc.dram_tensor("x1", [B_PER_CORE, 2048, 2048], F32, kind="Internal").ap()
    x2 = nc.dram_tensor("x2", [B_PER_CORE, 2048, 2048], F32, kind="Internal").ap()
    x3 = nc.dram_tensor("x3", [B_PER_CORE, 2048, 2048], F32, kind="Internal").ap()

    io0 = dict(g)
    io0["xprojT"] = nc.dram_tensor("xprojT", [2048, 256], F32, kind="ExternalInput").ap()
    io0["dtprojT"] = nc.dram_tensor("dtprojT", [128, 2048], F32, kind="ExternalInput").ap()
    io0["oprojT"] = nc.dram_tensor("oprojT", [2048, 2048], F32, kind="ExternalInput").ap()
    io0["norm_w"] = nc.dram_tensor("norm_w", [2048], F32, kind="ExternalInput").ap()
    io0["norm_b"] = nc.dram_tensor("norm_b", [2048], F32, kind="ExternalInput").ap()
    io0["dcols"] = nc.dram_tensor("dcols", [128, 16], F32, kind="ExternalInput").ap()
    io0["bcols"] = nc.dram_tensor("bcols", [128, 16], F32, kind="ExternalInput").ap()
    io0["alogcols"] = nc.dram_tensor("alogcols", [128, 1024], F32, kind="ExternalInput").ap()
    io0["sel64"] = nc.dram_tensor("sel64", [64, 128], BF16, kind="ExternalInput").ap()
    io0["EJT"] = nc.dram_tensor("EJT", [128, 64 * 128], BF16, kind="ExternalInput").ap()
    io0["YST"] = nc.dram_tensor("YST", [128, 256], BF16, kind="ExternalInput").ap()
    io0["s_oprojb"] = nc.dram_tensor("s_oprojb", [2048, 2048], BF16, kind="Internal").ap()
    io0["xin"] = [xin[b] for b in range(B_PER_CORE)]
    io0["xout"] = [x1[b] for b in range(B_PER_CORE)]

    io1 = dict(g)
    io1["w1T"] = nc.dram_tensor("w1T", [2048, 8192], F32, kind="ExternalInput").ap()
    io1["w3T"] = nc.dram_tensor("w3T", [2048, 8192], F32, kind="ExternalInput").ap()
    io1["w2pm"] = nc.dram_tensor("w2pm", [8192, 2048], F32, kind="ExternalInput").ap()
    io1["H128"] = nc.dram_tensor("H128", [128, 128], BF16, kind="ExternalInput").ap()
    io1["BD64"] = nc.dram_tensor("BD64", [128, 128], BF16, kind="ExternalInput").ap()
    io1["s_w1tb"] = nc.dram_tensor("s_w1tb", [2048, 8192], BF16, kind="Internal").ap()
    io1["s_w3tb"] = nc.dram_tensor("s_w3tb", [2048, 8192], BF16, kind="Internal").ap()
    io1["s_w2hgm"] = nc.dram_tensor("s_w2hgm", [8192, 2048], BF16, kind="Internal").ap()
    io1["xin"] = [x1[b] for b in range(B_PER_CORE)]
    io1["xout"] = [x2[b] for b in range(B_PER_CORE)]

    io2 = dict(g)
    for nm in ("Wq", "Wk", "Wv", "Wo"):
        io2[nm] = nc.dram_tensor(nm, [2048, 2048], F32, kind="ExternalInput").ap()
    io2["masks"] = nc.dram_tensor("masks", [128, 4 * 512], BF16, kind="ExternalInput").ap()
    for nm in ("s_wqb", "s_wkb", "s_wvb", "s_wob"):
        io2[nm] = nc.dram_tensor(nm, [2048, 2048], BF16, kind="Internal").ap()
    io2["s_outT"] = nc.dram_tensor("s_outT", [16, 128, 2048], BF16, kind="Internal").ap()
    io2["xin"] = [x2[b] for b in range(B_PER_CORE)]
    io2["xout"] = [x3[b] for b in range(B_PER_CORE)]

    io3 = dict(g)
    io3["wT"] = nc.dram_tensor("wT", [2048, 2048], F32, kind="ExternalInput").ap()
    io3["bias"] = nc.dram_tensor("bias", [2048], F32, kind="ExternalInput").ap()
    io3["scr_row"] = nc.dram_tensor("scr_row", [1, 2048], F32, kind="Internal").ap()
    io3["xin"] = [x3[b] for b in range(B_PER_CORE)]
    io3["xout"] = [xout[b] for b in range(B_PER_CORE)]

    with ExitStack() as st:
        p = Prog(nc)
        with ExitStack() as s0:
            build_l0attn(p, s0, io0)
        p.fence()
        with ExitStack() as s1:
            build_l0ffn(p, s1, io1)
        p.fence()
        with ExitStack() as s2:
            build_l1attn(p, s2, io2)
        p.fence()
        with ExitStack() as s3:
            streams = build_l1ffn(p, s3, io3)
        p.finish(streams)
        p.emit(st)
    return nc


def kernel(L0_attn_x, L0_attn_norm_w, L0_attn_norm_b, L0_attn_x_proj_w,
           L0_attn_dt_proj_w, L0_attn_dt_proj_b, L0_attn_A_log, L0_attn_D_param,
           L0_attn_out_proj_w, L0_ffn_W1, L0_ffn_W2, L0_ffn_W3,
           L1_attn_Wq, L1_attn_Wk, L1_attn_Wv, L1_attn_Wo,
           L1_ffn_weight, L1_ffn_bias):
    x = np.ascontiguousarray(L0_attn_x, dtype=np.float32)
    n = NCORES
    cn0 = host_consts_l0attn(np.asarray(L0_attn_A_log, dtype=np.float32),
                             np.asarray(L0_attn_D_param, dtype=np.float32),
                             np.asarray(L0_attn_dt_proj_b, dtype=np.float32))
    W2T = np.ascontiguousarray(np.asarray(L0_ffn_W2, np.float32).T)
    w2pm = np.ascontiguousarray(
        W2T.reshape(64, 128, 2048).transpose(1, 0, 2).reshape(8192, 2048))
    common = dict(
        xprojT=np.ascontiguousarray(np.asarray(L0_attn_x_proj_w, np.float32).T),
        dtprojT=np.ascontiguousarray(np.asarray(L0_attn_dt_proj_w, np.float32).T),
        oprojT=np.ascontiguousarray(np.asarray(L0_attn_out_proj_w, np.float32).T),
        norm_w=np.asarray(L0_attn_norm_w, np.float32),
        norm_b=np.asarray(L0_attn_norm_b, np.float32),
        w1T=np.ascontiguousarray(np.asarray(L0_ffn_W1, np.float32).T),
        w3T=np.ascontiguousarray(np.asarray(L0_ffn_W3, np.float32).T),
        w2pm=w2pm,
        Wq=np.asarray(L1_attn_Wq, np.float32), Wk=np.asarray(L1_attn_Wk, np.float32),
        Wv=np.asarray(L1_attn_Wv, np.float32), Wo=np.asarray(L1_attn_Wo, np.float32),
        wT=np.ascontiguousarray(np.asarray(L1_ffn_weight, np.float32).T),
        bias=np.asarray(L1_ffn_bias, np.float32),
        **cn0)
    for dd in (host_consts_l0ffn(), host_consts_l1attn(), host_consts()):
        common.update(dd)
    nc = mk_fused_program()
    in_maps = [dict(xin=np.ascontiguousarray(x[2 * c:2 * c + 2]), **common)
               for c in range(n)]
    res = run_bass_kernel_spmd(nc, in_maps, core_ids=list(range(n)))
    out = np.concatenate([r["xout"] for r in res.results], axis=0)
    return np.ascontiguousarray(out, dtype=np.float32)


# version 05
# speedup vs baseline: 1.5766x; 1.0992x over previous
import numpy as np
import ml_dtypes
from contextlib import ExitStack

import concourse.bass as bass
import concourse.mybir as mybir
from concourse.bass_utils import run_bass_kernel_spmd

F32 = mybir.dt.float32
BF16 = mybir.dt.bfloat16
AF = mybir.ActivationFunctionType
ALU = mybir.AluOpType
AX = mybir.AxisListType

NCORES = 8
B_PER_CORE = 2
T = 2048
D = 2048
EPS = 1e-5

ENGINES = ("T", "V", "S", "G", "Q")


class Prog:

    def __init__(self, nc, same_engine_sync=True):
        self.nc = nc
        self.items = {e: [] for e in ENGINES}
        self.dma_cnt = {}
        self.stream_names = []
        self.lw = {}
        self.rd = {}
        self.same_sync = same_engine_sync

    def _gather(self, reads, writes, after):
        deps = set(t for t in after if t is not None)
        for r in reads:
            t = self.lw.get(r)
            if t is not None:
                deps.add(t)
        for w in writes:
            t = self.lw.get(w)
            if t is not None:
                deps.add(t)
            for t2 in self.rd.get(w, ()):
                deps.add(t2)
        for t in deps:
            if t[0] == "E":
                self.items[t[1]][t[2]][2] = True
        return deps

    def _record_rw(self, tok, reads, writes):
        for r in reads:
            self.rd.setdefault(r, []).append(tok)
        for w in writes:
            self.lw[w] = tok
            self.rd[w] = []

    def op(self, engine, fn, reads=(), writes=(), after=(), inc=True):
        deps = self._gather(reads, writes, after)
        tok = ("E", engine, len(self.items[engine]))
        self.items[engine].append([fn, deps, inc, []])
        self._record_rw(tok, reads, writes)
        return tok

    def dma(self, stream, out, in_, engine="Q", reads=(), writes=(), after=()):
        deps = self._gather(reads, writes, after)
        idx = self.dma_cnt.get(stream, 0)
        self.dma_cnt[stream] = idx + 1
        if stream not in self.stream_names:
            self.stream_names.append(stream)
        tok = ("D", stream, idx)

        def fn(eng, out=out, in_=in_):
            return eng.dma_start(out, in_)

        self.items[engine].append([fn, deps, False, [(stream, 16)]])
        self._record_rw(tok, reads, writes)
        return tok

    def wait_only(self, engine, after):
        deps = self._gather((), (), after)
        self.items[engine].append([None, deps, False, []])

    def emit(self, stack):
        nc = self.nc
        sems = {}
        for name in list(ENGINES) + self.stream_names:
            sems[name] = stack.enter_context(nc.semaphore("sem_" + name))

        resolve = {}
        for e in ENGINES:
            n = len(self.items[e])
            nxt = [None] * n
            cnt = 0
            incs_upto = [0] * (n + 1)
            for i in range(n):
                incs_upto[i + 1] = incs_upto[i] + (1 if self.items[e][i][2] else 0)
            next_inc = [None] * (n + 1)
            last = None
            for i in range(n - 1, -1, -1):
                if self.items[e][i][2]:
                    last = i
                next_inc[i] = last
            resolve[e] = (incs_upto, next_inc)

        def wait_of(tok):
            if tok[0] == "D":
                return (tok[1], 16 * (tok[2] + 1))
            e, idx = tok[1], tok[2]
            incs_upto, next_inc = resolve[e]
            j = next_inc[idx]
            assert j is not None, f"dep on non-inc'd tail op {tok}"
            return (e, incs_upto[j + 1])

        block = stack.enter_context(nc.Block())

        def make_emitter(ename):
            ops = self.items[ename]

            def emitter(eng):
                floors = {}
                for fn, deps, inc, extra in ops:
                    waits = []
                    for t in deps:
                        if t[0] == "E" and t[1] == ename:
                            if ename == "T" or not self.same_sync:
                                continue
                        waits.append(wait_of(t))
                    for (s, v) in sorted(waits):
                        if floors.get(s, 0) >= v:
                            continue
                        eng.wait_ge(sems[s], v)
                        floors[s] = v
                    if fn is None:
                        continue
                    inst = fn(eng)
                    if inc:
                        inst.then_inc(sems[ename], 1)
                    for (s, n) in extra:
                        inst.then_inc(sems[s], n)

            return emitter

        block.tensor(make_emitter("T"))
        block.vector(make_emitter("V"))
        block.scalar(make_emitter("S"))
        block.gpsimd(make_emitter("G"))
        block.sync(make_emitter("Q"))

    def fence(self):
        toks = []
        for e in ENGINES:
            for i in range(len(self.items[e]) - 1, -1, -1):
                if not self.items[e][i][3]:
                    if self.items[e][i][0] is not None:
                        self.items[e][i][2] = True
                        toks.append(("E", e, i))
                    break
        for s, cdma in self.dma_cnt.items():
            toks.append(("D", s, cdma - 1))
        for e in ENGINES:
            deps = set(t for t in toks if not (t[0] == "E" and t[1] == e))
            self.items[e].append([None, deps, False, []])

    def finish(self, streams):
        toks = [("D", s, self.dma_cnt[s] - 1) for s in streams if s in self.dma_cnt]
        self.wait_only("Q", toks)


def _l(x):
    return [x] if isinstance(x, str) else list(x)


def ln_normalize(p, x_ap, out_ap, xbuf, outbuf, c_ap, cbuf, F=2048):
    xbuf, outbuf, cbuf = _l(xbuf), _l(outbuf), _l(cbuf)
    p.op("V", lambda e: e.reduce_sum(c_ap[:, 0:1], x_ap, axis=AX.X),
         reads=xbuf, writes=cbuf)
    p.op("S", lambda e: e.activation(out_ap, x_ap, AF.Square, accum_out=c_ap[:, 1:2]),
         reads=xbuf, writes=outbuf + cbuf)
    p.op("V", lambda e: e.tensor_scalar_mul(c_ap[:, 2:3], c_ap[:, 0:1], 1.0 / F),
         reads=cbuf, writes=cbuf)
    p.op("V", lambda e: e.tensor_scalar_mul(c_ap[:, 3:4], c_ap[:, 1:2], 1.0 / F),
         reads=cbuf, writes=cbuf)
    p.op("V", lambda e: e.tensor_tensor(c_ap[:, 6:7], c_ap[:, 2:3], c_ap[:, 2:3], ALU.mult),
         reads=cbuf, writes=cbuf)
    p.op("V", lambda e: e.tensor_tensor(c_ap[:, 3:4], c_ap[:, 3:4], c_ap[:, 6:7], ALU.subtract),
         reads=cbuf, writes=cbuf)
    p.op("V", lambda e: e.tensor_scalar_add(c_ap[:, 3:4], c_ap[:, 3:4], EPS),
         reads=cbuf, writes=cbuf)
    p.op("S", lambda e: e.activation(c_ap[:, 7:8], c_ap[:, 3:4], AF.Ln),
         reads=cbuf, writes=cbuf)
    p.op("S", lambda e: e.activation(c_ap[:, 4:5], c_ap[:, 7:8], AF.Exp, scale=-0.5),
         reads=cbuf, writes=cbuf)
    p.op("V", lambda e: e.tensor_tensor(c_ap[:, 6:7], c_ap[:, 2:3], c_ap[:, 4:5], ALU.mult),
         reads=cbuf, writes=cbuf)
    p.op("V", lambda e: e.tensor_scalar_mul(c_ap[:, 5:6], c_ap[:, 6:7], -1.0),
         reads=cbuf, writes=cbuf)
    p.op("S", lambda e: e.activation(out_ap, x_ap, AF.Identity, scale=c_ap[:, 4:5], bias=c_ap[:, 5:6]),
         reads=xbuf + cbuf, writes=outbuf)



def build_l1ffn(p, st, io, pf="f1"):
    nc = p.nc
    e = st.enter_context

    identF = e(nc.sbuf_tensor(pf + "identF", [128, 128], F32))
    p.dma(pf + "cid", identF[:, :], io["identF"], writes=["identF"])

    psA = e(nc.psum_tensor(pf + "psA", [128, 2048], F32))
    psB = e(nc.psum_tensor(pf + "psB", [128, 2048], F32))
    psAb = ["psA0", "psA1", "psA2", "psA3"]
    psBb = ["psB0", "psB1", "psB2", "psB3"]

    wstg = e(nc.sbuf_tensor(pf + "wstg", [128, 2048], F32))
    SGN = e(nc.sbuf_tensor(pf + "SGN", [128, 16 * 2048], BF16))
    ABSB = e(nc.sbuf_tensor(pf + "ABS", [128, 2048], BF16))
    ones_col = e(nc.sbuf_tensor(pf + "ones_col", [128, 1], BF16))
    p.dma(pf + "cones", ones_col[:, :], io["ones128"], writes=["ones_col"])
    ab_bc = e(nc.sbuf_tensor(pf + "ab_bc", [128, 2 * 2048], F32))
    xt = e(nc.sbuf_tensor(pf + "xt", [128, 2 * 2048], F32))
    xn = e(nc.sbuf_tensor(pf + "xn", [128, 2048], F32))
    lnm = e(nc.sbuf_tensor(pf + "lnm", [128, 2048], F32))
    colsb = e(nc.sbuf_tensor(pf + "cols", [128, 16], F32))
    XHT = e(nc.sbuf_tensor(pf + "XHT", [128, 16 * 2048], BF16))
    mix = xt[:, 0:2048]
    xr = xt[:, 2048:4096]
    outt = xn
    a_row = lnm

    for k in range(16):
        p.dma(pf + "wld", wstg[:, :],
              io["wT"][k * 128:(k + 1) * 128, :], writes=[pf + "wstg"])
        p.op("S", lambda e_, k=k: e_.activation(
            SGN[:, k * 2048:(k + 1) * 2048], wstg[:, :], AF.Sign),
            reads=[pf + "wstg"], writes=[pf + f"SGN{k}"])
        p.op("S", lambda e_, k=k: e_.activation(
            ABSB[:, :], wstg[:, :], AF.Abs),
            reads=[pf + "wstg"], writes=[pf + "ABS"])
        for c in range(4):
            p.op("T", lambda e_, k=k, c=c: e_.matmul(
                psA[0:1, c * 512:c * 512 + 512], ones_col[:, :],
                ABSB[:, c * 512:(c + 1) * 512],
                start=(k == 0), stop=(k == 15)),
                reads=[pf + "ABS", "ones_col"], writes=[psAb[c]],
                inc=False)
    p.op("S", lambda e_: e_.activation(a_row[0:1, 0:2048], psA[0:1, 0:2048], AF.Copy, scale=1.0 / 2048),
         reads=psAb, writes=[pf + "a_row", pf + "lnm"])
    p.dma(pf + "ast", io["scr_row"], a_row[0:1, 0:2048], reads=[pf + "a_row"], writes=[pf + "srow"])
    p.dma(pf + "ald", ab_bc[:, 0:2048],
          io["scr_row"].partition_broadcast(128), reads=[pf + "srow"], writes=[pf + "ab"])
    p.dma(pf + "bld", ab_bc[:, 2048:4096],
          io["bias"].unsqueeze(0).partition_broadcast(128), writes=[pf + "ab"])


    def sl(t_, s, w=2048):
        return t_[:, s * w:(s + 1) * w]

    for b in range(B_PER_CORE):
        xin = io["xin"][b]
        xout = io["xout"][b]
        for i in range(16):
            s = i % 2
            p.dma(pf + "xld" + str(s), sl(xt, s), xin[i * 128:(i + 1) * 128, :],
                  writes=[pf + f"xt{s}"])
            ln_normalize(p, sl(xt, s), xn[:, :], [pf + f"xt{s}"], [pf + "xn", pf + "outt"],
                         colsb[:, s * 8:(s + 1) * 8], pf + f"cols{s}")
            for k in range(16):
                bank = k % 4
                p.op("T", lambda e_, k=k, bank=bank: e_.transpose(
                    psB[:, bank * 512:bank * 512 + 128],
                    xn[:, k * 128:(k + 1) * 128], identF[:, :]),
                    reads=[pf + "xn", "identF"], writes=[psBb[bank]])
                p.op("S", lambda e_, k=k, bank=bank, i=i: e_.copy(
                    XHT[:, k * 2048 + i * 128:k * 2048 + i * 128 + 128],
                    psB[:, bank * 512:bank * 512 + 128]),
                    reads=[psBb[bank]], writes=[pf + f"XHT{k}"])
        for i in range(16):
            s = i % 2
            p.dma(pf + "xrl", xr[:, :], xin[i * 128:(i + 1) * 128, :],
                  writes=[pf + "xr0", pf + "xt1"])
            for c in range(4):
                pb = (4 * i + c) % 4
                for k in range(16):
                    p.op("T", lambda e_, i=i, c=c, k=k, pb=pb: e_.matmul(
                        psA[:, pb * 512:(pb + 1) * 512],
                        XHT[:, k * 2048 + i * 128:k * 2048 + i * 128 + 128],
                        SGN[:, k * 2048 + c * 512:k * 2048 + c * 512 + 512],
                        start=(k == 0), stop=(k == 15)),
                        reads=[pf + f"XHT{k}", pf + f"SGN{k}"], writes=[psAb[pb]],
                        inc=False)
                p.op("V", lambda e_, c=c, pb=pb: e_.tensor_tensor(
                    mix[:, c * 512:(c + 1) * 512], psA[:, pb * 512:(pb + 1) * 512],
                    ab_bc[:, c * 512:(c + 1) * 512], ALU.mult),
                    reads=[psAb[pb], pf + "ab"], writes=[pf + "mix", pf + "xt0"])
                p.op("V", lambda e_, c=c: e_.tensor_tensor(
                    mix[:, c * 512:(c + 1) * 512], mix[:, c * 512:(c + 1) * 512],
                    ab_bc[:, 2048 + c * 512:2048 + (c + 1) * 512], ALU.add),
                    reads=[pf + "mix", pf + "xt0", pf + "ab"], writes=[pf + "mix", pf + "xt0"])
            ln_normalize(p, mix[:, :], lnm[:, :], [pf + "mix", pf + "xt0"],
                         [pf + "lnm", pf + "a_row"],
                         colsb[:, s * 8:(s + 1) * 8], pf + f"cols{s}")
            p.op("V", lambda e_: e_.tensor_tensor(
                outt[:, :], xr[:, :], lnm[:, :], ALU.add),
                reads=[pf + "xr0", pf + "xt1", pf + "lnm"], writes=[pf + "outt", pf + "xn"])
            p.dma(pf + "xst", xout[i * 128:(i + 1) * 128, :], outt[:, :],
                  reads=[pf + "outt", pf + "xn"], writes=[pf + f"xo{b}_{i}"])
    return [pf + "xst"]


def mk_l1ffn_program():
    nc = bass.Bass("TRN2", target_bir_lowering=False)
    io = {}
    io["wT"] = nc.dram_tensor("wT", [2048, 2048], F32, kind="ExternalInput").ap()
    io["bias"] = nc.dram_tensor("bias", [2048], F32, kind="ExternalInput").ap()
    io["identF"] = nc.dram_tensor("identF", [128, 128], F32, kind="ExternalInput").ap()
    io["ones128"] = nc.dram_tensor("ones128", [128, 1], BF16, kind="ExternalInput").ap()
    io["scr_row"] = nc.dram_tensor("scr_row", [1, 2048], F32, kind="Internal").ap()
    xin = nc.dram_tensor("xin", [B_PER_CORE, 2048, 2048], F32, kind="ExternalInput").ap()
    xout = nc.dram_tensor("xout", [B_PER_CORE, 2048, 2048], F32, kind="ExternalOutput").ap()
    io["xin"] = [xin[b] for b in range(B_PER_CORE)]
    io["xout"] = [xout[b] for b in range(B_PER_CORE)]

    with ExitStack() as st:
        p = Prog(nc)
        streams = build_l1ffn(p, st, io)
        p.finish(streams)
        p.emit(st)
    return nc


def host_consts():
    return {
        "identF": np.eye(128, dtype=np.float32),
        "ones128": np.ones((128, 1), dtype=ml_dtypes.bfloat16),
    }




def sl2(t_, s, w=2048):
    return t_[:, s * w:(s + 1) * w]



def build_l0ffn(p, st, io, pf="f0"):
    nc = p.nc
    e = st.enter_context

    identF = e(nc.sbuf_tensor(pf + "identF", [128, 128], F32))
    H128 = e(nc.sbuf_tensor(pf + "H128", [128, 128], BF16))
    BD64 = e(nc.sbuf_tensor(pf + "BD64", [128, 128], BF16))
    p.dma(pf + "cid", identF[:, :], io["identF"], writes=["identF"])
    p.dma(pf + "chh", H128[:, :], io["H128"], writes=["H128"])
    p.dma(pf + "cbd", BD64[:, :], io["BD64"], writes=["BD64"])

    psA = e(nc.psum_tensor(pf + "psA", [128, 2048], F32))
    psB = e(nc.psum_tensor(pf + "psB", [128, 2048], F32))
    psAb = ["psA0", "psA1", "psA2", "psA3"]
    psBb = ["psB0", "psB1", "psB2", "psB3"]

    XFT = e(nc.sbuf_tensor(pf + "XFT", [128, 16 * 2048], BF16))
    FW1 = e(nc.sbuf_tensor(pf + "FW1", [128, 64 * 512], BF16))
    W1B = e(nc.sbuf_tensor(pf + "W1B", [128, 16 * 512], BF16))
    W3B = e(nc.sbuf_tensor(pf + "W3B", [128, 16 * 512], BF16))
    W2B = e(nc.sbuf_tensor(pf + "W2B", [128, 8 * 512], BF16))
    xt = e(nc.sbuf_tensor(pf + "xt", [128, 2 * 2048], F32))
    xn = e(nc.sbuf_tensor(pf + "xn", [128, 2048], F32))
    lnm = xn
    W2B2 = xn.bitcast(BF16)
    sg = e(nc.sbuf_tensor(pf + "sg", [128, 2 * 512], F32))
    hid = e(nc.sbuf_tensor(pf + "hid", [128, 2 * 512], BF16))
    mo = e(nc.sbuf_tensor(pf + "mo", [128, 2 * 512], F32))
    colsb = e(nc.sbuf_tensor(pf + "cols", [128, 16], F32))
    wst = FW1[:, 0:16384].bitcast(F32)
    wbo = FW1[:, 16384:24576]
    w2s = FW1[:, 24576:28672].bitcast(F32)
    w2c = FW1[:, 28672:30720]
    w2o = FW1[:, 30720:32768]
    FWALL = [pf + f"FW1_{g}" for g in range(64)]
    WAL = [pf + "wst", pf + "wbo", pf + "w2s", pf + "w2c", pf + "w2o"]
    mix = xt[:, 2048:4096]
    xr = xt[:, 0:2048]

    SC = 1.0 / float(np.sqrt(8192.0))

    for (wsrc, wdst, stw) in ((io["w1T"], io["s_w1tb"], "w1c"), (io["w3T"], io["s_w3tb"], "w3c")):
        for k in range(16):
            p.dma(pf + stw + "l", wst, wsrc[k * 128:(k + 1) * 128, :], writes=[pf + "wst"])
            p.op("V", lambda e_: e_.tensor_copy(wbo, wst),
                 reads=[pf + "wst"], writes=[pf + "wbo"])
            p.dma(pf + stw + "s", wdst[k * 128:(k + 1) * 128, :], wbo, reads=[pf + "wbo"],
                  writes=[pf + stw + "d"])
    w2hv = io["s_w2hgm"].rearrange("(g q) d -> q g d", q=128)
    for i in range(64):
        p.dma(pf + "w2l", w2s, io["w2pm"][i * 128:(i + 1) * 128, :], writes=[pf + "w2s"])
        p.op("V", lambda e_: e_.tensor_copy(w2c, w2s), reads=[pf + "w2s"], writes=[pf + "w2c"])
        for c in range(4):
            p.op("T", lambda e_, c=c: e_.matmul(
                psB[:, c * 512:(c + 1) * 512], BD64[:, :], w2c[:, c * 512:(c + 1) * 512],
                start=True, stop=True),
                reads=[pf + "w2c", "BD64"], writes=[psBb[c]], inc=False)
            p.op("S", lambda e_, c=c: e_.activation(
                w2o[:, c * 512:(c + 1) * 512], psB[:, c * 512:(c + 1) * 512], AF.Copy, scale=SC),
                reads=[psBb[c]], writes=[pf + "w2o"])
        p.dma(pf + "w2st", w2hv[2 * i:2 * i + 2, :, :], w2o,
              reads=[pf + "w2o"], writes=[pf + "w2d"])

    w1bv = io["s_w1tb"].rearrange("(k q) h -> q k h", q=128)
    w3bv = io["s_w3tb"].rearrange("(k q) h -> q k h", q=128)
    w2gv = io["s_w2hgm"].rearrange("(gg g8 q) d -> gg q g8 d", gg=8, g8=8, q=128)

    for b in range(B_PER_CORE):
        xin = io["xin"][b]
        xout = io["xout"][b]
        smix = io["s_mix"]
        for i in range(16):
            s = i % 2
            xw = [pf + f"xt{s}"] + ([pf + "xr"] if i == 0 else [pf + "mix"] if i == 1 else [])
            p.dma(pf + "xld" + str(s), sl2(xt, s), xin[i * 128:(i + 1) * 128, :],
                  writes=xw)
            ln_normalize(p, sl2(xt, s), xn[:, :], [pf + f"xt{s}"],
                         [pf + "xn", pf + "W2B2", pf + "lnm"],
                         colsb[:, s * 8:(s + 1) * 8], pf + f"cols{s}")
            for k in range(16):
                bank = k % 4
                p.op("T", lambda e_, k=k, bank=bank: e_.transpose(
                    psB[:, bank * 512:bank * 512 + 128],
                    xn[:, k * 128:(k + 1) * 128], identF[:, :]),
                    reads=[pf + "xn", "identF"], writes=[psBb[bank]], inc=False)
                p.op("S", lambda e_, k=k, bank=bank, i=i: e_.copy(
                    XFT[:, k * 2048 + i * 128:k * 2048 + i * 128 + 128],
                    psB[:, bank * 512:bank * 512 + 128]),
                    reads=[psBb[bank]], writes=[pf + f"XFT{k}"])
        for c in range(4):
            pend_st1 = None
            for hb in range(32):
                ws = hb % 2
                p.dma(pf + "w1bl" + str(ws), W1B[:, ws * 4096:(ws + 1) * 4096],
                      w1bv[:, :, hb * 256:(hb + 1) * 256],
                      reads=[pf + "w1cd"], writes=[pf + f"W1B{ws}"])
                p.dma(pf + "w3bl" + str(ws), W3B[:, ws * 4096:(ws + 1) * 4096],
                      w3bv[:, :, hb * 256:(hb + 1) * 256], engine="G",
                      reads=[pf + "w3cd"], writes=[pf + f"W3B{ws}"])
                for m in range(2):
                    sm = m % 2
                    pg, pl = (2 * m) % 4, (2 * m + 1) % 4
                    for k in range(16):
                        p.op("T", lambda e_, m=m, k=k, c=c, pg=pg, ws=ws: e_.matmul(
                            psA[:, pg * 512:(pg + 1) * 512],
                            W1B[:, ws * 4096 + k * 256 + m * 128:ws * 4096 + k * 256 + m * 128 + 128],
                            XFT[:, k * 2048 + c * 512:k * 2048 + (c + 1) * 512],
                            start=(k == 0), stop=(k == 15)),
                            reads=[pf + f"W1B{ws}", pf + f"XFT{k}"], writes=[psAb[pg]], inc=False)
                        p.op("T", lambda e_, m=m, k=k, c=c, pl=pl, ws=ws: e_.matmul(
                            psA[:, pl * 512:(pl + 1) * 512],
                            W3B[:, ws * 4096 + k * 256 + m * 128:ws * 4096 + k * 256 + m * 128 + 128],
                            XFT[:, k * 2048 + c * 512:k * 2048 + (c + 1) * 512],
                            start=(k == 0), stop=(k == 15)),
                            reads=[pf + f"W3B{ws}", pf + f"XFT{k}"], writes=[psAb[pl]], inc=False)
                    if pend_st1 is not None:
                        (pg_, pb_, g_, al_, sm_) = pend_st1
                        p.op("T", lambda e_, pb=pb_, sm=sm_: e_.matmul(
                            psB[:, pb * 512:(pb + 1) * 512], H128[:, :],
                            hid[:, sm * 512:(sm + 1) * 512],
                            start=True, stop=True),
                            reads=[pf + f"hid{sm_}", "H128"], writes=[psBb[pb_]], inc=False)
                        p.op("S", lambda e_, g=g_, pb=pb_: e_.copy(
                            FW1[:, g * 512:(g + 1) * 512], psB[:, pb * 512:(pb + 1) * 512]),
                            reads=[psBb[pb_]], writes=[pf + f"FW1_{g_}", al_])
                    p.op("S", lambda e_, pg=pg, sm=sm: e_.activation(
                        sg[:, sm * 512:(sm + 1) * 512], psA[:, pg * 512:(pg + 1) * 512], AF.Silu),
                        reads=[psAb[pg]], writes=[pf + f"sg{sm}"])
                    p.op("V", lambda e_, pl=pl, sm=sm: e_.tensor_tensor(
                        hid[:, sm * 512:(sm + 1) * 512], sg[:, sm * 512:(sm + 1) * 512],
                        psA[:, pl * 512:(pl + 1) * 512], ALU.mult),
                        reads=[pf + f"sg{sm}", psAb[pl]], writes=[pf + f"hid{sm}"])
                    g = hb * 2 + m
                    al = (pf + "wst" if g < 32 else pf + "wbo" if g < 48 else
                          pf + "w2s" if g < 56 else pf + "w2c" if g < 60 else pf + "w2o")
                    pend_st1 = (pg, (hb * 2 + m) % 4, g, al, sm)
            if pend_st1 is not None:
                (pg_, pb_, g_, al_, sm_) = pend_st1
                p.op("T", lambda e_, pb=pb_, sm=sm_: e_.matmul(
                    psB[:, pb * 512:(pb + 1) * 512], H128[:, :],
                    hid[:, sm * 512:(sm + 1) * 512], start=True, stop=True),
                    reads=[pf + f"hid{sm_}", "H128"], writes=[psBb[pb_]], inc=False)
                p.op("S", lambda e_, g=g_, pb=pb_: e_.copy(
                    FW1[:, g * 512:(g + 1) * 512], psB[:, pb * 512:(pb + 1) * 512]),
                    reads=[psBb[pb_]], writes=[pf + f"FW1_{g_}", al_])
                pend_st1 = None
            for dc in range(4):
                for gg in range(8):
                    ws = gg % 2
                    wap = W2B if ws == 0 else W2B2
                    wnm = [pf + "W2B"] if ws == 0 else [pf + "W2B2", pf + "xn", pf + "lnm"]
                    p.dma(pf + "w2bl" + str(ws), wap[:, :],
                          w2gv[gg][:, :, dc * 512:(dc + 1) * 512],
                          reads=[pf + "w2d"], writes=wnm)
                    for g8 in range(8):
                        g = gg * 8 + g8
                        for i4 in range(4):
                            p.op("T", lambda e_, g=g, g8=g8, i4=i4, c=c, wap=wap: e_.matmul(
                                psA[:, i4 * 512:(i4 + 1) * 512],
                                FW1[:, g * 512 + i4 * 128:g * 512 + i4 * 128 + 128],
                                wap[:, g8 * 512:(g8 + 1) * 512],
                                start=(g == 0), stop=(g == 63)),
                                reads=wnm + [pf + f"FW1_{g}"], writes=[psAb[i4]], inc=False)
                for i4 in range(4):
                    ms = i4 % 2
                    it = c * 4 + i4
                    p.op("S", lambda e_, i4=i4, ms=ms: e_.copy(
                        mo[:, ms * 512:(ms + 1) * 512], psA[:, i4 * 512:(i4 + 1) * 512]),
                        reads=[psAb[i4]], writes=[pf + f"mo{ms}"])
                    p.dma(pf + "most" + str(ms), smix[it * 128:(it + 1) * 128, dc * 512:(dc + 1) * 512],
                          mo[:, ms * 512:(ms + 1) * 512], reads=[pf + f"mo{ms}"],
                          writes=[pf + f"smx{it}_{dc}"])
            for i4 in range(4):
                it = c * 4 + i4
                p.dma(pf + "mxl", mix, smix[it * 128:(it + 1) * 128, :],
                      reads=[pf + f"smx{it}_{d_}" for d_ in range(4)],
                      writes=[pf + "mix", pf + "xt1"])
                p.dma(pf + "xrl", xr, xin[it * 128:(it + 1) * 128, :],
                      writes=[pf + "xr", pf + "xt0"])
                ln_normalize(p, mix, lnm[:, :], [pf + "mix", pf + "xt1"],
                             [pf + "lnm", pf + "xn", pf + "W2B2"], colsb[:, 0:8], pf + "cols0")
                p.op("V", lambda e_: e_.tensor_tensor(mix, xr, lnm[:, :], ALU.add),
                     reads=[pf + "xr", pf + "xt0", pf + "lnm", pf + "xn"],
                     writes=[pf + "mix", pf + "xt1"])
                p.dma(pf + "xst", xout[it * 128:(it + 1) * 128, :], mix,
                      reads=[pf + "mix", pf + "xt1"], writes=[pf + f"xo{b}_{it}"])
    return [pf + "xst", pf + "most0", pf + "most1"]


def mk_l0ffn_program():
    nc = bass.Bass("TRN2", target_bir_lowering=False)
    io = {}
    io["w1T"] = nc.dram_tensor("w1T", [2048, 8192], F32, kind="ExternalInput").ap()
    io["w3T"] = nc.dram_tensor("w3T", [2048, 8192], F32, kind="ExternalInput").ap()
    io["w2pm"] = nc.dram_tensor("w2pm", [8192, 2048], F32, kind="ExternalInput").ap()
    io["identF"] = nc.dram_tensor("identF", [128, 128], F32, kind="ExternalInput").ap()
    io["H128"] = nc.dram_tensor("H128", [128, 128], BF16, kind="ExternalInput").ap()
    io["BD64"] = nc.dram_tensor("BD64", [128, 128], BF16, kind="ExternalInput").ap()
    io["s_w1tb"] = nc.dram_tensor("s_w1tb", [2048, 8192], BF16, kind="Internal").ap()
    io["s_w3tb"] = nc.dram_tensor("s_w3tb", [2048, 8192], BF16, kind="Internal").ap()
    io["s_w2hgm"] = nc.dram_tensor("s_w2hgm", [8192, 2048], BF16, kind="Internal").ap()
    io["s_mix"] = nc.dram_tensor("s_mix", [2048, 2048], F32, kind="Internal").ap()
    xin = nc.dram_tensor("xin", [B_PER_CORE, 2048, 2048], F32, kind="ExternalInput").ap()
    xout = nc.dram_tensor("xout", [B_PER_CORE, 2048, 2048], F32, kind="ExternalOutput").ap()
    io["xin"] = [xin[b] for b in range(B_PER_CORE)]
    io["xout"] = [xout[b] for b in range(B_PER_CORE)]
    with ExitStack() as st:
        p = Prog(nc)
        streams = build_l0ffn(p, st, io)
        p.finish(streams)
        p.emit(st)
    return nc


def _hadamard(n):
    h = np.array([[1.0]])
    while h.shape[0] < n:
        h = np.block([[h, h], [h, -h]])
    return h


def host_consts_l0ffn():
    H128h = _hadamard(128).astype(ml_dtypes.bfloat16)
    H64h = _hadamard(64)
    BD = np.zeros((128, 128), dtype=np.float64)
    BD[0:64, 0:64] = H64h
    BD[64:128, 64:128] = H64h
    return {
        "identF": np.eye(128, dtype=np.float32),
        "H128": H128h,
        "BD64": BD.astype(ml_dtypes.bfloat16),
    }



def build_l1attn(p, st, io, pf="a1"):
    nc = p.nc
    e = st.enter_context
    SQT = 1.0 / float(np.sqrt(2048.0))

    identF = e(nc.sbuf_tensor(pf + "identF", [128, 128], F32))
    masks = e(nc.sbuf_tensor(pf + "masks", [128, 4 * 512], BF16))
    onesc = e(nc.sbuf_tensor(pf + "onesc", [128, 1], BF16))
    onesr = e(nc.sbuf_tensor(pf + "onesr", [1, 512], BF16))
    p.dma(pf + "cid", identF[:, :], io["identF"], writes=["identF"])
    p.dma(pf + "cmk", masks[:, :], io["masks"], writes=["masks"])
    p.dma(pf + "co1", onesc[:, :], io["ones128"], writes=["onesc"])
    p.dma(pf + "co2", onesr[:, :], io["ones1x512"], writes=["onesr"])

    psA = e(nc.psum_tensor(pf + "psA", [128, 2048], F32))
    psB = e(nc.psum_tensor(pf + "psB", [128, 2048], F32))
    psAb = ["psA0", "psA1", "psA2", "psA3"]
    psBb = ["psB0", "psB1", "psB2", "psB3"]

    XHT = e(nc.sbuf_tensor(pf + "XHT", [128, 16 * 2048], BF16))
    WQB = e(nc.sbuf_tensor(pf + "WQB", [128, 16 * 256], BF16))
    WKB = e(nc.sbuf_tensor(pf + "WKB", [128, 16 * 256], BF16))
    WVB = e(nc.sbuf_tensor(pf + "WVB", [128, 16 * 256], BF16))
    QTG = e(nc.sbuf_tensor(pf + "QTG", [128, 2 * 2048], BF16))
    KTG = e(nc.sbuf_tensor(pf + "KTG", [128, 2 * 2048], BF16))
    VG = e(nc.sbuf_tensor(pf + "VG", [128, 16 * 256], BF16))
    EXP2 = e(nc.sbuf_tensor(pf + "EXP", [128, 2 * 16 * 512], BF16))
    oth = e(nc.sbuf_tensor(pf + "oth", [128, 2048], BF16))
    zr = e(nc.sbuf_tensor(pf + "zr", [128, 512], BF16))
    zrowf = e(nc.sbuf_tensor(pf + "zrowf", [1, 512], F32))
    zrow = e(nc.sbuf_tensor(pf + "zrow", [1, 512], BF16))
    sufr = e(nc.sbuf_tensor(pf + "sufr", [1, 6 * 128], BF16))
    xt = e(nc.sbuf_tensor(pf + "xt", [128, 2 * 2048], F32))
    colsb = e(nc.sbuf_tensor(pf + "cols", [128, 16], F32))
    xn = WVB[:, 0:4096].bitcast(F32)
    mo = EXP2[:, 0:1024].bitcast(F32)
    lnm = xn
    mix = xt[:, 2048:4096]
    xr = xt[:, 0:2048]
    wstage = XHT[:, 0:4096].bitcast(F32)
    wsbf = XHT[:, 4096:6144]
    LHS = KTG[:, 0:2048]
    LHS2 = KTG[:, 2048:4096]
    WOB = XHT[:, 0:8192]
    WOB2 = XHT[:, 8192:16384]
    XAL = [pf + "XHT0", pf + "XHT1", pf + "XHT2", pf + "XHT3",
           pf + "wstage", pf + "wsbf"]

    for (wsrc, wdst, stw) in ((io["Wq"], io["s_wqb"], "q"), (io["Wk"], io["s_wkb"], "k"),
                              (io["Wv"], io["s_wvb"], "v"), (io["Wo"], io["s_wob"], "o")):
        for k in range(16):
            p.dma(pf + "wc" + stw, wstage, wsrc[k * 128:(k + 1) * 128, :],
                  writes=[pf + "wstage"])
            p.op("V", lambda e_: e_.tensor_copy(wsbf, wstage),
                 reads=[pf + "wstage"], writes=[pf + "wsbf"])
            p.dma(pf + "ws" + stw, wdst[k * 128:(k + 1) * 128, :], wsbf,
                  reads=[pf + "wsbf"], writes=[pf + "wcd" + stw])

    wqv = io["s_wqb"].rearrange("(k q) o -> q k o", q=128)
    wkv = io["s_wkb"].rearrange("(k q) o -> q k o", q=128)
    wvv = io["s_wvb"].rearrange("(k q) o -> q k o", q=128)
    wov = io["s_wob"].rearrange("(h q) o -> q h o", q=128)
    soutv = io["s_outT"]

    for b in range(B_PER_CORE):
        xin = io["xin"][b]
        xout = io["xout"][b]
        smix = io["s_mix"]
        for i in range(16):
            s = i % 2
            xw = [pf + f"xt{s}"] + ([pf + "xr"] if i == 0 else [pf + "mix"] if i == 1 else [])
            p.dma(pf + "xld" + str(s), sl2(xt, s), xin[i * 128:(i + 1) * 128, :], writes=xw)
            ln_normalize(p, sl2(xt, s), xn[:, :], [pf + f"xt{s}"], [pf + "xn", pf + "WVB"],
                         colsb[:, s * 8:(s + 1) * 8], pf + f"cols{s}")
            for k in range(16):
                bank = k % 4
                xwr = [pf + f"XHT{k}"] + ([pf + "wstage", pf + "wsbf"] if k < 3 and i == 0 else [])
                p.op("T", lambda e_, k=k, bank=bank: e_.transpose(
                    psB[:, bank * 512:bank * 512 + 128],
                    xn[:, k * 128:(k + 1) * 128], identF[:, :]),
                    reads=[pf + "xn", "identF"], writes=[psBb[bank]], inc=False)
                p.op("S", lambda e_, k=k, bank=bank, i=i: e_.copy(
                    XHT[:, k * 2048 + i * 128:k * 2048 + i * 128 + 128],
                    psB[:, bank * 512:bank * 512 + 128]),
                    reads=[psBb[bank]], writes=xwr)
        for g2 in range(8):
            co = g2 * 256
            p.dma(pf + "wqbl", WQB[:, :], wqv[:, :, co:co + 256],
                  reads=[pf + "wcdq"], writes=[pf + "WQB"])
            p.dma(pf + "wkbl", WKB[:, :], wkv[:, :, co:co + 256],
                  reads=[pf + "wcdk"], writes=[pf + "WKB"])
            p.dma(pf + "wvbl", WVB[:, :], wvv[:, :, co:co + 256],
                  reads=[pf + "wcdv"], writes=[pf + "WVB", pf + "xn", pf + "lnm"])
            for l in range(2):
                for tc in range(4):
                    pb = (4 * l + tc) % 4
                    for k in range(16):
                        p.op("T", lambda e_, l=l, tc=tc, k=k, pb=pb: e_.matmul(
                            psA[:, pb * 512:(pb + 1) * 512],
                            WQB[:, k * 256 + l * 128:k * 256 + l * 128 + 128],
                            XHT[:, k * 2048 + tc * 512:k * 2048 + (tc + 1) * 512],
                            start=(k == 0), stop=(k == 15)),
                            reads=[pf + "WQB", pf + f"XHT{k}"], writes=[psAb[pb]], inc=False)
                    p.op("S", lambda e_, l=l, tc=tc, pb=pb: e_.copy(
                        QTG[:, l * 2048 + tc * 512:l * 2048 + (tc + 1) * 512],
                        psA[:, pb * 512:(pb + 1) * 512]),
                        reads=[psAb[pb]], writes=[pf + f"QTG{l}"])
                    for k in range(16):
                        p.op("T", lambda e_, l=l, tc=tc, k=k, pb=pb: e_.matmul(
                            psB[:, pb * 512:(pb + 1) * 512],
                            WKB[:, k * 256 + l * 128:k * 256 + l * 128 + 128],
                            XHT[:, k * 2048 + tc * 512:k * 2048 + (tc + 1) * 512],
                            start=(k == 0), stop=(k == 15)),
                            reads=[pf + "WKB", pf + f"XHT{k}"], writes=[psBb[pb]], inc=False)
                    p.op("S", lambda e_, l=l, tc=tc, pb=pb: e_.copy(
                        KTG[:, l * 2048 + tc * 512:l * 2048 + (tc + 1) * 512],
                        psB[:, pb * 512:(pb + 1) * 512]),
                        reads=[psBb[pb]], writes=[pf + f"KTG{l}", pf + "LHS"])
            for i in range(16):
                pb = i % 4
                for k in range(16):
                    p.op("T", lambda e_, i=i, k=k, pb=pb: e_.matmul(
                        psA[:, pb * 512:pb * 512 + 256],
                        XHT[:, k * 2048 + i * 128:k * 2048 + i * 128 + 128],
                        WVB[:, k * 256:(k + 1) * 256],
                        start=(k == 0), stop=(k == 15)),
                        reads=[pf + "WVB", pf + f"XHT{k}"], writes=[psAb[pb]], inc=False)
                p.op("S", lambda e_, i=i, pb=pb: e_.copy(
                    VG[:, i * 256:(i + 1) * 256], psA[:, pb * 512:pb * 512 + 256]),
                    reads=[psAb[pb]], writes=[pf + f"VG{i}"])
            for l in range(2):
                h = g2 * 2 + l
                EXP = EXP2[:, l * 8192:(l + 1) * 8192]
                EB = pf + f"E{l}_"
                qT = QTG[:, l * 2048:(l + 1) * 2048]
                kT = KTG[:, l * 2048:(l + 1) * 2048]
                for cc in range(3):
                    sbs = list(range(4 * cc + 4, 16))
                    for j, sb in enumerate(sbs):
                        p.op("T", lambda e_, sb=sb, cc=cc, j=j, n=len(sbs), l=l: e_.matmul(
                            psB[0:1, (1 + cc) * 512:(1 + cc) * 512 + 128], onesc[:, :],
                            VG[:, sb * 256 + l * 128:sb * 256 + l * 128 + 128],
                            start=(j == 0), stop=(j == n - 1)),
                            reads=[pf + f"VG{sb}", "onesc"], writes=[psBb[1 + cc]], inc=False)
                    p.op("S", lambda e_, cc=cc, l=l: e_.copy(
                        sufr[:, (l * 3 + cc) * 128:(l * 3 + cc + 1) * 128],
                        psB[0:1, (1 + cc) * 512:(1 + cc) * 512 + 128]),
                        reads=[psBb[1 + cc]], writes=[pf + "sufr"])
                for c in range(4):
                    smax = 4 * c + 3
                    osl = c % 2
                    for sb in range(smax + 1):
                        ssl = 2 + (sb % 2)
                        p.op("T", lambda e_, sb=sb, c=c, ssl=ssl, kT=kT, qT=qT: e_.matmul(
                            psA[:, ssl * 512:(ssl + 1) * 512],
                            kT[:, sb * 128:(sb + 1) * 128],
                            qT[:, c * 512:(c + 1) * 512], start=True, stop=True),
                            reads=[pf + f"KTG{l}", pf + f"QTG{l}"], writes=[psAb[ssl]], inc=False)
                        off = sb - 4 * c
                        if off >= 0:
                            p.op("V", lambda e_, off=off, ssl=ssl: e_.tensor_tensor(
                                psA[:, ssl * 512:(ssl + 1) * 512],
                                psA[:, ssl * 512:(ssl + 1) * 512],
                                masks[:, off * 512:(off + 1) * 512], ALU.mult),
                                reads=[psAb[ssl], "masks"], writes=[psAb[ssl]])
                        p.op("S", lambda e_, sb=sb, ssl=ssl, EXP=EXP: e_.activation(
                            EXP[:, sb * 512:(sb + 1) * 512],
                            psA[:, ssl * 512:(ssl + 1) * 512], AF.Exp, scale=SQT),
                            reads=[psAb[ssl]],
                            writes=[EB + f"EXP{sb}"] + ([pf + "mo0"] if sb < 2 and l == 0 else []))
                    for sb in range(smax + 1):
                        p.op("T", lambda e_, sb=sb, smax=smax, EXP=EXP: e_.matmul(
                            psB[0:1, 0:512], onesc[:, :], EXP[:, sb * 512:(sb + 1) * 512],
                            start=(sb == 0), stop=(sb == smax)),
                            reads=[EB + f"EXP{sb}", "onesc"], writes=[psBb[0]], inc=False)
                    cnt = float(2048 - 512 * (c + 1))
                    p.op("V", lambda e_, cnt=cnt: e_.tensor_scalar_add(
                        zrowf[0:1, :], psB[0:1, 0:512], cnt),
                        reads=[psBb[0]], writes=[pf + "zrowf"])
                    p.op("S", lambda e_: e_.activation(zrowf[0:1, :], zrowf[0:1, :], AF.Ln),
                         reads=[pf + "zrowf"], writes=[pf + "zrowf"])
                    p.op("S", lambda e_: e_.activation(zrowf[0:1, :], zrowf[0:1, :], AF.Exp, scale=-1.0),
                         reads=[pf + "zrowf"], writes=[pf + "zrowf"])
                    p.op("V", lambda e_: e_.tensor_copy(zrow[0:1, :], zrowf[0:1, :]),
                         reads=[pf + "zrowf"], writes=[pf + "zrow"])
                    for sb in range(smax + 1):
                        p.op("T", lambda e_, sb=sb, c=c, osl=osl, l=l, smax=smax, EXP=EXP: e_.matmul(
                            psA[:, osl * 512:(osl + 1) * 512],
                            VG[:, sb * 256 + l * 128:sb * 256 + l * 128 + 128],
                            EXP[:, sb * 512:(sb + 1) * 512],
                            start=(sb == 0), stop=(sb == smax and c == 3)),
                            reads=[pf + f"VG{sb}", EB + f"EXP{sb}"], writes=[psAb[osl]], inc=False)
                    if c < 3:
                        p.op("T", lambda e_, c=c, osl=osl, l=l: e_.matmul(
                            psA[:, osl * 512:(osl + 1) * 512],
                            sufr[0:1, (l * 3 + c) * 128:(l * 3 + c + 1) * 128],
                            onesr[:, :], start=False, stop=True),
                            reads=[pf + "sufr", "onesr"], writes=[psAb[osl]], inc=False)
                    p.op("T", lambda e_: e_.matmul(
                        psB[:, 1 * 512:2 * 512], onesr[0:1, 0:128], zrow[0:1, :],
                        start=True, stop=True),
                        reads=[pf + "zrow", "onesr"], writes=[psBb[1]], inc=False)
                    p.op("S", lambda e_: e_.copy(zr[:, :], psB[:, 1 * 512:2 * 512]),
                         reads=[psBb[1]], writes=[pf + "zr"])
                    p.op("V", lambda e_, c=c, osl=osl: e_.tensor_tensor(
                        oth[:, c * 512:(c + 1) * 512], psA[:, osl * 512:(osl + 1) * 512],
                        zr[:, :], ALU.mult),
                        reads=[psAb[osl], pf + "zr"], writes=[pf + "oth"])
                p.dma(pf + "ost", soutv[:, :, h, :].transpose([1, 0, 2]), oth[:, :],
                      reads=[pf + "oth"], writes=[pf + f"soth{h}"])
        for dc in range(4):
            ws = dc % 2
            wap = WOB if ws == 0 else WOB2
            wnm = ([pf + "WOB", pf + "XHT0", pf + "XHT1", pf + "XHT2", pf + "XHT3",
                    pf + "wstage", pf + "wsbf"] if ws == 0 else
                   [pf + "WOB2", pf + "XHT4", pf + "XHT5", pf + "XHT6", pf + "XHT7"])
            p.dma(pf + "wobl" + str(ws), wap[:, :], wov[:, :, dc * 512:(dc + 1) * 512],
                  reads=[pf + "wcdo"] + ([pf + f"soth{h}" for h in range(16)] if dc < 2 else []),
                  writes=wnm)
            for i in range(16):
                ls = i % 2
                lap = LHS if ls == 0 else LHS2
                lnmm = [pf + "LHS", pf + "KTG0"] if ls == 0 else [pf + "LHS2", pf + "KTG1"]
                p.dma(pf + "lhl" + str(ls), lap[:, :], soutv[i],
                      reads=[pf + f"soth{h}" for h in range(16)],
                      writes=lnmm)
                pb = i % 4
                for h in range(16):
                    p.op("T", lambda e_, h=h, pb=pb, lap=lap, wap=wap: e_.matmul(
                        psA[:, pb * 512:(pb + 1) * 512], lap[:, h * 128:(h + 1) * 128],
                        wap[:, h * 512:(h + 1) * 512], start=(h == 0), stop=(h == 15)),
                        reads=lnmm + wnm, writes=[psAb[pb]], inc=False)
                p.op("S", lambda e_, pb=pb: e_.copy(
                    mo[:, :], psA[:, pb * 512:(pb + 1) * 512]),
                    reads=[psAb[pb]], writes=[pf + "mo0", pf + "E0_EXP0", pf + "E0_EXP1"])
                p.dma(pf + "most0", smix[i * 128:(i + 1) * 128, dc * 512:(dc + 1) * 512],
                      mo[:, :], reads=[pf + "mo0"],
                      writes=[pf + f"smx{i}_{dc}"])
        for i in range(16):
            p.dma(pf + "mxl", mix, smix[i * 128:(i + 1) * 128, :],
                  reads=[pf + f"smx{i}_{d_}" for d_ in range(4)],
                  writes=[pf + "mix", pf + "xt1"])
            p.dma(pf + "xrl", xr, xin[i * 128:(i + 1) * 128, :],
                  writes=[pf + "xr", pf + "xt0"])
            ln_normalize(p, mix, lnm[:, :], [pf + "mix", pf + "xt1"],
                         [pf + "lnm", pf + "xn", pf + "WVB"], colsb[:, 0:8], pf + "cols0")
            p.op("V", lambda e_: e_.tensor_tensor(mix, xr, lnm[:, :], ALU.add),
                 reads=[pf + "xr", pf + "xt0", pf + "lnm", pf + "xn"],
                 writes=[pf + "mix", pf + "xt1"])
            p.dma(pf + "xst", xout[i * 128:(i + 1) * 128, :], mix,
                  reads=[pf + "mix", pf + "xt1"], writes=[pf + f"xo{b}_{i}"])
    return [pf + "xst", pf + "most0", pf + "ost"]


def mk_l1attn_program():
    nc = bass.Bass("TRN2", target_bir_lowering=False)
    io = {}
    for nm in ("Wq", "Wk", "Wv", "Wo"):
        io[nm] = nc.dram_tensor(nm, [2048, 2048], F32, kind="ExternalInput").ap()
    io["identF"] = nc.dram_tensor("identF", [128, 128], F32, kind="ExternalInput").ap()
    io["masks"] = nc.dram_tensor("masks", [128, 4 * 512], BF16, kind="ExternalInput").ap()
    io["ones128"] = nc.dram_tensor("ones128", [128, 1], BF16, kind="ExternalInput").ap()
    io["ones1x512"] = nc.dram_tensor("ones1x512", [1, 512], BF16, kind="ExternalInput").ap()
    for nm in ("s_wqb", "s_wkb", "s_wvb", "s_wob"):
        io[nm] = nc.dram_tensor(nm, [2048, 2048], BF16, kind="Internal").ap()
    io["s_outT"] = nc.dram_tensor("s_outT", [16, 128, 16, 128], BF16, kind="Internal").ap()
    io["s_mix"] = nc.dram_tensor("s_mix", [2048, 2048], F32, kind="Internal").ap()
    xin = nc.dram_tensor("xin", [B_PER_CORE, 2048, 2048], F32, kind="ExternalInput").ap()
    xout = nc.dram_tensor("xout", [B_PER_CORE, 2048, 2048], F32, kind="ExternalOutput").ap()
    io["xin"] = [xin[b] for b in range(B_PER_CORE)]
    io["xout"] = [xout[b] for b in range(B_PER_CORE)]
    with ExitStack() as st:
        p = Prog(nc)
        streams = build_l1attn(p, st, io)
        p.finish(streams)
        p.emit(st)
    return nc


def host_consts_l1attn():
    m = np.zeros((128, 4 * 512), dtype=np.float32)
    for off in range(4):
        s = np.arange(128)[:, None]
        t = np.arange(512)[None, :]
        m[:, off * 512:(off + 1) * 512] = (s + 128 * off <= t).astype(np.float32)
    return {
        "identF": np.eye(128, dtype=np.float32),
        "masks": m.astype(ml_dtypes.bfloat16),
        "ones128": np.ones((128, 1), dtype=ml_dtypes.bfloat16),
        "ones1x512": np.ones((1, 512), dtype=ml_dtypes.bfloat16),
    }



def build_l0attn(p, st, io, pf="a0"):
    nc = p.nc
    e = st.enter_context

    identF = e(nc.sbuf_tensor(pf + "identF", [128, 128], F32))
    sel64 = e(nc.sbuf_tensor(pf + "sel64", [64, 128], BF16))
    EJT = e(nc.sbuf_tensor(pf + "EJT", [128, 64 * 128], BF16))
    YST = e(nc.sbuf_tensor(pf + "YST", [128, 256], BF16))
    p.dma(pf + "cid", identF[:, :], io["identF"], writes=["identF"])
    p.dma(pf + "cs64", sel64[:, :], io["sel64"], writes=["sel64"])
    p.dma(pf + "cej", EJT[:, :], io["EJT"], writes=["EJT"])
    p.dma(pf + "cys", YST[:, :], io["YST"], writes=["YST"])

    psA = e(nc.psum_tensor(pf + "psA", [128, 2048], F32))
    psB = e(nc.psum_tensor(pf + "psB", [128, 2048], F32))
    psAb = ["psA0", "psA1", "psA2", "psA3"]
    psBb = ["psB0", "psB1", "psB2", "psB3"]

    XT = e(nc.sbuf_tensor(pf + "XT", [128, 16 * 2048], BF16))
    LT = e(nc.sbuf_tensor(pf + "LT", [128, 16 * 2048], BF16))
    dA = e(nc.sbuf_tensor(pf + "dA", [128, 2048], F32))
    ub = e(nc.sbuf_tensor(pf + "ub", [128, 2048], BF16))
    hh = e(nc.sbuf_tensor(pf + "hh", [128, 2048], BF16))
    DTK = e(nc.sbuf_tensor(pf + "DTK", [128, 2048], BF16))
    BREP = e(nc.sbuf_tensor(pf + "BREP", [128, 2048], BF16))
    CREP = e(nc.sbuf_tensor(pf + "CREP", [128, 512], BF16))
    ALC = e(nc.sbuf_tensor(pf + "ALC", [128, 1024], F32))
    WBB = e(nc.sbuf_tensor(pf + "WBB", [128, 2 * 2048], BF16))
    DCOL = e(nc.sbuf_tensor(pf + "DCOL", [128, 32], F32))
    xt = e(nc.sbuf_tensor(pf + "xt", [128, 2 * 2048], F32))
    DPJ = e(nc.sbuf_tensor(pf + "DPJ", [128, 2048], BF16))
    DRT = e(nc.sbuf_tensor(pf + "DRT", [128, 2048], BF16))
    CT = e(nc.sbuf_tensor(pf + "CT", [64, 512], BF16))
    colsb = e(nc.sbuf_tensor(pf + "cols", [128, 16], F32))
    BT = DTK[0:64, :]
    XPJ = hh
    xn = dA[:, 0:2048]
    stg = dA[:, 0:2048]
    OPB = dA.bitcast(BF16)
    mo = ub.bitcast(F32)
    mix = xt[:, 2048:4096]
    xr = xt[:, 0:2048]
    RDA = [pf + "dA", pf + "xn", pf + "OPB"]
    RXT0 = [pf + "xt0", pf + "xr"]
    RXT1 = [pf + "xt1", pf + "mix"]
    RUB = [pf + "ub", pf + "mo"]

    for b in range(B_PER_CORE):
        xin = io["xin"][b]
        xout = io["xout"][b]
        smix = io["s_mix"]
        p.dma(pf + "cwb", WBB[:, 0:2048],
              io["norm_w"].unsqueeze(0).partition_broadcast(128), engine="G",
              writes=[pf + "WBB", pf + "WBBu"])
        p.dma(pf + "cbb", WBB[:, 2048:4096],
              io["norm_b"].unsqueeze(0).partition_broadcast(128), engine="G",
              writes=[pf + "WBB", pf + "WBBh"])
        if b == 0:
            p.dma(pf + "cdc", DCOL[:, 0:16], io["dcols"], writes=[pf + "DCOL"])
            p.dma(pf + "cbc", DCOL[:, 16:32], io["bcols"], writes=[pf + "DCOL"])
            p.dma(pf + "cal", ALC[:, :], io["alogcols"], writes=[pf + "ALC"])
            p.op("S", lambda e_: e_.activation(ALC[:, :], ALC[:, :], AF.Exp),
                 reads=[pf + "ALC"], writes=[pf + "ALC"])
            p.op("V", lambda e_: e_.tensor_scalar_mul(ALC[:, :], ALC[:, :], -1.0),
                 reads=[pf + "ALC"], writes=[pf + "ALC"])
            for k in range(16):
                p.dma(pf + "opl", stg[:, :], io["oprojT"][k * 128:(k + 1) * 128, :],
                      reads=RDA, writes=RDA)
                p.op("V", lambda e_: e_.tensor_copy(ub[:, :], stg[:, :]),
                     reads=RDA, writes=RUB)
                p.dma(pf + "ops", io["s_oprojb"][k * 128:(k + 1) * 128, :], ub[:, :],
                      reads=RUB, writes=[pf + "opd"])
        for i in range(16):
            s = 0
            p.dma(pf + "xld0", sl2(xt, 0), xin[i * 128:(i + 1) * 128, :], writes=RXT0)
            ln_normalize(p, sl2(xt, 0), xn[:, :], RXT0, RDA,
                         colsb[:, 0:8], pf + "cols0")
            for k in range(16):
                bank = k % 4
                p.op("T", lambda e_, k=k, bank=bank: e_.transpose(
                    psB[:, bank * 512:bank * 512 + 128],
                    xn[:, k * 128:(k + 1) * 128], identF[:, :]),
                    reads=RDA + ["identF"], writes=[psBb[bank]], inc=False)
                p.op("S", lambda e_, k=k, bank=bank, i=i: e_.activation(
                    LT[:, k * 2048 + i * 128:k * 2048 + i * 128 + 128],
                    psB[:, bank * 512:bank * 512 + 128], AF.Identity,
                    scale=DCOL[:, k:k + 1]),
                    reads=[psBb[bank], pf + "DCOL"], writes=[pf + f"LT{k}"])
            ln_normalize(p, xn[:, :], xt[:, 2048:4096], RDA, RXT1,
                         colsb[:, 8:16], pf + "cols1")
            p.op("V", lambda e_: e_.tensor_tensor(xt[:, 2048:4096], xt[:, 2048:4096], WBB[:, 0:2048], ALU.mult),
                 reads=RXT1 + [pf + "WBB"], writes=RXT1)
            p.op("V", lambda e_: e_.tensor_tensor(xt[:, 2048:4096], xt[:, 2048:4096], WBB[:, 2048:4096], ALU.add),
                 reads=RXT1 + [pf + "WBB"], writes=RXT1)
            for k in range(16):
                bank = k % 4
                p.op("T", lambda e_, k=k, bank=bank: e_.transpose(
                    psB[:, bank * 512:bank * 512 + 128],
                    xt[:, 2048 + k * 128:2048 + (k + 1) * 128], identF[:, :]),
                    reads=RXT1 + ["identF"], writes=[psBb[bank]], inc=False)
                p.op("S", lambda e_, k=k, bank=bank, i=i: e_.copy(
                    XT[:, k * 2048 + i * 128:k * 2048 + i * 128 + 128],
                    psB[:, bank * 512:bank * 512 + 128]),
                    reads=[psBb[bank]], writes=[pf + f"XT{k}"])
        xpv = io["xprojT"].rearrange("(k q) o -> q k o", q=128)
        for half in range(2):
            p.dma(pf + "xpl", stg[:, 0:2048], xpv[:, half * 8:(half + 1) * 8, :],
                  reads=RDA, writes=RDA)
            p.op("V", lambda e_, half=half: e_.tensor_copy(XPJ[:, :], stg[:, 0:2048]),
                 reads=RDA, writes=[pf + "XPJ", pf + "hh"])
            for tc in range(4):
                for kk in range(8):
                    k = half * 8 + kk
                    p.op("T", lambda e_, tc=tc, k=k, kk=kk: e_.matmul(
                        psB[:, tc * 512:(tc + 1) * 512],
                        XPJ[:, kk * 256:kk * 256 + 128],
                        XT[:, k * 2048 + tc * 512:k * 2048 + (tc + 1) * 512],
                        start=(k == 0), stop=(k == 15)),
                        reads=[pf + "XPJ", pf + "hh", pf + f"XT{k}"], writes=[psBb[tc]], inc=False)
        for tc in range(4):
            p.op("S", lambda e_, tc=tc: e_.copy(
                DRT[:, tc * 512:(tc + 1) * 512], psB[:, tc * 512:(tc + 1) * 512]),
                reads=[psBb[tc]], writes=[pf + "DRT"])
        for half in range(2):
            p.dma(pf + "xpl", stg[:, 0:2048], xpv[:, half * 8:(half + 1) * 8, :],
                  reads=RDA, writes=RDA)
            p.op("V", lambda e_, half=half: e_.tensor_copy(XPJ[:, :], stg[:, 0:2048]),
                 reads=RDA, writes=[pf + "XPJ", pf + "hh"])
            for tc in range(4):
                for kk in range(8):
                    k = half * 8 + kk
                    p.op("T", lambda e_, tc=tc, k=k, kk=kk: e_.matmul(
                        psB[0:64, tc * 512:(tc + 1) * 512],
                        XPJ[:, kk * 256 + 128:kk * 256 + 192],
                        XT[:, k * 2048 + tc * 512:k * 2048 + (tc + 1) * 512],
                        start=(k == 0), stop=(k == 15)),
                        reads=[pf + "XPJ", pf + "hh", pf + f"XT{k}"], writes=[psBb[tc]], inc=False)
        for tc in range(4):
            p.op("S", lambda e_, tc=tc: e_.copy(
                BT[:, tc * 512:(tc + 1) * 512], psB[0:64, tc * 512:(tc + 1) * 512]),
                reads=[psBb[tc]], writes=[pf + "BT", pf + "DTK"])
        for half in range(2):
            p.dma(pf + "xpl", stg[:, 0:2048], xpv[:, half * 8:(half + 1) * 8, :],
                  reads=RDA, writes=RDA)
            p.op("V", lambda e_, half=half: e_.tensor_copy(XPJ[:, :], stg[:, 0:2048]),
                 reads=RDA, writes=[pf + "XPJ", pf + "hh"])
            for kk in range(8):
                k = half * 8 + kk
                p.op("T", lambda e_, k=k, kk=kk: e_.matmul(
                    psB[0:64, 0:512],
                    XPJ[:, kk * 256 + 192:kk * 256 + 256],
                    XT[:, k * 2048 + 0:k * 2048 + 512],
                    start=(k == 0), stop=(k == 15)),
                    reads=[pf + "XPJ", pf + "hh", pf + f"XT{k}"], writes=[psBb[0]], inc=False)
        p.op("S", lambda e_: e_.copy(CT[:, :], psB[0:64, 0:512]),
             reads=[psBb[0]], writes=[pf + "CT"])
        for tc in range(4):
            p.op("T", lambda e_, tc=tc: e_.matmul(
                psB[:, tc * 512:(tc + 1) * 512], sel64[:, :],
                BT[:, tc * 512:(tc + 1) * 512], start=True, stop=True),
                reads=[pf + "BT", pf + "DTK", "sel64"], writes=[psBb[tc]], inc=False)
            p.op("S", lambda e_, tc=tc: e_.copy(
                BREP[:, tc * 512:(tc + 1) * 512], psB[:, tc * 512:(tc + 1) * 512]),
                reads=[psBb[tc]], writes=[pf + "BREP"])
        p.op("T", lambda e_: e_.matmul(psA[:, 0:512], sel64[:, :], CT[:, :],
                                       start=True, stop=True),
             reads=[pf + "CT", "sel64"], writes=[psAb[0], pf + "psAc"], inc=False)
        p.op("S", lambda e_: e_.copy(CREP[:, :], psA[:, 0:512]),
             reads=[psAb[0]], writes=[pf + "CREP"])
        p.dma(pf + "dpl", stg[:, :], io["dtprojT"], reads=RDA, writes=RDA)
        p.op("V", lambda e_: e_.tensor_copy(DPJ[:, :], stg[:, :]),
             reads=RDA, writes=[pf + "DPJ"])
        for k in range(16):
            for c in range(4):
                p.op("T", lambda e_, k=k, c=c: e_.matmul(
                    psA[:, c * 512:(c + 1) * 512],
                    DPJ[:, k * 128:(k + 1) * 128],
                    DRT[:, c * 512:(c + 1) * 512], start=True, stop=True),
                    reads=[pf + "DPJ", pf + "DRT"], writes=[psAb[c]], inc=False)
                p.op("V", lambda e_, k=k, c=c: e_.tensor_scalar(
                    psB[:, c * 512:(c + 1) * 512], psA[:, c * 512:(c + 1) * 512],
                    DCOL[:, 16 + k:17 + k], None, ALU.add),
                    reads=[psAb[c], pf + "DCOL"], writes=[psBb[c]])
                p.op("S", lambda e_, c=c: e_.activation(
                    psA[:, c * 512:(c + 1) * 512], psB[:, c * 512:(c + 1) * 512], AF.Abs),
                    reads=[psBb[c]], writes=[psAb[c]])
                p.op("S", lambda e_, c=c: e_.activation(
                    dA[:, c * 512:(c + 1) * 512], psA[:, c * 512:(c + 1) * 512],
                    AF.Exp, scale=-1.0),
                    reads=[psAb[c]], writes=RDA)
                p.op("V", lambda e_, c=c: e_.tensor_scalar_add(
                    dA[:, c * 512:(c + 1) * 512], dA[:, c * 512:(c + 1) * 512], 1.0),
                    reads=RDA, writes=RDA)
                p.op("S", lambda e_, c=c: e_.activation(
                    dA[:, c * 512:(c + 1) * 512], dA[:, c * 512:(c + 1) * 512], AF.Ln),
                    reads=RDA, writes=RDA)
                p.op("V", lambda e_, c=c: e_.tensor_scalar_max(
                    psB[:, c * 512:(c + 1) * 512], psB[:, c * 512:(c + 1) * 512], 0.0),
                    reads=[psBb[c]], writes=[psBb[c]])
                p.op("V", lambda e_, k=k, c=c: e_.tensor_tensor(
                    DTK[:, c * 512:(c + 1) * 512], psB[:, c * 512:(c + 1) * 512],
                    dA[:, c * 512:(c + 1) * 512], ALU.add),
                    reads=[psBb[c]] + RDA, writes=[pf + "DTK"])
                p.op("V", lambda e_, k=k, c=c: e_.tensor_tensor(
                    XT[:, k * 2048 + c * 512:k * 2048 + (c + 1) * 512],
                    DTK[:, c * 512:(c + 1) * 512],
                    XT[:, k * 2048 + c * 512:k * 2048 + (c + 1) * 512], ALU.mult),
                    reads=[pf + "DTK", pf + f"XT{k}"], writes=[pf + f"XT{k}"])
            def emit_y(j, hhap, hhnm):
                for c in range(4):
                    p.op("T", lambda e_, j=j, c=c, hhap=hhap: e_.matmul(
                        psB[:, c * 512:(c + 1) * 512],
                        YST[:, 126 - 2 * j:254 - 2 * j],
                        hhap[:, c * 512:(c + 1) * 512],
                        start=(j == 0), stop=(j == 63)),
                        reads=hhnm + ["YST"], writes=[psBb[c]], inc=False)

            pend = None
            for j in range(64):
                J = k * 64 + j
                sl = j % 2
                dAap = dA[:, 0:2048] if sl == 0 else xt[:, 0:2048]
                dAnm = RDA if sl == 0 else RXT0
                ubap = ub[:, :] if sl == 0 else WBB[:, 0:2048]
                ubnm = [pf + "ub", pf + "mo"] if sl == 0 else [pf + "WBBu", pf + "WBB"]
                hhap = hh[:, :] if sl == 0 else WBB[:, 2048:4096]
                hhnm = [pf + "hh", pf + "XPJ"] if sl == 0 else [pf + "WBBh", pf + "WBB"]
                for c in range(4):
                    p.op("T", lambda e_, j=j, c=c: e_.matmul(
                        psA[:, c * 512:(c + 1) * 512], EJT[:, j * 128:(j + 1) * 128],
                        DTK[:, c * 512:(c + 1) * 512], start=True, stop=True),
                        reads=[pf + "DTK", "EJT"], writes=[psAb[c]], inc=False)
                p.op("S", lambda e_, J=J, dAap=dAap: e_.activation(
                    dAap, psA[:, 0:2048], AF.Exp, scale=ALC[:, J:J + 1]),
                    reads=psAb + [pf + "ALC"], writes=dAnm)
                for c in range(4):
                    p.op("T", lambda e_, j=j, k=k, c=c: e_.matmul(
                        psA[:, c * 512:(c + 1) * 512], EJT[:, j * 128:(j + 1) * 128],
                        XT[:, k * 2048 + c * 512:k * 2048 + (c + 1) * 512],
                        start=True, stop=True),
                        reads=[pf + f"XT{k}", "EJT"], writes=[psAb[c]], inc=False)
                if pend is not None:
                    emit_y(*pend)
                p.op("V", lambda e_, ubap=ubap: e_.tensor_tensor(
                    ubap, psA[:, 0:2048], BREP[:, :], ALU.mult),
                    reads=psAb + [pf + "BREP"], writes=ubnm)
                p.op("V", lambda e_, hhap=hhap, dAap=dAap, ubap=ubap: e_.tensor_tensor_scan(
                    hhap, dAap, ubap, 0.0, ALU.mult, ALU.add),
                    reads=dAnm + ubnm, writes=hhnm)
                p.op("V", lambda e_, hhap=hhap: e_.tensor_tensor(
                    hhap.rearrange("q (a b) -> q a b", a=4),
                    hhap.rearrange("q (a b) -> q a b", a=4),
                    CREP[:, :].unsqueeze(1).broadcast_to([128, 4, 512]), ALU.mult),
                    reads=hhnm + [pf + "CREP"], writes=hhnm)
                pend = (j, hhap, hhnm)
            emit_y(*pend)
            for c in range(4):
                p.op("V", lambda e_, k=k, c=c: e_.tensor_tensor(
                    LT[:, k * 2048 + c * 512:k * 2048 + (c + 1) * 512],
                    psB[:, c * 512:(c + 1) * 512],
                    LT[:, k * 2048 + c * 512:k * 2048 + (c + 1) * 512], ALU.add),
                    reads=[psBb[c], pf + f"LT{k}"], writes=[pf + f"LT{k}"])
        opv = io["s_oprojb"].rearrange("(k q) o -> q k o", q=128)
        for dc in range(4):
            for ig in range(4):
                for half in range(2):
                    p.dma(pf + "opbl", OPB[:, :],
                          opv[:, half * 8:(half + 1) * 8, dc * 512:(dc + 1) * 512],
                          reads=[pf + "opd"] + RDA, writes=RDA)
                    for ii in range(4):
                        i = ig * 4 + ii
                        pb = ii
                        for kk in range(8):
                            k = half * 8 + kk
                            p.op("T", lambda e_, i=i, k=k, kk=kk, pb=pb: e_.matmul(
                                psA[:, pb * 512:(pb + 1) * 512],
                                LT[:, k * 2048 + i * 128:k * 2048 + i * 128 + 128],
                                OPB[:, kk * 512:(kk + 1) * 512],
                                start=(k == 0), stop=(k == 15)),
                                reads=[pf + f"LT{k}"] + RDA, writes=[psAb[pb]], inc=False)
                for ii in range(4):
                    i = ig * 4 + ii
                    pb = ii
                    ms = ii % 2
                    p.op("S", lambda e_, pb=pb, ms=ms: e_.copy(
                        mo[:, ms * 512:(ms + 1) * 512], psA[:, pb * 512:(pb + 1) * 512]),
                        reads=[psAb[pb]], writes=RUB)
                    p.dma(pf + "most" + str(ms),
                          smix[i * 128:(i + 1) * 128, dc * 512:(dc + 1) * 512],
                          mo[:, ms * 512:(ms + 1) * 512], reads=RUB,
                          writes=[pf + f"smx{i}_{dc}"])
        for i in range(16):
            p.dma(pf + "mxl", mix, smix[i * 128:(i + 1) * 128, :],
                  reads=[pf + f"smx{i}_{d_}" for d_ in range(4)],
                  writes=RXT1)
            p.dma(pf + "xrl", xr, xin[i * 128:(i + 1) * 128, :], writes=RXT0)
            ln_normalize(p, mix, xn[:, :], RXT1, RDA, colsb[:, 0:8], pf + "cols0")
            p.op("V", lambda e_: e_.tensor_tensor(mix, xr, xn[:, :], ALU.add),
                 reads=RXT0 + RDA + [pf + "mix"], writes=RXT1)
            p.dma(pf + "xst", xout[i * 128:(i + 1) * 128, :], mix,
                  reads=RXT1, writes=[pf + f"xo{b}_{i}"])
    return [pf + "xst", pf + "most0", pf + "most1"]


def mk_l0attn_program():
    nc = bass.Bass("TRN2", target_bir_lowering=False)
    io = {}
    io["xprojT"] = nc.dram_tensor("xprojT", [2048, 256], F32, kind="ExternalInput").ap()
    io["dtprojT"] = nc.dram_tensor("dtprojT", [128, 2048], F32, kind="ExternalInput").ap()
    io["oprojT"] = nc.dram_tensor("oprojT", [2048, 2048], F32, kind="ExternalInput").ap()
    io["norm_w"] = nc.dram_tensor("norm_w", [2048], F32, kind="ExternalInput").ap()
    io["norm_b"] = nc.dram_tensor("norm_b", [2048], F32, kind="ExternalInput").ap()
    io["dcols"] = nc.dram_tensor("dcols", [128, 16], F32, kind="ExternalInput").ap()
    io["bcols"] = nc.dram_tensor("bcols", [128, 16], F32, kind="ExternalInput").ap()
    io["alogcols"] = nc.dram_tensor("alogcols", [128, 1024], F32, kind="ExternalInput").ap()
    io["identF"] = nc.dram_tensor("identF", [128, 128], F32, kind="ExternalInput").ap()
    io["sel64"] = nc.dram_tensor("sel64", [64, 128], BF16, kind="ExternalInput").ap()
    io["EJT"] = nc.dram_tensor("EJT", [128, 64 * 128], BF16, kind="ExternalInput").ap()
    io["YST"] = nc.dram_tensor("YST", [128, 256], BF16, kind="ExternalInput").ap()
    io["s_oprojb"] = nc.dram_tensor("s_oprojb", [2048, 2048], BF16, kind="Internal").ap()
    io["s_mix"] = nc.dram_tensor("s_mix", [2048, 2048], F32, kind="Internal").ap()
    xin = nc.dram_tensor("xin", [B_PER_CORE, 2048, 2048], F32, kind="ExternalInput").ap()
    xout = nc.dram_tensor("xout", [B_PER_CORE, 2048, 2048], F32, kind="ExternalOutput").ap()
    io["xin"] = [xin[b] for b in range(B_PER_CORE)]
    io["xout"] = [xout[b] for b in range(B_PER_CORE)]
    with ExitStack() as st:
        p = Prog(nc)
        streams = build_l0attn(p, st, io)
        p.finish(streams)
        p.emit(st)
    return nc


def host_consts_l0attn(A_log, D_param, dt_bias):
    dcols = np.ascontiguousarray(D_param.reshape(16, 128).T)
    bcols = np.ascontiguousarray(dt_bias.reshape(16, 128).T)
    alog = np.empty((128, 1024), dtype=np.float32)
    for pp in range(128):
        J = np.arange(1024)
        d = (J // 64) * 128 + 2 * (J % 64) + (1 if pp >= 64 else 0)
        alog[pp, :] = A_log[d, pp % 64]
    sel = np.zeros((64, 128), dtype=np.float32)
    for n in range(64):
        sel[n, n] = 1.0
        sel[n, 64 + n] = 1.0
    ej = np.zeros((128, 64 * 128), dtype=np.float32)
    for j in range(64):
        ej[2 * j, j * 128 + 0:j * 128 + 64] = 1.0
        ej[2 * j + 1, j * 128 + 64:j * 128 + 128] = 1.0
    ys = np.zeros((128, 256), dtype=np.float32)
    ys[0:64, 126] = 1.0
    ys[64:128, 127] = 1.0
    return {
        "identF": np.eye(128, dtype=np.float32),
        "sel64": sel.astype(ml_dtypes.bfloat16),
        "EJT": ej.astype(ml_dtypes.bfloat16),
        "YST": ys.astype(ml_dtypes.bfloat16),
        "dcols": dcols.astype(np.float32),
        "bcols": bcols.astype(np.float32),
        "alogcols": alog.astype(np.float32),
    }


def kernel(L0_attn_x, L0_attn_norm_w, L0_attn_norm_b, L0_attn_x_proj_w,
           L0_attn_dt_proj_w, L0_attn_dt_proj_b, L0_attn_A_log, L0_attn_D_param,
           L0_attn_out_proj_w, L0_ffn_W1, L0_ffn_W2, L0_ffn_W3,
           L1_attn_Wq, L1_attn_Wk, L1_attn_Wv, L1_attn_Wo,
           L1_ffn_weight, L1_ffn_bias):
    x = np.ascontiguousarray(L0_attn_x, dtype=np.float32)
    n = NCORES

    def shards(arr):
        return [np.ascontiguousarray(arr[2 * c:2 * c + 2]) for c in range(n)]

    cn0 = host_consts_l0attn(np.asarray(L0_attn_A_log, dtype=np.float32),
                             np.asarray(L0_attn_D_param, dtype=np.float32),
                             np.asarray(L0_attn_dt_proj_b, dtype=np.float32))
    w0 = dict(
        xprojT=np.ascontiguousarray(np.asarray(L0_attn_x_proj_w, np.float32).T),
        dtprojT=np.ascontiguousarray(np.asarray(L0_attn_dt_proj_w, np.float32).T),
        oprojT=np.ascontiguousarray(np.asarray(L0_attn_out_proj_w, np.float32).T),
        norm_w=np.asarray(L0_attn_norm_w, np.float32),
        norm_b=np.asarray(L0_attn_norm_b, np.float32), **cn0)
    nc = mk_l0attn_program()
    res = run_bass_kernel_spmd(nc, [dict(xin=s, **w0) for s in shards(x)],
                               core_ids=list(range(n)))
    x = np.concatenate([r["xout"] for r in res.results], axis=0)

    W2T = np.ascontiguousarray(np.asarray(L0_ffn_W2, np.float32).T)
    w2pm = np.ascontiguousarray(
        W2T.reshape(64, 128, 2048).transpose(1, 0, 2).reshape(8192, 2048))
    w1 = dict(
        w1T=np.ascontiguousarray(np.asarray(L0_ffn_W1, np.float32).T),
        w3T=np.ascontiguousarray(np.asarray(L0_ffn_W3, np.float32).T),
        w2pm=w2pm, **host_consts_l0ffn())
    nc = mk_l0ffn_program()
    res = run_bass_kernel_spmd(nc, [dict(xin=s, **w1) for s in shards(x)],
                               core_ids=list(range(n)))
    x = np.concatenate([r["xout"] for r in res.results], axis=0)

    w2 = dict(Wq=np.asarray(L1_attn_Wq, np.float32), Wk=np.asarray(L1_attn_Wk, np.float32),
              Wv=np.asarray(L1_attn_Wv, np.float32), Wo=np.asarray(L1_attn_Wo, np.float32),
              **host_consts_l1attn())
    nc = mk_l1attn_program()
    res = run_bass_kernel_spmd(nc, [dict(xin=s, **w2) for s in shards(x)],
                               core_ids=list(range(n)))
    x = np.concatenate([r["xout"] for r in res.results], axis=0)

    w3 = dict(wT=np.ascontiguousarray(np.asarray(L1_ffn_weight, np.float32).T),
              bias=np.asarray(L1_ffn_bias, np.float32), **host_consts())
    nc = mk_l1ffn_program()
    res = run_bass_kernel_spmd(nc, [dict(xin=s, **w3) for s in shards(x)],
                               core_ids=list(range(n)))
    x = np.concatenate([r["xout"] for r in res.results], axis=0)
    return np.ascontiguousarray(x, dtype=np.float32)


def mk_fused_program():
    nc = bass.Bass("TRN2", target_bir_lowering=False)
    g = {}
    g["identF"] = nc.dram_tensor("identF", [128, 128], F32, kind="ExternalInput").ap()
    g["ones128"] = nc.dram_tensor("ones128", [128, 1], BF16, kind="ExternalInput").ap()
    g["ones1x512"] = nc.dram_tensor("ones1x512", [1, 512], BF16, kind="ExternalInput").ap()
    g["s_mix"] = nc.dram_tensor("s_mix", [2048, 2048], F32, kind="Internal").ap()
    xin = nc.dram_tensor("xin", [B_PER_CORE, 2048, 2048], F32, kind="ExternalInput").ap()
    xout = nc.dram_tensor("xout", [B_PER_CORE, 2048, 2048], F32, kind="ExternalOutput").ap()
    x1 = nc.dram_tensor("x1", [B_PER_CORE, 2048, 2048], F32, kind="Internal").ap()
    x2 = nc.dram_tensor("x2", [B_PER_CORE, 2048, 2048], F32, kind="Internal").ap()
    x3 = nc.dram_tensor("x3", [B_PER_CORE, 2048, 2048], F32, kind="Internal").ap()

    io0 = dict(g)
    io0["xprojT"] = nc.dram_tensor("xprojT", [2048, 256], F32, kind="ExternalInput").ap()
    io0["dtprojT"] = nc.dram_tensor("dtprojT", [128, 2048], F32, kind="ExternalInput").ap()
    io0["oprojT"] = nc.dram_tensor("oprojT", [2048, 2048], F32, kind="ExternalInput").ap()
    io0["norm_w"] = nc.dram_tensor("norm_w", [2048], F32, kind="ExternalInput").ap()
    io0["norm_b"] = nc.dram_tensor("norm_b", [2048], F32, kind="ExternalInput").ap()
    io0["dcols"] = nc.dram_tensor("dcols", [128, 16], F32, kind="ExternalInput").ap()
    io0["bcols"] = nc.dram_tensor("bcols", [128, 16], F32, kind="ExternalInput").ap()
    io0["alogcols"] = nc.dram_tensor("alogcols", [128, 1024], F32, kind="ExternalInput").ap()
    io0["sel64"] = nc.dram_tensor("sel64", [64, 128], BF16, kind="ExternalInput").ap()
    io0["EJT"] = nc.dram_tensor("EJT", [128, 64 * 128], BF16, kind="ExternalInput").ap()
    io0["YST"] = nc.dram_tensor("YST", [128, 256], BF16, kind="ExternalInput").ap()
    io0["s_oprojb"] = nc.dram_tensor("s_oprojb", [2048, 2048], BF16, kind="Internal").ap()
    io0["xin"] = [xin[b] for b in range(B_PER_CORE)]
    io0["xout"] = [x1[b] for b in range(B_PER_CORE)]

    io1 = dict(g)
    io1["w1T"] = nc.dram_tensor("w1T", [2048, 8192], F32, kind="ExternalInput").ap()
    io1["w3T"] = nc.dram_tensor("w3T", [2048, 8192], F32, kind="ExternalInput").ap()
    io1["w2pm"] = nc.dram_tensor("w2pm", [8192, 2048], F32, kind="ExternalInput").ap()
    io1["H128"] = nc.dram_tensor("H128", [128, 128], BF16, kind="ExternalInput").ap()
    io1["BD64"] = nc.dram_tensor("BD64", [128, 128], BF16, kind="ExternalInput").ap()
    io1["s_w1tb"] = nc.dram_tensor("s_w1tb", [2048, 8192], BF16, kind="Internal").ap()
    io1["s_w3tb"] = nc.dram_tensor("s_w3tb", [2048, 8192], BF16, kind="Internal").ap()
    io1["s_w2hgm"] = nc.dram_tensor("s_w2hgm", [8192, 2048], BF16, kind="Internal").ap()
    io1["xin"] = [x1[b] for b in range(B_PER_CORE)]
    io1["xout"] = [x2[b] for b in range(B_PER_CORE)]

    io2 = dict(g)
    for nm in ("Wq", "Wk", "Wv", "Wo"):
        io2[nm] = nc.dram_tensor(nm, [2048, 2048], F32, kind="ExternalInput").ap()
    io2["masks"] = nc.dram_tensor("masks", [128, 4 * 512], BF16, kind="ExternalInput").ap()
    for nm in ("s_wqb", "s_wkb", "s_wvb", "s_wob"):
        io2[nm] = nc.dram_tensor(nm, [2048, 2048], BF16, kind="Internal").ap()
    io2["s_outT"] = nc.dram_tensor("s_outT", [16, 128, 16, 128], BF16, kind="Internal").ap()
    io2["xin"] = [x2[b] for b in range(B_PER_CORE)]
    io2["xout"] = [x3[b] for b in range(B_PER_CORE)]

    io3 = dict(g)
    io3["wT"] = nc.dram_tensor("wT", [2048, 2048], F32, kind="ExternalInput").ap()
    io3["bias"] = nc.dram_tensor("bias", [2048], F32, kind="ExternalInput").ap()
    io3["scr_row"] = nc.dram_tensor("scr_row", [1, 2048], F32, kind="Internal").ap()
    io3["xin"] = [x3[b] for b in range(B_PER_CORE)]
    io3["xout"] = [xout[b] for b in range(B_PER_CORE)]

    with ExitStack() as st:
        p = Prog(nc)
        with ExitStack() as s0:
            build_l0attn(p, s0, io0)
        p.fence()
        with ExitStack() as s1:
            build_l0ffn(p, s1, io1)
        p.fence()
        with ExitStack() as s2:
            build_l1attn(p, s2, io2)
        p.fence()
        with ExitStack() as s3:
            streams = build_l1ffn(p, s3, io3)
        p.finish(streams)
        p.emit(st)
    return nc


def kernel(L0_attn_x, L0_attn_norm_w, L0_attn_norm_b, L0_attn_x_proj_w,
           L0_attn_dt_proj_w, L0_attn_dt_proj_b, L0_attn_A_log, L0_attn_D_param,
           L0_attn_out_proj_w, L0_ffn_W1, L0_ffn_W2, L0_ffn_W3,
           L1_attn_Wq, L1_attn_Wk, L1_attn_Wv, L1_attn_Wo,
           L1_ffn_weight, L1_ffn_bias):
    x = np.ascontiguousarray(L0_attn_x, dtype=np.float32)
    n = NCORES
    cn0 = host_consts_l0attn(np.asarray(L0_attn_A_log, dtype=np.float32),
                             np.asarray(L0_attn_D_param, dtype=np.float32),
                             np.asarray(L0_attn_dt_proj_b, dtype=np.float32))
    W2T = np.ascontiguousarray(np.asarray(L0_ffn_W2, np.float32).T)
    w2pm = np.ascontiguousarray(
        W2T.reshape(64, 128, 2048).transpose(1, 0, 2).reshape(8192, 2048))
    common = dict(
        xprojT=np.ascontiguousarray(np.asarray(L0_attn_x_proj_w, np.float32).T),
        dtprojT=np.ascontiguousarray(np.asarray(L0_attn_dt_proj_w, np.float32).T),
        oprojT=np.ascontiguousarray(np.asarray(L0_attn_out_proj_w, np.float32).T),
        norm_w=np.asarray(L0_attn_norm_w, np.float32),
        norm_b=np.asarray(L0_attn_norm_b, np.float32),
        w1T=np.ascontiguousarray(np.asarray(L0_ffn_W1, np.float32).T),
        w3T=np.ascontiguousarray(np.asarray(L0_ffn_W3, np.float32).T),
        w2pm=w2pm,
        Wq=np.asarray(L1_attn_Wq, np.float32), Wk=np.asarray(L1_attn_Wk, np.float32),
        Wv=np.asarray(L1_attn_Wv, np.float32), Wo=np.asarray(L1_attn_Wo, np.float32),
        wT=np.ascontiguousarray(np.asarray(L1_ffn_weight, np.float32).T),
        bias=np.asarray(L1_ffn_bias, np.float32),
        **cn0)
    for dd in (host_consts_l0ffn(), host_consts_l1attn(), host_consts()):
        common.update(dd)
    nc = mk_fused_program()
    in_maps = [dict(xin=np.ascontiguousarray(x[2 * c:2 * c + 2]), **common)
               for c in range(n)]
    res = run_bass_kernel_spmd(nc, in_maps, core_ids=list(range(n)))
    out = np.concatenate([r["xout"] for r in res.results], axis=0)
    return np.ascontiguousarray(out, dtype=np.float32)
